# version 01
# baseline (speedup 1.0000x reference)
import math
from contextlib import ExitStack
import numpy as np
import ml_dtypes
import concourse.bass as bass
import concourse.mybir as mybir
from concourse.bass import ds
from concourse.bass_core_rust import IndirectOffsetOnAxis
from concourse.bass_utils import run_bass_kernel_spmd

F32 = mybir.dt.float32
BF16 = mybir.dt.bfloat16
I32 = mybir.dt.int32
AF = mybir.ActivationFunctionType
ALU = mybir.AluOpType
AXX = mybir.AxisListType.X

D = 1024
S = 4096
NCORE = 8
KD = D // 128
CHUNK = 512
LR = 0.001


RACE_DETECT = False

def mknc():
    return bass.Bass("TRN2", target_bir_lowering=False,
                     detect_race_conditions=RACE_DETECT)


class Ctr:
    def __init__(self):
        self.c = {}
    def add(self, sem, n=1):
        self.c[sem.name] = self.c.get(sem.name, 0) + n
        return self.c[sem.name]
    def val(self, sem):
        return self.c.get(sem.name, 0)



class ConstPool:
    def __init__(self, nc, es, values, sem_name="CN"):
        self.nc = nc
        self.vals = list(values)
        self.tile = es.enter_context(nc.sbuf_tensor("constpool", [128, max(1, len(self.vals))], F32))
        self.CN = es.enter_context(nc.semaphore(sem_name))
        for i, val in enumerate(self.vals):
            nc.vector.memset(self.tile[:, i:i + 1], float(val))
        nc.vector.nop().then_inc(self.CN, 1)
    def ap(self, value, rows=128):
        i = self.vals.index(value)
        return self.tile[0:rows, i:i + 1]



def build_ln():
    nc = mknc()
    xT = nc.dram_tensor("xT", [D, S], F32, kind="ExternalInput").ap()
    xn_out = nc.dram_tensor("xn_out", [D, S], BF16, kind="ExternalOutput").ap()
    xnf_out = nc.dram_tensor("xnf_out", [D, S], F32, kind="ExternalOutput").ap()
    ones128 = nc.inline_tensor(np.ones((128, 1), np.float32), name="ones128")
    ones1x128 = nc.inline_tensor(np.ones((1, 128), np.float32), name="ones1x128")
    g, t, v, s, sy = nc.gpsimd, nc.tensor, nc.vector, nc.scalar, nc.sync
    es = ExitStack()
    pool = ConstPool(nc, es, [1e-5])
    x_sb = es.enter_context(nc.sbuf_tensor("x_sb", [128, 2 * KD * 512], F32))
    sq_sb = es.enter_context(nc.sbuf_tensor("sq_sb", [128, KD * 512], F32))
    xn_sb = es.enter_context(nc.sbuf_tensor("xn_sb", [128, 2 * KD * 512], F32))
    t1_sb = es.enter_context(nc.sbuf_tensor("t1_sb", [128, 512], F32))
    st_sb = es.enter_context(nc.sbuf_tensor("st_sb", [1, 4 * 512], F32))
    ones_sb = es.enter_context(nc.sbuf_tensor("ones_sb", [128, 129], F32))
    ps = es.enter_context(nc.psum_tensor("ps", [128, 8 * 512], F32))
    LD0 = es.enter_context(nc.semaphore("LD0"))
    LD1 = es.enter_context(nc.semaphore("LD1"))
    MM = es.enter_context(nc.semaphore("MM"))
    EV = es.enter_context(nc.semaphore("EV"))
    SQ = es.enter_context(nc.semaphore("SQ"))
    ST0 = es.enter_context(nc.semaphore("ST0"))
    ST1 = es.enter_context(nc.semaphore("ST1"))
    if True:
        C = Ctr()
        sy.dma_start(out=ones_sb[:, 0:1], in_=ones128.ap()[:, :]).then_inc(LD0, 16)
        sy.dma_start(out=ones_sb[0:1, 1:129], in_=ones1x128.ap()[:, :]).then_inc(LD0, 16)
        C.add(LD0, 2)
        mm_n = ev_n = sq_n = 0
        var_done = {}
        apply_done = {}
        for nt in range(8):
            par = nt % 2
            LD = LD0 if par == 0 else LD1
            xo = par * KD * 512
            tok0 = nt * 512
            if nt >= 2:
                sy.wait_ge(EV, apply_done[nt - 2])
                sy.wait_ge(LD, 16 * C.val(LD))
            for k in range(KD):
                sy.dma_start(out=x_sb[:, xo + k * 512:xo + (k + 1) * 512],
                             in_=xT[k * 128:(k + 1) * 128, tok0:tok0 + 512]).then_inc(LD, 16)
            C.add(LD, KD)
            ld_tot = 16 * C.val(LD)
            s.wait_ge(LD, ld_tot)
            for k in range(KD):
                li = s.activation(sq_sb[:, k * 512:(k + 1) * 512],
                             x_sb[:, xo + k * 512:xo + (k + 1) * 512], AF.Square)
            li.then_inc(SQ, 1)
            sq_n += 1
            t.wait_ge(LD, ld_tot)
            if nt >= 1:
                t.wait_ge(EV, var_done[nt - 1])
            for k in range(KD):
                t.matmul(ps[0:1, 0:512], ones_sb[:, 0:1],
                         x_sb[:, xo + k * 512:xo + (k + 1) * 512],
                         start=(k == 0), stop=(k == KD - 1))
            t.wait_ge(SQ, sq_n)
            for k in range(KD):
                li = t.matmul(ps[0:1, 512:1024], ones_sb[:, 0:1], sq_sb[:, k * 512:(k + 1) * 512],
                         start=(k == 0), stop=(k == KD - 1))
            li.then_inc(MM, 1)
            mm_n += 1
            v.wait_ge(MM, mm_n)
            v.tensor_scalar(st_sb[0:1, 0:512], ps[0:1, 0:512], 1.0 / D, None, ALU.mult)
            v.tensor_scalar(st_sb[0:1, 512:1024], ps[0:1, 512:1024], 1.0 / D, None, ALU.mult)
            v.tensor_tensor(st_sb[0:1, 1024:1536], st_sb[0:1, 0:512], st_sb[0:1, 0:512], ALU.mult)
            v.tensor_tensor(st_sb[0:1, 1024:1536], st_sb[0:1, 512:1024], st_sb[0:1, 1024:1536], ALU.subtract).then_inc(EV, 1)
            ev_n += 1
            var_done[nt] = ev_n
            s.wait_ge(EV, ev_n)
            if nt == 0:
                s.wait_ge(pool.CN, 1)
            s.activation(st_sb[0:1, 1024:1536], st_sb[0:1, 1024:1536], AF.Sqrt, bias=pool.ap(1e-5, rows=1)).then_inc(SQ, 1)
            sq_n += 1
            v.wait_ge(SQ, sq_n)
            v.reciprocal(st_sb[0:1, 1024:1536], st_sb[0:1, 1024:1536])
            v.tensor_tensor(st_sb[0:1, 1536:2048], st_sb[0:1, 0:512], st_sb[0:1, 1024:1536], ALU.mult).then_inc(EV, 1)
            ev_n += 1
            t.wait_ge(EV, ev_n)
            t.matmul(ps[:, 2 * 512:3 * 512], ones_sb[0:1, 1:129], st_sb[0:1, 1024:1536], start=True, stop=True)
            t.matmul(ps[:, 3 * 512:4 * 512], ones_sb[0:1, 1:129], st_sb[0:1, 1536:2048], start=True, stop=True).then_inc(MM, 1)
            mm_n += 1
            v.wait_ge(MM, mm_n)
            ST = ST0 if par == 0 else ST1
            if nt >= 2:
                v.wait_ge(ST, 16 * C.val(ST))
            for k in range(KD):
                v.tensor_tensor(t1_sb[:], x_sb[:, xo + k * 512:xo + (k + 1) * 512], ps[:, 2 * 512:3 * 512], ALU.mult)
                li = v.tensor_tensor(xn_sb[:, xo + k * 512:xo + (k + 1) * 512], t1_sb[:], ps[:, 3 * 512:4 * 512], ALU.subtract)
            li.then_inc(EV, 1)
            ev_n += 1
            apply_done[nt] = ev_n
            g.wait_ge(EV, ev_n)
            for k in range(KD):
                g.dma_start(out=xn_out[k * 128:(k + 1) * 128, tok0:tok0 + 512],
                            in_=xn_sb[:, xo + k * 512:xo + (k + 1) * 512]).then_inc(ST, 16)
                g.dma_start(out=xnf_out[k * 128:(k + 1) * 128, tok0:tok0 + 512],
                            in_=xn_sb[:, xo + k * 512:xo + (k + 1) * 512]).then_inc(ST, 16)
            C.add(ST, 2 * KD)
        g.wait_ge(ST0, 16 * C.val(ST0))
        g.wait_ge(ST1, 16 * C.val(ST1))
    es.close()
    return nc



def build_chunk(kchunk):
    nc = mknc()
    xnT = nc.dram_tensor("xnT", [D, CHUNK], BF16, kind="ExternalInput").ap()
    w0 = nc.dram_tensor("w0", [D, 4096], F32, kind="ExternalInput").ap()
    w2 = nc.dram_tensor("w2", [D, 4096], F32, kind="ExternalInput").ap()
    w1 = nc.dram_tensor("w1", [4096, D], F32, kind="ExternalInput").ap()
    if kchunk > 0:
        ssep = nc.dram_tensor("ssep", [8, kchunk], F32, kind="ExternalInput").ap()
    s_out = nc.dram_tensor("s_out", [D, CHUNK], BF16, kind="ExternalOutput").ap()
    sse_out = nc.dram_tensor("sse_out", [1, 1], F32, kind="ExternalOutput").ap()

    ones128 = nc.inline_tensor(np.ones((128, 1), np.float32), name="ones128")
    ones1x128 = nc.inline_tensor(np.ones((1, 128), np.float32), name="ones1x128")
    ones8 = nc.inline_tensor(np.ones((8, 1), np.float32), name="ones8")
    g, t, v, s, sy = nc.gpsimd, nc.tensor, nc.vector, nc.scalar, nc.sync
    es = ExitStack()
    xn_sb = es.enter_context(nc.sbuf_tensor("xn_sb", [128, KD * CHUNK], BF16))
    w_sb = es.enter_context(nc.sbuf_tensor("w_sb", [128, 8 * 4096], BF16))
    p_sb = es.enter_context(nc.sbuf_tensor("p_sb", [128, 32 * CHUNK], BF16))
    q_sb = es.enter_context(nc.sbuf_tensor("q_sb", [128, 32 * CHUNK], BF16))
    t1_sb = es.enter_context(nc.sbuf_tensor("t1_sb", [128, 512], F32))
    t2_sb = es.enter_context(nc.sbuf_tensor("t2_sb", [128, 512], F32))
    diff_sb = es.enter_context(nc.sbuf_tensor("diff_sb", [128, 512], F32))
    junk_sb = es.enter_context(nc.sbuf_tensor("junk_sb", [128, 512], F32))
    acc_sb = es.enter_context(nc.sbuf_tensor("acc_sb", [128, 8], F32))
    sse_sb = es.enter_context(nc.sbuf_tensor("sse_sb", [1, 8], F32))
    ones_sb = es.enter_context(nc.sbuf_tensor("ones_sb", [128, 129], F32))
    o8_sb = es.enter_context(nc.sbuf_tensor("o8_sb", [8, 1], F32))
    ssep_sb = es.enter_context(nc.sbuf_tensor("ssep_sb", [8, 8], F32))
    dsc_sb = es.enter_context(nc.sbuf_tensor("dsc_sb", [1, 16], F32))
    dbc_sb = es.enter_context(nc.sbuf_tensor("dbc_sb", [128, 1], F32))
    ps = es.enter_context(nc.psum_tensor("ps", [128, 8 * 512], F32))
    LD = es.enter_context(nc.semaphore("LD"))
    LW = es.enter_context(nc.semaphore("LW"))
    MM = es.enter_context(nc.semaphore("MM"))
    EV = es.enter_context(nc.semaphore("EV"))
    SQ = es.enter_context(nc.semaphore("SQ"))
    ST = es.enter_context(nc.semaphore("ST"))
    if True:
        C = Ctr()
        mm_n = ev_n = sq_n = 0
        sy.dma_start(out=ones_sb[:, 0:1], in_=ones128.ap()[:, :]).then_inc(LD, 16)
        sy.dma_start(out=ones_sb[0:1, 1:129], in_=ones1x128.ap()[:, :]).then_inc(LD, 16)
        sy.dma_start(out=o8_sb[:, :], in_=ones8.ap()[:, :]).then_inc(LD, 16)
        C.add(LD, 3)
        for k in range(KD):
            sy.dma_start(out=xn_sb[:, k * CHUNK:(k + 1) * CHUNK],
                         in_=xnT[k * 128:(k + 1) * 128, :]).then_inc(LD, 16)
        C.add(LD, KD)

        if kchunk > 0:
            v.memset(ssep_sb[:, :], 0.0).then_inc(EV, 1)
            ev_n += 1
            sy.wait_ge(EV, ev_n)
            sy.dma_start(out=ssep_sb[0:8, 0:kchunk], in_=ssep[:, :]).then_inc(LD, 16)
            C.add(LD, 1)
        t.wait_ge(LD, 16 * C.val(LD))
        if kchunk > 0:
            t.matmul(ps[0:1, 0:8], o8_sb[0:8, 0:1], ssep_sb[0:8, 0:8], start=True, stop=True).then_inc(MM, 1)
            mm_n += 1
            v.wait_ge(MM, mm_n)
            v.tensor_scalar(dsc_sb[0:1, 0:kchunk], ps[0:1, 0:kchunk],
                            -LR / (NCORE * CHUNK * D), 1.0, ALU.mult, ALU.add).then_inc(EV, 1)
            ev_n += 1
            v.wait_ge(EV, ev_n)
            v.tensor_copy(dsc_sb[0:1, 8:9], dsc_sb[0:1, 0:1]).then_inc(EV, 1)
            ev_n += 1
            for j in range(1, kchunk):
                v.wait_ge(EV, ev_n)
                v.tensor_tensor(dsc_sb[0:1, 8:9], dsc_sb[0:1, 8:9], dsc_sb[0:1, j:j + 1], ALU.mult).then_inc(EV, 1)
                ev_n += 1
            v.wait_ge(EV, ev_n)
            li = v.tensor_copy(dsc_sb[0:1, 8:12], dsc_sb[0:1, 8:9].to_broadcast([1, 4]))
            li.then_inc(EV, 1)
            ev_n += 1
            t.wait_ge(EV, ev_n)
            t.matmul(ps[:, 0:4], ones_sb[0:1, 1:129], dsc_sb[0:1, 8:12], start=True, stop=True).then_inc(MM, 1)
            mm_n += 1
            v.wait_ge(MM, mm_n)
            v.tensor_copy(dbc_sb[:, 0:1], ps[:, 0:1]).then_inc(EV, 1)
            ev_n += 1
            t.wait_ge(EV, ev_n)

        def load_w_kchunks(src, n_k, width):
            for k in range(n_k):
                g.dma_start(out=w_sb[:, k * width:(k + 1) * width],
                            in_=src[k * 128:(k + 1) * 128, :]).then_inc(LW, 16)
            C.add(LW, n_k)

        def proj_32m(dst_sb):
            nonlocal mm_n, ev_n
            t.wait_ge(LW, 16 * C.val(LW))
            for m in range(32):
                bank = m % 4
                for k in range(KD):
                    ins = t.matmul(ps[:, bank * 512:(bank + 1) * 512],
                                   w_sb[:, k * 4096 + m * 128:k * 4096 + (m + 1) * 128],
                                   xn_sb[:, k * CHUNK:k * CHUNK + 512],
                                   start=(k == 0), stop=(k == KD - 1))
                ins.then_inc(MM, 1)
                mm_n += 1
                v.wait_ge(MM, mm_n)
                v.tensor_copy(dst_sb[:, m * 512:(m + 1) * 512], ps[:, bank * 512:(bank + 1) * 512]).then_inc(EV, 1)
                ev_n += 1
                if m >= 3:
                    t.wait_ge(EV, ev_n - 3)

        load_w_kchunks(w0, KD, 4096)
        proj_32m(p_sb)
        g.wait_ge(MM, mm_n)
        load_w_kchunks(w2, KD, 4096)
        proj_32m(q_sb)

        s.wait_ge(EV, ev_n)
        for m in range(32):
            if kchunk > 0:
                s.activation(t1_sb[:], p_sb[:, m * 512:(m + 1) * 512], AF.Silu, scale=dbc_sb[:, 0:1]).then_inc(SQ, 1)
            else:
                s.activation(t1_sb[:], p_sb[:, m * 512:(m + 1) * 512], AF.Silu).then_inc(SQ, 1)
            sq_n += 1
            v.wait_ge(SQ, sq_n)
            if kchunk > 0:
                v.tensor_scalar(t2_sb[:], q_sb[:, m * 512:(m + 1) * 512], dbc_sb[:, 0:1], None, ALU.mult)
                li = v.tensor_tensor(p_sb[:, m * 512:(m + 1) * 512], t1_sb[:], t2_sb[:], ALU.mult)
            else:
                li = v.tensor_tensor(p_sb[:, m * 512:(m + 1) * 512], t1_sb[:],
                                q_sb[:, m * 512:(m + 1) * 512], ALU.mult)
            li.then_inc(EV, 1)
            ev_n += 1
            s.wait_ge(EV, ev_n)

        g.wait_ge(MM, mm_n)
        for k in range(32):
            g.dma_start(out=w_sb[:, k * 1024:(k + 1) * 1024],
                        in_=w1[k * 128:(k + 1) * 128, :]).then_inc(LW, 16)
        C.add(LW, 32)
        t.wait_ge(LW, 16 * C.val(LW))
        t.wait_ge(EV, ev_n)
        for m in range(8):
            for k in range(32):
                ins = t.matmul(ps[:, m * 512:(m + 1) * 512],
                               w_sb[:, k * 1024 + m * 128:k * 1024 + (m + 1) * 128],
                               p_sb[:, k * 512:(k + 1) * 512],
                               start=(k == 0), stop=(k == 31))
        ins.then_inc(MM, 1)
        mm_n += 1
        s.wait_ge(MM, mm_n)
        v.wait_ge(MM, mm_n)
        for m in range(8):
            if kchunk > 0:
                s.activation(p_sb[:, m * 512:(m + 1) * 512], ps[:, m * 512:(m + 1) * 512], AF.Copy,
                             scale=dbc_sb[:, 0:1]).then_inc(SQ, 1)
            else:
                s.activation(p_sb[:, m * 512:(m + 1) * 512], ps[:, m * 512:(m + 1) * 512], AF.Copy).then_inc(SQ, 1)
            sq_n += 1
            v.wait_ge(SQ, sq_n)
            v.tensor_tensor(diff_sb[:], p_sb[:, m * 512:(m + 1) * 512],
                            xn_sb[:, m * CHUNK:(m + 1) * CHUNK], ALU.subtract).then_inc(EV, 1)
            ev_n += 1
            s.wait_ge(EV, ev_n)
            s.activation(junk_sb[:], diff_sb[:], AF.Square, accum_out=acc_sb[:, m:m + 1]).then_inc(SQ, 1)
            sq_n += 1
        t.wait_ge(SQ, sq_n)
        t.matmul(ps[0:1, 0:8], ones_sb[:, 0:1], acc_sb[:], start=True, stop=True).then_inc(MM, 1)
        mm_n += 1
        v.wait_ge(MM, mm_n)
        v.reduce_sum(sse_sb[0:1, 0:1], ps[0:1, 0:8], axis=AXX).then_inc(EV, 1)
        ev_n += 1
        g.wait_ge(EV, ev_n)
        g.wait_ge(SQ, sq_n)
        for m in range(8):
            g.dma_start(out=s_out[m * 128:(m + 1) * 128, :],
                        in_=p_sb[:, m * 512:(m + 1) * 512]).then_inc(ST, 16)
        g.dma_start(out=sse_out[:, :], in_=sse_sb[0:1, 0:1]).then_inc(ST, 16)
        C.add(ST, 9)
        g.wait_ge(ST, 16 * C.val(ST))
    es.close()
    return nc





def build_l9a():
    nc = mknc()
    xT = nc.dram_tensor("xT", [D, S], F32, kind="ExternalInput").ap()
    sT = nc.dram_tensor("sT", [D, S], BF16, kind="ExternalInput").ap()
    fT = nc.dram_tensor("fT", [D, S], BF16, kind="ExternalInput").ap()
    ln1_g = nc.dram_tensor("ln1_g", [D, 1], F32, kind="ExternalInput").ap()
    ln1_b = nc.dram_tensor("ln1_b", [D, 1], F32, kind="ExternalInput").ap()
    gate = nc.dram_tensor("gate", [D, 8], F32, kind="ExternalInput").ap()
    x1T = nc.dram_tensor("x1T", [D, S], F32, kind="ExternalOutput").ap()
    xn1T = nc.dram_tensor("xn1T", [D, S], BF16, kind="ExternalOutput").ap()
    yT = nc.dram_tensor("yT", [D, S], BF16, kind="ExternalOutput").ap()
    probs = nc.dram_tensor("probs", [S, 8], F32, kind="ExternalOutput").ap()
    ones128 = nc.inline_tensor(np.ones((128, 1), np.float32), name="ones128")
    ones1x128 = nc.inline_tensor(np.ones((1, 128), np.float32), name="ones1x128")
    g, t, v, s, sy = nc.gpsimd, nc.tensor, nc.vector, nc.scalar, nc.sync
    es = ExitStack()
    pool = ConstPool(nc, es, [1e-5])
    x_sb = es.enter_context(nc.sbuf_tensor("x_sb", [128, 2 * KD * 512], F32))
    s_sb = es.enter_context(nc.sbuf_tensor("s_sb", [128, 2 * KD * 512], BF16))
    f_sb = es.enter_context(nc.sbuf_tensor("f_sb", [128, 2 * KD * 512], BF16))
    x1_sb = es.enter_context(nc.sbuf_tensor("x1_sb", [128, 2 * KD * 512], F32))
    xn1_sb = es.enter_context(nc.sbuf_tensor("xn1_sb", [128, 2 * KD * 512], BF16))
    y_sb = es.enter_context(nc.sbuf_tensor("y_sb", [128, 2 * KD * 512], BF16))
    sq_sb = es.enter_context(nc.sbuf_tensor("sq_sb", [128, KD * 512], F32))
    t1_sb = es.enter_context(nc.sbuf_tensor("t1_sb", [128, 512], F32))
    st_sb = es.enter_context(nc.sbuf_tensor("st_sb", [1, 4 * 512], F32))
    lnp_sb = es.enter_context(nc.sbuf_tensor("lnp_sb", [128, 16], F32))
    gate_sb = es.enter_context(nc.sbuf_tensor("gate_sb", [128, KD * 8], BF16))
    sm_sb = es.enter_context(nc.sbuf_tensor("sm_sb", [128, 16], F32))
    pr_sb = es.enter_context(nc.sbuf_tensor("pr_sb", [128, 8], F32))
    ones_sb = es.enter_context(nc.sbuf_tensor("ones_sb", [128, 129], F32))
    ones_bsb = es.enter_context(nc.sbuf_tensor("ones_bsb", [128, 1], BF16))
    ps = es.enter_context(nc.psum_tensor("ps", [128, 8 * 512], F32))
    LD0 = es.enter_context(nc.semaphore("LD0"))
    LD1 = es.enter_context(nc.semaphore("LD1"))
    MM = es.enter_context(nc.semaphore("MM"))
    EV = es.enter_context(nc.semaphore("EV"))
    SQ = es.enter_context(nc.semaphore("SQ"))
    ST0 = es.enter_context(nc.semaphore("ST0"))
    ST1 = es.enter_context(nc.semaphore("ST1"))
    GL = es.enter_context(nc.semaphore("GL"))
    C = Ctr()
    mm_n = ev_n = sq_n = 0
    g.dma_start(out=ones_bsb[:, 0:1], in_=ones128.ap()[:, :]).then_inc(GL, 16)
    C.add(GL, 1)
    sy.dma_start(out=ones_sb[:, 0:1], in_=ones128.ap()[:, :]).then_inc(LD0, 16)
    sy.dma_start(out=ones_sb[0:1, 1:129], in_=ones1x128.ap()[:, :]).then_inc(LD0, 16)
    for k in range(KD):
        sy.dma_start(out=lnp_sb[:, k:k + 1], in_=ln1_g[k * 128:(k + 1) * 128, :]).then_inc(LD0, 16)
        sy.dma_start(out=lnp_sb[:, 8 + k:9 + k], in_=ln1_b[k * 128:(k + 1) * 128, :]).then_inc(LD0, 16)
        g.dma_start(out=gate_sb[:, k * 8:(k + 1) * 8], in_=gate[k * 128:(k + 1) * 128, :]).then_inc(GL, 16)
    C.add(LD0, 2 + 2 * KD)
    C.add(GL, KD)

    def ln_stats(src_sb, xo, n_sq_off=0):
        nonlocal mm_n, ev_n, sq_n
        o1 = ones_sb[:, 0:1] if src_sb.dtype == F32 else ones_bsb[:, 0:1]
        s.wait_ge(EV, ev_n)
        for k in range(KD):
            li = s.activation(sq_sb[:, k * 512:(k + 1) * 512],
                              src_sb[:, xo + k * 512:xo + (k + 1) * 512], AF.Square)
        li.then_inc(SQ, 1)
        sq_n += 1
        for k in range(KD):
            t.matmul(ps[0:1, 0:512], o1,
                     src_sb[:, xo + k * 512:xo + (k + 1) * 512],
                     start=(k == 0), stop=(k == KD - 1))
        t.wait_ge(SQ, sq_n)
        for k in range(KD):
            li = t.matmul(ps[0:1, 512:1024], ones_sb[:, 0:1], sq_sb[:, k * 512:(k + 1) * 512],
                          start=(k == 0), stop=(k == KD - 1))
        li.then_inc(MM, 1)
        mm_n += 1
        v.wait_ge(MM, mm_n)
        v.tensor_scalar(st_sb[0:1, 0:512], ps[0:1, 0:512], 1.0 / D, None, ALU.mult)
        v.tensor_scalar(st_sb[0:1, 512:1024], ps[0:1, 512:1024], 1.0 / D, None, ALU.mult)
        v.tensor_tensor(st_sb[0:1, 1024:1536], st_sb[0:1, 0:512], st_sb[0:1, 0:512], ALU.mult)
        v.tensor_tensor(st_sb[0:1, 1024:1536], st_sb[0:1, 512:1024], st_sb[0:1, 1024:1536], ALU.subtract).then_inc(EV, 1)
        ev_n += 1
        s.wait_ge(EV, ev_n)
        s.activation(st_sb[0:1, 1024:1536], st_sb[0:1, 1024:1536], AF.Sqrt, bias=pool.ap(1e-5, rows=1)).then_inc(SQ, 1)
        sq_n += 1
        v.wait_ge(SQ, sq_n)
        v.reciprocal(st_sb[0:1, 1024:1536], st_sb[0:1, 1024:1536])
        v.tensor_tensor(st_sb[0:1, 1536:2048], st_sb[0:1, 0:512], st_sb[0:1, 1024:1536], ALU.mult).then_inc(EV, 1)
        ev_n += 1
        t.wait_ge(EV, ev_n)
        t.matmul(ps[:, 2 * 512:3 * 512], ones_sb[0:1, 1:129], st_sb[0:1, 1024:1536], start=True, stop=True)
        li = t.matmul(ps[:, 3 * 512:4 * 512], ones_sb[0:1, 1:129], st_sb[0:1, 1536:2048], start=True, stop=True)
        li.then_inc(MM, 1)
        mm_n += 1
        v.wait_ge(MM, mm_n)

    s.wait_ge(pool.CN, 1)
    cons_done = {}
    store_done = {}
    for nt in range(8):
        par = nt % 2
        LD = LD0 if par == 0 else LD1
        ST = ST0 if par == 0 else ST1
        xo = par * KD * 512
        tok0 = nt * 512
        if nt >= 2:
            sy.wait_ge(EV, cons_done[nt - 2])
            sy.wait_ge(LD, 16 * C.val(LD))
            g.wait_ge(EV, cons_done[nt - 2])
        for k in range(KD):
            sy.dma_start(out=x_sb[:, xo + k * 512:xo + (k + 1) * 512],
                         in_=xT[k * 128:(k + 1) * 128, tok0:tok0 + 512]).then_inc(LD, 16)
            sy.dma_start(out=s_sb[:, xo + k * 512:xo + (k + 1) * 512],
                         in_=sT[k * 128:(k + 1) * 128, tok0:tok0 + 512]).then_inc(LD, 16)
            g.dma_start(out=f_sb[:, xo + k * 512:xo + (k + 1) * 512],
                        in_=fT[k * 128:(k + 1) * 128, tok0:tok0 + 512]).then_inc(GL, 16)
        C.add(LD, 2 * KD)
        C.add(GL, KD)
        ld_tot = 16 * C.val(LD)
        s.wait_ge(LD, ld_tot)
        s.wait_ge(GL, 16 * C.val(GL))
        v.wait_ge(LD, ld_tot)
        v.wait_ge(GL, 16 * C.val(GL))
        if nt >= 2:
            v.wait_ge(ST, 16 * C.val(ST))
        for k in range(KD):
            s.activation(t1_sb[:], f_sb[:, xo + k * 512:xo + (k + 1) * 512], AF.Silu).then_inc(SQ, 1)
            sq_n += 1
            v.wait_ge(SQ, sq_n)
            v.tensor_tensor(t1_sb[:], t1_sb[:], s_sb[:, xo + k * 512:xo + (k + 1) * 512], ALU.mult)
            li = v.tensor_tensor(x1_sb[:, xo + k * 512:xo + (k + 1) * 512], t1_sb[:],
                                 x_sb[:, xo + k * 512:xo + (k + 1) * 512], ALU.add)
            li.then_inc(EV, 1)
            ev_n += 1
            s.wait_ge(EV, ev_n)
        t.wait_ge(EV, ev_n)
        ln_stats(x1_sb, xo)
        for k in range(KD):
            v.tensor_tensor(t1_sb[:], x1_sb[:, xo + k * 512:xo + (k + 1) * 512], ps[:, 2 * 512:3 * 512], ALU.mult)
            li = v.tensor_tensor(xn1_sb[:, xo + k * 512:xo + (k + 1) * 512], t1_sb[:], ps[:, 3 * 512:4 * 512], ALU.subtract)
        li.then_inc(EV, 1)
        ev_n += 1
        t.wait_ge(EV, ev_n)
        ln_stats(xn1_sb, xo)
        for k in range(KD):
            v.tensor_tensor(t1_sb[:], xn1_sb[:, xo + k * 512:xo + (k + 1) * 512], ps[:, 2 * 512:3 * 512], ALU.mult)
            v.tensor_tensor(t1_sb[:], t1_sb[:], ps[:, 3 * 512:4 * 512], ALU.subtract)
            li = v.tensor_scalar(y_sb[:, xo + k * 512:xo + (k + 1) * 512], t1_sb[:],
                                 lnp_sb[:, k:k + 1], lnp_sb[:, 8 + k:9 + k], ALU.mult, ALU.add)
        li.then_inc(EV, 1)
        ev_n += 1
        cons_done[nt] = ev_n
        t.wait_ge(EV, ev_n)
        for tg in range(4):
            for k in range(KD):
                li = t.matmul(ps[:, 4 * 512 + tg * 8:4 * 512 + (tg + 1) * 8],
                              xn1_sb[:, xo + k * 512 + tg * 128:xo + k * 512 + (tg + 1) * 128],
                              gate_sb[:, k * 8:(k + 1) * 8],
                              start=(k == 0), stop=(k == KD - 1))
        li.then_inc(MM, 1)
        mm_n += 1
        v.wait_ge(MM, mm_n)
        for tg in range(4):
            pslice = ps[:, 4 * 512 + tg * 8:4 * 512 + (tg + 1) * 8]
            v.tensor_copy(sm_sb[:, 0:8], pslice).then_inc(EV, 1)
            ev_n += 1
            v.wait_ge(EV, ev_n)
            v.reduce_max(sm_sb[:, 8:9], sm_sb[:, 0:8], axis=AXX).then_inc(EV, 1)
            ev_n += 1
            v.wait_ge(EV, ev_n)
            v.tensor_scalar(sm_sb[:, 0:8], sm_sb[:, 0:8], sm_sb[:, 8:9], None, ALU.subtract).then_inc(EV, 1)
            ev_n += 1
            s.wait_ge(EV, ev_n)
            s.activation(sm_sb[:, 0:8], sm_sb[:, 0:8], AF.Exp).then_inc(SQ, 1)
            sq_n += 1
            v.wait_ge(SQ, sq_n)
            v.reduce_sum(sm_sb[:, 8:9], sm_sb[:, 0:8], axis=AXX).then_inc(EV, 1)
            ev_n += 1
            v.wait_ge(EV, ev_n)
            v.reciprocal(sm_sb[:, 9:10], sm_sb[:, 8:9]).then_inc(EV, 1)
            ev_n += 1
            v.wait_ge(EV, ev_n)
            v.tensor_scalar(pr_sb[:, 0:8], sm_sb[:, 0:8], sm_sb[:, 9:10], None, ALU.mult).then_inc(EV, 1)
            ev_n += 1
            g.wait_ge(EV, ev_n)
            g.dma_start(out=probs[tok0 + tg * 128:tok0 + (tg + 1) * 128, :],
                        in_=pr_sb[:, 0:8]).then_inc(ST, 16)
            C.add(ST, 1)
            v.wait_ge(ST, 16 * C.val(ST))
        g.wait_ge(EV, cons_done[nt])
        for k in range(KD):
            g.dma_start(out=x1T[k * 128:(k + 1) * 128, tok0:tok0 + 512],
                        in_=x1_sb[:, xo + k * 512:xo + (k + 1) * 512]).then_inc(ST, 16)
            g.dma_start(out=xn1T[k * 128:(k + 1) * 128, tok0:tok0 + 512],
                        in_=xn1_sb[:, xo + k * 512:xo + (k + 1) * 512]).then_inc(ST, 16)
            g.dma_start(out=yT[k * 128:(k + 1) * 128, tok0:tok0 + 512],
                        in_=y_sb[:, xo + k * 512:xo + (k + 1) * 512]).then_inc(ST, 16)
        C.add(ST, 3 * KD)
    g.wait_ge(ST0, 16 * C.val(ST0))
    g.wait_ge(ST1, 16 * C.val(ST1))
    es.close()
    return nc



def build_l9b():
    nc = mknc()
    yT = nc.dram_tensor("yT", [D, S], BF16, kind="ExternalInput").ap()
    Ws = {}
    bs = {}
    for P in ("q", "k", "v", "g"):
        Ws[P] = nc.dram_tensor("W" + P, [D, D], F32, kind="ExternalInput").ap()
        bs[P] = nc.dram_tensor("b" + P, [D, 1], F32, kind="ExternalInput").ap()
    dlog = nc.dram_tensor("dlog", [16, 1], F32, kind="ExternalInput").ap()
    outT = nc.dram_tensor("outT", [D, S], BF16, kind="ExternalOutput").ap()

    mask16 = np.zeros((16, 8), np.float32)
    for h in range(16):
        mask16[h, h // 2] = 1.0
    hsel = np.zeros((16, 128), np.float32)
    for h in range(16):
        for p in range(128):
            if p // 64 == h % 2:
                hsel[h, p] = 1.0
    mask16_t = nc.inline_tensor(mask16, name="mask16")
    hsel_t = nc.inline_tensor(hsel, name="hsel")

    g, t, v, s, sy = nc.gpsimd, nc.tensor, nc.vector, nc.scalar, nc.sync
    es = ExitStack()
    y_sb = es.enter_context(nc.sbuf_tensor("y_sb", [128, KD * S], BF16))
    wp_sb = es.enter_context(nc.sbuf_tensor("wp_sb", [128, 4 * KD * 128], BF16))
    bias_sb = es.enter_context(nc.sbuf_tensor("bias_sb", [128, 32], F32))
    qe_sb = es.enter_context(nc.sbuf_tensor("qe_sb", [128, 512], F32))
    ke_sb = es.enter_context(nc.sbuf_tensor("ke_sb", [128, 512], F32))
    ve_sb = es.enter_context(nc.sbuf_tensor("ve_sb", [128, 512], F32))
    ge_sb = es.enter_context(nc.sbuf_tensor("ge_sb", [128, 512], F32))
    kv_sb = es.enter_context(nc.sbuf_tensor("kv_sb", [128, S], F32))
    qg_sb = es.enter_context(nc.sbuf_tensor("qg_sb", [128, S], F32))
    state_sb = es.enter_context(nc.sbuf_tensor("state_sb", [128, S], F32))
    out_sb = es.enter_context(nc.sbuf_tensor("out_sb", [128, 2 * S], BF16))
    dl_sb = es.enter_context(nc.sbuf_tensor("dl_sb", [16, 2], F32))
    m16_sb = es.enter_context(nc.sbuf_tensor("m16_sb", [16, 8], F32))
    hsel_sb = es.enter_context(nc.sbuf_tensor("hsel_sb", [16, 128], F32))
    rhs_sb = es.enter_context(nc.sbuf_tensor("rhs_sb", [16, 8], F32))
    dec_sb = es.enter_context(nc.sbuf_tensor("dec_sb", [128, 8], F32))
    ps = es.enter_context(nc.psum_tensor("ps", [128, 8 * 512], F32))
    LD = es.enter_context(nc.semaphore("LD"))
    LW = es.enter_context(nc.semaphore("LW"))
    MM = es.enter_context(nc.semaphore("MM"))
    EV = es.enter_context(nc.semaphore("EV"))
    SQ = es.enter_context(nc.semaphore("SQ"))
    ST0 = es.enter_context(nc.semaphore("ST0"))
    ST1 = es.enter_context(nc.semaphore("ST1"))
    C = Ctr()
    mm_n = ev_n = sq_n = 0
    for k in range(KD):
        for nt2 in range(8):
            sy.dma_start(out=y_sb[:, k * S + nt2 * 512:k * S + (nt2 + 1) * 512],
                         in_=yT[k * 128:(k + 1) * 128, nt2 * 512:(nt2 + 1) * 512]).then_inc(LD, 16)
    C.add(LD, KD * 8)
    sy.dma_start(out=dl_sb[0:16, 0:1], in_=dlog[:, :]).then_inc(LD, 16)
    sy.dma_start(out=m16_sb[:, :], in_=mask16_t.ap()[:, :]).then_inc(LD, 16)
    sy.dma_start(out=hsel_sb[:, :], in_=hsel_t.ap()[:, :]).then_inc(LD, 16)
    C.add(LD, 3)
    for Pi, P in enumerate(("q", "k", "v", "g")):
        for c in range(8):
            sy.dma_start(out=bias_sb[:, Pi * 8 + c:Pi * 8 + c + 1],
                         in_=bs[P][c * 128:(c + 1) * 128, :]).then_inc(LD, 16)
    C.add(LD, 32)
    s.wait_ge(LD, 16 * C.val(LD))
    s.activation(dl_sb[0:16, 1:2], dl_sb[0:16, 0:1], AF.Sigmoid).then_inc(SQ, 1)
    sq_n += 1
    v.wait_ge(SQ, sq_n)
    v.tensor_scalar(rhs_sb[0:16, :], m16_sb[0:16, :], dl_sb[0:16, 1:2], None, ALU.mult).then_inc(EV, 1)
    ev_n += 1
    t.wait_ge(EV, ev_n)
    t.matmul(ps[:, 7 * 512:7 * 512 + 8], hsel_sb[0:16, :], rhs_sb[0:16, :], start=True, stop=True).then_inc(MM, 1)
    mm_n += 1
    v.wait_ge(MM, mm_n)
    v.tensor_copy(dec_sb[:, :], ps[:, 7 * 512:7 * 512 + 8]).then_inc(EV, 1)
    ev_n += 1

    t.wait_ge(LD, 16 * C.val(LD))
    for c in range(8):
        if c > 0:
            g.wait_ge(MM, mm_n)
        for Pi, P in enumerate(("q", "k", "v", "g")):
            g.dma_start(out=wp_sb[:, Pi * KD * 128:(Pi + 1) * KD * 128].rearrange("p (kk m) -> p kk m", kk=KD),
                        in_=Ws[P][:, c * 128:(c + 1) * 128].rearrange("(kk p) m -> p kk m", p=128)).then_inc(LW, 16)
        C.add(LW, 4)
        t.wait_ge(LW, 16 * C.val(LW))
        cstart_sq = sq_n
        for nt in range(8):
            for Pi in range(4):
                for kk in range(KD):
                    li = t.matmul(ps[:, Pi * 512:(Pi + 1) * 512],
                                  wp_sb[:, Pi * KD * 128 + kk * 128:Pi * KD * 128 + (kk + 1) * 128],
                                  y_sb[:, kk * S + nt * 512:kk * S + (nt + 1) * 512],
                                  start=(kk == 0), stop=(kk == KD - 1))
            li.then_inc(MM, 1)
            mm_n += 1
            s.wait_ge(MM, mm_n)
            li = s.activation(ge_sb[:], ps[:, 1536:2048], AF.Sigmoid, bias=bias_sb[:, 3 * 8 + c:3 * 8 + c + 1])
            li.then_inc(SQ, 1)
            sq_n += 1
            v.wait_ge(MM, mm_n)
            v.tensor_scalar(qe_sb[:], ps[:, 0:512], bias_sb[:, 0 * 8 + c:0 * 8 + c + 1], None, ALU.add)
            v.tensor_scalar(ke_sb[:], ps[:, 512:1024], bias_sb[:, 1 * 8 + c:1 * 8 + c + 1], None, ALU.add)
            v.tensor_scalar(ve_sb[:], ps[:, 1024:1536], bias_sb[:, 2 * 8 + c:2 * 8 + c + 1], None, ALU.add)
            v.tensor_tensor(kv_sb[:, nt * 512:(nt + 1) * 512], ke_sb[:], ve_sb[:], ALU.mult)
            v.wait_ge(SQ, sq_n)
            li = v.tensor_tensor(qg_sb[:, nt * 512:(nt + 1) * 512], qe_sb[:], ge_sb[:], ALU.mult)
            li.then_inc(EV, 1)
            ev_n += 1
            t.wait_ge(SQ, sq_n)
            t.wait_ge(EV, ev_n)
        v.tensor_tensor_scan(state_sb[:, :], dec_sb[:, c:c + 1].to_broadcast([128, S]),
                             kv_sb[:, :], 0.0, ALU.mult, ALU.add)
        par = c % 2
        ST = ST0 if par == 0 else ST1
        if c >= 2:
            v.wait_ge(ST, 16 * C.val(ST))
        oo = par * S
        li = v.scalar_tensor_tensor(out_sb[:, oo:oo + S], state_sb[:, :], 0.125, qg_sb[:, :],
                                    ALU.mult, ALU.mult)
        li.then_inc(EV, 1)
        ev_n += 1
        g.wait_ge(EV, ev_n)
        g.dma_start(out=outT[c * 128:(c + 1) * 128, :], in_=out_sb[:, oo:oo + S]).then_inc(ST, 16)
        C.add(ST, 1)
    g.wait_ge(ST0, 16 * C.val(ST0))
    g.wait_ge(ST1, 16 * C.val(ST1))
    es.close()
    return nc



def build_l9c():
    nc = mknc()
    outT = nc.dram_tensor("outT", [D, S], BF16, kind="ExternalInput").ap()
    xn1T = nc.dram_tensor("xn1T", [D, S], BF16, kind="ExternalInput").ap()
    Wo = nc.dram_tensor("Wo", [D, D], F32, kind="ExternalInput").ap()
    bo = nc.dram_tensor("bo", [D, 1], F32, kind="ExternalInput").ap()
    ln2_g = nc.dram_tensor("ln2_g", [D, 1], F32, kind="ExternalInput").ap()
    ln2_b = nc.dram_tensor("ln2_b", [D, 1], F32, kind="ExternalInput").ap()
    W1 = nc.dram_tensor("W1", [D, 4096], F32, kind="ExternalInput").ap()
    b1 = nc.dram_tensor("b1", [4096, 1], F32, kind="ExternalInput").ap()
    W2 = nc.dram_tensor("W2", [4096, D], F32, kind="ExternalInput").ap()
    b2 = nc.dram_tensor("b2", [D, 1], F32, kind="ExternalInput").ap()
    s1T = nc.dram_tensor("s1T", [D, S], BF16, kind="ExternalOutput").ap()
    x2s = nc.dram_tensor("x2s", [D, S], F32, kind="Internal")
    hsrc = nc.dram_tensor("hsrc", [D, S], BF16, kind="Internal")
    hgs = nc.dram_tensor("hgs", [4096, S], BF16, kind="Internal")
    ones128 = nc.inline_tensor(np.ones((128, 1), np.float32), name="ones128")
    ones1x128 = nc.inline_tensor(np.ones((1, 128), np.float32), name="ones1x128")
    g, t, v, s, sy = nc.gpsimd, nc.tensor, nc.vector, nc.scalar, nc.sync
    es = ExitStack()
    pool = ConstPool(nc, es, [1e-5])
    w_sb = es.enter_context(nc.sbuf_tensor("w_sb", [128, 32 * 1024], BF16))
    wo_sb = es.enter_context(nc.sbuf_tensor("wo_sb", [128, KD * 1024], BF16))
    a_sb = es.enter_context(nc.sbuf_tensor("a_sb", [128, 2 * KD * 512], BF16))
    b_sb = es.enter_context(nc.sbuf_tensor("b_sb", [128, 2 * KD * 512], BF16))
    x2_sb = es.enter_context(nc.sbuf_tensor("x2_sb", [128, 2 * KD * 512], F32))
    h_sb = es.enter_context(nc.sbuf_tensor("h_sb", [128, 2 * KD * 512], BF16))
    hg_sb = es.enter_context(nc.sbuf_tensor("hg_sb", [128, 2 * 512], BF16))
    sq_sb = es.enter_context(nc.sbuf_tensor("sq_sb", [128, KD * 512], F32))
    t1_sb = es.enter_context(nc.sbuf_tensor("t1_sb", [128, 512], F32))
    st_sb = es.enter_context(nc.sbuf_tensor("st_sb", [1, 4 * 512], F32))
    bvec_sb = es.enter_context(nc.sbuf_tensor("bvec_sb", [128, 64], F32))
    ones_sb = es.enter_context(nc.sbuf_tensor("ones_sb", [128, 129], F32))
    ones_bsb = es.enter_context(nc.sbuf_tensor("ones_bsb", [128, 1], BF16))
    ps = es.enter_context(nc.psum_tensor("ps", [128, 8 * 512], F32))
    LD0 = es.enter_context(nc.semaphore("LD0"))
    LD1 = es.enter_context(nc.semaphore("LD1"))
    LW = es.enter_context(nc.semaphore("LW"))
    MM = es.enter_context(nc.semaphore("MM"))
    EV = es.enter_context(nc.semaphore("EV"))
    SQ = es.enter_context(nc.semaphore("SQ"))
    ST0 = es.enter_context(nc.semaphore("ST0"))
    ST1 = es.enter_context(nc.semaphore("ST1"))
    C = Ctr()
    mm_n = ev_n = sq_n = 0
    g.dma_start(out=ones_bsb[:, 0:1], in_=ones128.ap()[:, :]).then_inc(LW, 16)
    C.add(LW, 1)
    sy.dma_start(out=ones_sb[:, 0:1], in_=ones128.ap()[:, :]).then_inc(LD0, 16)
    sy.dma_start(out=ones_sb[0:1, 1:129], in_=ones1x128.ap()[:, :]).then_inc(LD0, 16)
    C.add(LD0, 2)
    for k in range(KD):
        sy.dma_start(out=bvec_sb[:, k:k + 1], in_=bo[k * 128:(k + 1) * 128, :]).then_inc(LD0, 16)
        sy.dma_start(out=bvec_sb[:, 8 + k:9 + k], in_=ln2_g[k * 128:(k + 1) * 128, :]).then_inc(LD0, 16)
        sy.dma_start(out=bvec_sb[:, 16 + k:17 + k], in_=ln2_b[k * 128:(k + 1) * 128, :]).then_inc(LD0, 16)
        sy.dma_start(out=bvec_sb[:, 56 + k:57 + k], in_=b2[k * 128:(k + 1) * 128, :]).then_inc(LD0, 16)
    for m in range(32):
        sy.dma_start(out=bvec_sb[:, 24 + m:25 + m], in_=b1[m * 128:(m + 1) * 128, :]).then_inc(LD0, 16)
    C.add(LD0, 4 * KD + 32)
    for kk in range(KD):
        g.dma_start(out=wo_sb[:, kk * 1024:(kk + 1) * 1024],
                    in_=Wo[kk * 128:(kk + 1) * 128, :]).then_inc(LW, 16)
    C.add(LW, KD)

    def ln_stats2(src_sb, xo):
        nonlocal mm_n, ev_n, sq_n
        o1 = ones_sb[:, 0:1] if src_sb.dtype == F32 else ones_bsb[:, 0:1]
        s.wait_ge(EV, ev_n)
        for k in range(KD):
            li = s.activation(sq_sb[:, k * 512:(k + 1) * 512],
                              src_sb[:, xo + k * 512:xo + (k + 1) * 512], AF.Square)
        li.then_inc(SQ, 1)
        sq_n += 1
        for k in range(KD):
            t.matmul(ps[0:1, 0:512], o1, src_sb[:, xo + k * 512:xo + (k + 1) * 512],
                     start=(k == 0), stop=(k == KD - 1))
        t.wait_ge(SQ, sq_n)
        for k in range(KD):
            li = t.matmul(ps[0:1, 512:1024], ones_sb[:, 0:1], sq_sb[:, k * 512:(k + 1) * 512],
                          start=(k == 0), stop=(k == KD - 1))
        li.then_inc(MM, 1)
        mm_n += 1
        v.wait_ge(MM, mm_n)
        v.tensor_scalar(st_sb[0:1, 0:512], ps[0:1, 0:512], 1.0 / D, None, ALU.mult)
        v.tensor_scalar(st_sb[0:1, 512:1024], ps[0:1, 512:1024], 1.0 / D, None, ALU.mult)
        v.tensor_tensor(st_sb[0:1, 1024:1536], st_sb[0:1, 0:512], st_sb[0:1, 0:512], ALU.mult)
        v.tensor_tensor(st_sb[0:1, 1024:1536], st_sb[0:1, 512:1024], st_sb[0:1, 1024:1536], ALU.subtract).then_inc(EV, 1)
        ev_n += 1
        s.wait_ge(EV, ev_n)
        s.activation(st_sb[0:1, 1024:1536], st_sb[0:1, 1024:1536], AF.Sqrt, bias=pool.ap(1e-5, rows=1)).then_inc(SQ, 1)
        sq_n += 1
        v.wait_ge(SQ, sq_n)
        v.reciprocal(st_sb[0:1, 1024:1536], st_sb[0:1, 1024:1536])
        v.tensor_tensor(st_sb[0:1, 1536:2048], st_sb[0:1, 0:512], st_sb[0:1, 1024:1536], ALU.mult).then_inc(EV, 1)
        ev_n += 1
        t.wait_ge(EV, ev_n)
        t.matmul(ps[:, 2 * 512:3 * 512], ones_sb[0:1, 1:129], st_sb[0:1, 1024:1536], start=True, stop=True)
        li = t.matmul(ps[:, 3 * 512:4 * 512], ones_sb[0:1, 1:129], st_sb[0:1, 1536:2048], start=True, stop=True)
        li.then_inc(MM, 1)
        mm_n += 1
        v.wait_ge(MM, mm_n)

    s.wait_ge(pool.CN, 1)
    mm_n_c2a_gate = {}
    c2b_cons = {}
    t.wait_ge(LW, 16 * C.val(LW))
    t.wait_ge(LD0, 16 * C.val(LD0))
    cons = {}
    for nt in range(8):
        par = nt % 2
        LD = LD0 if par == 0 else LD1
        ST = ST0 if par == 0 else ST1
        xo = par * KD * 512
        tok0 = nt * 512
        if nt >= 2:
            sy.wait_ge(EV, cons[nt - 2])
            sy.wait_ge(LD, 16 * C.val(LD))
        for k in range(KD):
            sy.dma_start(out=a_sb[:, xo + k * 512:xo + (k + 1) * 512],
                         in_=outT[k * 128:(k + 1) * 128, tok0:tok0 + 512]).then_inc(LD, 16)
            sy.dma_start(out=b_sb[:, xo + k * 512:xo + (k + 1) * 512],
                         in_=xn1T[k * 128:(k + 1) * 128, tok0:tok0 + 512]).then_inc(LD, 16)
        C.add(LD, 2 * KD)
        t.wait_ge(LD, 16 * C.val(LD))
        v.wait_ge(LD, 16 * C.val(LD))
        if nt >= 2:
            v.wait_ge(ST, 16 * C.val(ST))
        for m in range(KD):
            bank = 4 + m % 4
            for kk in range(KD):
                li = t.matmul(ps[:, bank * 512:(bank + 1) * 512],
                              wo_sb[:, kk * 1024 + m * 128:kk * 1024 + (m + 1) * 128],
                              a_sb[:, xo + kk * 512:xo + (kk + 1) * 512],
                              start=(kk == 0), stop=(kk == KD - 1))
            li.then_inc(MM, 1)
            mm_n += 1
            v.wait_ge(MM, mm_n)
            v.tensor_scalar(t1_sb[:], ps[:, bank * 512:(bank + 1) * 512],
                            bvec_sb[:, m:m + 1], None, ALU.add)
            li = v.tensor_tensor(x2_sb[:, xo + m * 512:xo + (m + 1) * 512], t1_sb[:],
                                 b_sb[:, xo + m * 512:xo + (m + 1) * 512], ALU.add)
            li.then_inc(EV, 1)
            ev_n += 1
            if m >= 3:
                t.wait_ge(EV, ev_n - 3)
        t.wait_ge(EV, ev_n)
        ln_stats2(x2_sb, xo)
        for k in range(KD):
            v.tensor_tensor(t1_sb[:], x2_sb[:, xo + k * 512:xo + (k + 1) * 512], ps[:, 2 * 512:3 * 512], ALU.mult)
            v.tensor_tensor(t1_sb[:], t1_sb[:], ps[:, 3 * 512:4 * 512], ALU.subtract)
            li = v.tensor_scalar(h_sb[:, xo + k * 512:xo + (k + 1) * 512], t1_sb[:],
                                 bvec_sb[:, 8 + k:9 + k], bvec_sb[:, 16 + k:17 + k], ALU.mult, ALU.add)
        li.then_inc(EV, 1)
        ev_n += 1
        cons[nt] = ev_n
        g.wait_ge(EV, ev_n)
        for k in range(KD):
            g.dma_start(out=x2s.ap()[k * 128:(k + 1) * 128, tok0:tok0 + 512],
                        in_=x2_sb[:, xo + k * 512:xo + (k + 1) * 512]).then_inc(ST, 16)
            g.dma_start(out=hsrc.ap()[k * 128:(k + 1) * 128, tok0:tok0 + 512],
                        in_=h_sb[:, xo + k * 512:xo + (k + 1) * 512]).then_inc(ST, 16)
        C.add(ST, 2 * KD)

    g.wait_ge(MM, mm_n)
    for kk in range(KD):
        g.dma_start(out=w_sb[:, kk * 4096:(kk + 1) * 4096],
                    in_=W1[kk * 128:(kk + 1) * 128, :]).then_inc(LW, 16)
    C.add(LW, KD)
    t.wait_ge(LW, 16 * C.val(LW))
    hg_ev = {}
    for nt in range(8):
        par = nt % 2
        LD = LD0 if par == 0 else LD1
        ST = ST0 if par == 0 else ST1
        xo = par * KD * 512
        tok0 = nt * 512
        sy.wait_ge(ST, 16 * C.val(ST))
        if nt >= 2:
            sy.wait_ge(MM, mm_n_c2a_gate[nt - 2])
            sy.wait_ge(LD, 16 * C.val(LD))
        for k in range(KD):
            sy.dma_start(out=a_sb[:, xo + k * 512:xo + (k + 1) * 512],
                         in_=hsrc.ap()[k * 128:(k + 1) * 128, tok0:tok0 + 512]).then_inc(LD, 16)
        C.add(LD, KD)
        t.wait_ge(LD, 16 * C.val(LD))
        for m in range(32):
            bank = m % 4
            for kk in range(KD):
                li = t.matmul(ps[:, bank * 512:(bank + 1) * 512],
                              w_sb[:, kk * 4096 + m * 128:kk * 4096 + (m + 1) * 128],
                              a_sb[:, xo + kk * 512:xo + (kk + 1) * 512],
                              start=(kk == 0), stop=(kk == KD - 1))
            li.then_inc(MM, 1)
            mm_n += 1
            hpar = m % 2
            s.wait_ge(MM, mm_n)
            if m >= 2:
                s.wait_ge(ST0 if hpar == 0 else ST1, 16 * C.val(ST0 if hpar == 0 else ST1))
            li = s.activation(hg_sb[:, hpar * 512:(hpar + 1) * 512], ps[:, bank * 512:(bank + 1) * 512],
                              AF.Gelu, bias=bvec_sb[:, 24 + m:25 + m])
            li.then_inc(SQ, 1)
            sq_n += 1
            g.wait_ge(SQ, sq_n)
            g.dma_start(out=hgs.ap()[m * 128:(m + 1) * 128, tok0:tok0 + 512],
                        in_=hg_sb[:, hpar * 512:(hpar + 1) * 512]).then_inc(ST0 if hpar == 0 else ST1, 16)
            C.add(ST0 if hpar == 0 else ST1, 1)
            if m >= 3:
                t.wait_ge(SQ, sq_n - 3)
        mm_n_c2a_gate[nt] = mm_n
    g.wait_ge(MM, mm_n)
    g.wait_ge(ST0, 16 * C.val(ST0))
    g.wait_ge(ST1, 16 * C.val(ST1))
    sy.wait_ge(ST0, 16 * C.val(ST0))
    sy.wait_ge(ST1, 16 * C.val(ST1))
    for kk in range(32):
        g.dma_start(out=w_sb[:, kk * 1024:(kk + 1) * 1024],
                    in_=W2[kk * 128:(kk + 1) * 128, :]).then_inc(LW, 16)
    C.add(LW, 32)
    t.wait_ge(LW, 16 * C.val(LW))
    for nt in range(8):
        par = nt % 2
        LD = LD0 if par == 0 else LD1
        ST = ST0 if par == 0 else ST1
        xo = par * KD * 512
        tok0 = nt * 512
        if nt >= 2:
            sy.wait_ge(EV, c2b_cons[nt - 2])
            sy.wait_ge(LD, 16 * C.val(LD))
        for k in range(KD):
            sy.dma_start(out=x2_sb[:, xo + k * 512:xo + (k + 1) * 512],
                         in_=x2s.ap()[k * 128:(k + 1) * 128, tok0:tok0 + 512]).then_inc(LD, 16)
        C.add(LD, KD)
        tile_mm_base = mm_n
        for kk in range(32):
            hp = 0
            if kk >= 1:
                g.wait_ge(MM, tile_mm_base + kk)
            g.dma_start(out=b_sb[:, xo + hp * 512:xo + (hp + 1) * 512],
                        in_=hgs.ap()[kk * 128:(kk + 1) * 128, tok0:tok0 + 512]).then_inc(LW, 16)
            C.add(LW, 1)
            t.wait_ge(LW, 16 * C.val(LW))
            for m in range(KD):
                li = t.matmul(ps[:, m * 512:(m + 1) * 512],
                              w_sb[:, kk * 1024 + m * 128:kk * 1024 + (m + 1) * 128],
                              b_sb[:, xo + hp * 512:xo + (hp + 1) * 512],
                              start=(kk == 0), stop=(kk == 31))
            li.then_inc(MM, 1)
            mm_n += 1
        v.wait_ge(MM, mm_n)
        v.wait_ge(LD, 16 * C.val(LD))
        if nt >= 2:
            v.wait_ge(ST, 16 * C.val(ST))
        for m in range(KD):
            v.tensor_scalar(t1_sb[:], ps[:, m * 512:(m + 1) * 512],
                            bvec_sb[:, 56 + m:57 + m], None, ALU.add)
            li = v.tensor_tensor(h_sb[:, xo + m * 512:xo + (m + 1) * 512], t1_sb[:],
                                 x2_sb[:, xo + m * 512:xo + (m + 1) * 512], ALU.add)
        li.then_inc(EV, 1)
        ev_n += 1
        c2b_cons[nt] = ev_n
        t.wait_ge(EV, ev_n)
        g.wait_ge(EV, ev_n)
        for k in range(KD):
            g.dma_start(out=s1T[k * 128:(k + 1) * 128, tok0:tok0 + 512],
                        in_=h_sb[:, xo + k * 512:xo + (k + 1) * 512]).then_inc(ST, 16)
        C.add(ST, KD)
    g.wait_ge(ST0, 16 * C.val(ST0))
    g.wait_ge(ST1, 16 * C.val(ST1))
    es.close()
    return nc



CAP = 768
NBIS = 40
L10_PHASE = 3
L10_NEXP = 8
L10_P2CUT = 5

def build_l10():
    nc = mknc()
    probsA = nc.dram_tensor("probsA", [8, 32768], F32, kind="ExternalInput").ap()
    probsO = nc.dram_tensor("probsO", [8, 4096], F32, kind="ExternalInput").ap()
    xn1tm = nc.dram_tensor("xn1tm", [4096, 1024], BF16, kind="ExternalInput").ap()
    xn1T = nc.dram_tensor("xn1T", [D, S], BF16, kind="ExternalInput").ap()
    x1T = nc.dram_tensor("x1T", [D, S], F32, kind="ExternalInput").ap()
    s1T = nc.dram_tensor("s1T", [D, S], BF16, kind="ExternalInput").ap()
    W1e = nc.dram_tensor("W1e", [8, 1024, 4096], F32, kind="ExternalInput").ap()
    W2e = nc.dram_tensor("W2e", [8, 4096, 1024], F32, kind="ExternalInput").ap()
    b1et = nc.dram_tensor("b1et", [8, 128, 32], F32, kind="ExternalInput").ap()
    b2et = nc.dram_tensor("b2et", [8, 128, 8], F32, kind="ExternalInput").ap()
    Ws1 = nc.dram_tensor("Ws1", [D, 4096], F32, kind="ExternalInput").ap()
    bs1 = nc.dram_tensor("bs1", [4096, 1], F32, kind="ExternalInput").ap()
    Ws2 = nc.dram_tensor("Ws2", [4096, D], F32, kind="ExternalInput").ap()
    bs2 = nc.dram_tensor("bs2", [D, 1], F32, kind="ExternalInput").ap()
    outF = nc.dram_tensor("outF", [D, S], F32, kind="ExternalOutput").ap()
    dbg = nc.dram_tensor("dbg", [8 * CAP, 2], F32, kind="ExternalOutput").ap()
    rdbg = nc.dram_tensor("rdbg", [4097, 1024], F32, kind="ExternalOutput").ap()
    routed = nc.dram_tensor("routed", [4097, 1024], F32, kind="Internal")
    compact = nc.dram_tensor("compact", [8 * CAP, 2], F32, kind="Internal")
    hgs2 = nc.dram_tensor("hgs2", [4096, S], BF16, kind="Internal")

    esel8 = np.zeros((8, 128), np.float32)
    for e in range(8):
        esel8[e, e * 16:(e + 1) * 16] = 1.0
    eselT = np.ascontiguousarray(esel8.T)
    eye8 = np.eye(8, dtype=np.float32)
    on8 = np.ones((8, 128), np.float32)
    ltri = np.tril(np.ones((128, 128), np.float32), -1)
    ident = np.eye(128, dtype=np.float32)
    esel8_t = nc.inline_tensor(esel8, name="esel8")
    eselT_t = nc.inline_tensor(eselT, name="eselT")
    eye8_t = nc.inline_tensor(eye8, name="eye8")
    on8_t = nc.inline_tensor(on8, name="on8")
    ltri_t = nc.inline_tensor(ltri, name="ltri")
    idb_t = nc.inline_tensor(ident, name="idb")
    ones128 = nc.inline_tensor(np.ones((128, 1), np.float32), name="ones128")
    ones1x128 = nc.inline_tensor(np.ones((1, 128), np.float32), name="ones1x128")

    g, t, v, s, sy = nc.gpsimd, nc.tensor, nc.vector, nc.scalar, nc.sync
    es = ExitStack()
    pool = ConstPool(nc, es, [1e-5])
    wa_sb = es.enter_context(nc.sbuf_tensor("wa_sb", [128, 8 * 4096], BF16))
    wb_sb = es.enter_context(nc.sbuf_tensor("wb_sb", [128, 32 * 1024], BF16))
    tokf_sb = es.enter_context(nc.sbuf_tensor("tokf_sb", [128, 8 * CAP], BF16))
    hq_sb = es.enter_context(nc.sbuf_tensor("hq_sb", [128, 32 * 192], BF16))
    otf_sb = es.enter_context(nc.sbuf_tensor("otf_sb", [128, 8 * CAP], BF16))
    pa_sb = es.enter_context(nc.sbuf_tensor("pa_sb", [128, 2048], F32))
    x1_sb = es.enter_context(nc.sbuf_tensor("x1_sb", [128, 4 * 512], F32))
    a_sb = es.enter_context(nc.sbuf_tensor("a_sb", [128, KD * 512], BF16))
    otm_sb = es.enter_context(nc.sbuf_tensor("otm_sb", [128, 1024], F32))
    tokr_sb = es.enter_context(nc.sbuf_tensor("tokr_sb", [128, 1024], BF16))
    zz_sb = es.enter_context(nc.sbuf_tensor("zz_sb", [128, 256], F32))
    hgp_sb = es.enter_context(nc.sbuf_tensor("hgp_sb", [128, 2 * 512], BF16))
    t1_sb = es.enter_context(nc.sbuf_tensor("t1_sb", [128, 512], F32))
    t2_sb = es.enter_context(nc.sbuf_tensor("t2_sb", [128, 512], F32))
    po_sb = es.enter_context(nc.sbuf_tensor("po_sb", [128, 8 * 32], F32))
    bis_sb = es.enter_context(nc.sbuf_tensor("bis_sb", [8, 16], F32))
    thbc_sb = es.enter_context(nc.sbuf_tensor("thbc_sb", [128, 4], F32))
    thall_sb = es.enter_context(nc.sbuf_tensor("thall_sb", [128, 8], F32))
    sel_sb = es.enter_context(nc.sbuf_tensor("sel_sb", [128, 32], F32))
    cum_sb = es.enter_context(nc.sbuf_tensor("cum_sb", [128, 72], F32))
    iota_sb = es.enter_context(nc.sbuf_tensor("iota_sb", [128, 32], F32))
    pay_sb = es.enter_context(nc.sbuf_tensor("pay_sb", [128, 64], F32))
    fil_sb = es.enter_context(nc.sbuf_tensor("fil_sb", [128, 96], F32))
    pos_sb = es.enter_context(nc.sbuf_tensor("pos_sb", [128, 32], I32))
    ones_sb = es.enter_context(nc.sbuf_tensor("ones_sb", [128, 129], F32))
    lt_sb = es.enter_context(nc.sbuf_tensor("lt_sb", [128, 128], F32))
    es8_sb = es.enter_context(nc.sbuf_tensor("es8_sb", [8, 264], F32))
    esT_sb = es.enter_context(nc.sbuf_tensor("esT_sb", [128, 8], F32))
    idb_sb = es.enter_context(nc.sbuf_tensor("idb_sb", [128, 128], BF16))
    idf_sb = es.enter_context(nc.sbuf_tensor("idf_sb", [128, 128], F32))
    cidx_sb = es.enter_context(nc.sbuf_tensor("cidx_sb", [128, 12], F32))
    idxi_sb = es.enter_context(nc.sbuf_tensor("idxi_sb", [128, 6], I32))
    bv_sb = es.enter_context(nc.sbuf_tensor("bv_sb", [128, 64], F32))
    rf_sb = pa_sb[:, :].bitcast(BF16)
    ps = es.enter_context(nc.psum_tensor("ps", [128, 7 * 512], F32))
    psb = es.enter_context(nc.psum_tensor("psb", [128, 1024], BF16))
    LD = es.enter_context(nc.semaphore("LD"))
    LD0 = es.enter_context(nc.semaphore("LD0"))
    LD1 = es.enter_context(nc.semaphore("LD1"))
    LW0 = es.enter_context(nc.semaphore("LW0"))
    LW1 = es.enter_context(nc.semaphore("LW1"))
    MM = es.enter_context(nc.semaphore("MM"))
    EV = es.enter_context(nc.semaphore("EV"))
    SQ = es.enter_context(nc.semaphore("SQ"))
    SC = es.enter_context(nc.semaphore("SC"))
    ST0 = es.enter_context(nc.semaphore("ST0"))
    ST1 = es.enter_context(nc.semaphore("ST1"))
    C = Ctr()
    mm_n = ev_n = sq_n = 0

    sy.dma_start(out=ones_sb[:, 0:1], in_=ones128.ap()[:, :]).then_inc(LD, 16)
    sy.dma_start(out=ones_sb[0:1, 1:129], in_=ones1x128.ap()[:, :]).then_inc(LD, 16)
    sy.dma_start(out=lt_sb[:, :], in_=ltri_t.ap()[:, :]).then_inc(LD, 16)
    sy.dma_start(out=es8_sb[0:8, 0:128], in_=esel8_t.ap()[:, :]).then_inc(LD, 16)
    sy.dma_start(out=es8_sb[0:8, 128:136], in_=eye8_t.ap()[:, :]).then_inc(LD, 16)
    sy.dma_start(out=es8_sb[0:8, 136:264], in_=on8_t.ap()[:, :]).then_inc(LD, 16)
    sy.dma_start(out=esT_sb[:, :], in_=eselT_t.ap()[:, :]).then_inc(LD, 16)
    g.dma_start(out=idb_sb[:, :], in_=idb_t.ap()[:, :]).then_inc(SC, 16)
    sy.dma_start(out=idf_sb[:, :], in_=idb_t.ap()[:, :]).then_inc(LD, 16)
    for e in range(8):
        sy.dma_start(out=pa_sb[e * 16:(e + 1) * 16, :],
                     in_=probsA[e:e + 1, :].rearrange("one (tc j) -> (one tc) j", tc=16)).then_inc(LD, 16)
        sy.dma_start(out=po_sb[:, e * 32:(e + 1) * 32],
                     in_=probsO[e:e + 1, :].rearrange("one (p j) -> (one p) j", p=128)).then_inc(LD, 16)
    C.add(LD, 8 + 16)
    C.add(SC, 1)
    g.iota(iota_sb[:, :], pattern=[[1, 32]], base=0, channel_multiplier=32,
           allow_small_or_imprecise_dtypes=True).then_inc(EV, 1)
    ev_n += 1

    def vch(op):
        nonlocal ev_n
        op.then_inc(EV, 1)
        ev_n += 1
        v.wait_ge(EV, ev_n)

    v.memset(bis_sb[0:8, 0:1], 0.0)
    v.memset(bis_sb[0:8, 1:2], 1.0)
    vch(v.memset(bis_sb[0:8, 2:3], 0.5))
    v.wait_ge(LD, 16 * C.val(LD))
    s.wait_ge(LD, 16 * C.val(LD))
    t.wait_ge(LD, 16 * C.val(LD))
    s.wait_ge(pool.CN, 1)
    for it in range(NBIS):
        vch(v.tensor_copy(bis_sb[0:8, 4:8], bis_sb[0:8, 2:3].to_broadcast([8, 4])))
        t.wait_ge(EV, ev_n)
        t.matmul(ps[:, 0:4], es8_sb[0:8, 0:128], bis_sb[0:8, 4:8], start=True, stop=True).then_inc(MM, 1)
        mm_n += 1
        v.wait_ge(MM, mm_n)
        vch(v.tensor_scalar(thbc_sb[:, 0:1], ps[:, 0:1], -1.0, None, ALU.mult))
        s.wait_ge(EV, ev_n)
        s.activation(otf_sb[:, 0:2048], pa_sb[:, :], AF.Sign, bias=thbc_sb[:, 0:1],
                     accum_out=thbc_sb[:, 1:2]).then_inc(SQ, 1)
        sq_n += 1
        t.wait_ge(SQ, sq_n)
        t.matmul(ps[0:8, 512:520], esT_sb[:, :], thbc_sb[:, 1:2].to_broadcast([128, 8]),
                 start=True, stop=True).then_inc(MM, 1)
        mm_n += 1
        v.wait_ge(MM, mm_n)
        vch(v.tensor_scalar(bis_sb[0:8, 3:4], ps[0:8, 512:513], 0.5, 16384.0, ALU.mult, ALU.add))
        vch(v.tensor_single_scalar(bis_sb[0:8, 4:5], bis_sb[0:8, 3:4], 4096.0, ALU.is_ge))
        vch(v.tensor_tensor(bis_sb[0:8, 5:6], bis_sb[0:8, 2:3], bis_sb[0:8, 0:1], ALU.subtract))
        vch(v.tensor_tensor(bis_sb[0:8, 5:6], bis_sb[0:8, 5:6], bis_sb[0:8, 4:5], ALU.mult))
        vch(v.tensor_tensor(bis_sb[0:8, 0:1], bis_sb[0:8, 0:1], bis_sb[0:8, 5:6], ALU.add))
        vch(v.tensor_tensor(bis_sb[0:8, 6:7], bis_sb[0:8, 2:3], bis_sb[0:8, 1:2], ALU.subtract))
        vch(v.tensor_tensor(bis_sb[0:8, 7:8], bis_sb[0:8, 6:7], bis_sb[0:8, 4:5], ALU.mult))
        vch(v.tensor_tensor(bis_sb[0:8, 6:7], bis_sb[0:8, 6:7], bis_sb[0:8, 7:8], ALU.subtract))
        vch(v.tensor_tensor(bis_sb[0:8, 1:2], bis_sb[0:8, 1:2], bis_sb[0:8, 6:7], ALU.add))
        vch(v.tensor_tensor(bis_sb[0:8, 2:3], bis_sb[0:8, 0:1], bis_sb[0:8, 1:2], ALU.add))
        vch(v.tensor_scalar(bis_sb[0:8, 2:3], bis_sb[0:8, 2:3], 0.5, None, ALU.mult))
    vch(v.tensor_scalar(bis_sb[0:8, 8:16], es8_sb[0:8, 128:136], bis_sb[0:8, 0:1], None, ALU.mult))
    t.wait_ge(EV, ev_n)
    t.matmul(ps[:, 0:8], es8_sb[0:8, 136:264], bis_sb[0:8, 8:16], start=True, stop=True).then_inc(MM, 1)
    mm_n += 1
    v.wait_ge(MM, mm_n)
    vch(v.tensor_copy(thall_sb[:, :], ps[:, 0:8]))

    v.memset(zz_sb[:, :], 0.0)
    v.memset(fil_sb[:, :], 0.0)
    vch(v.memset(fil_sb[:, 0:96:2], 4096.0))
    g.wait_ge(EV, ev_n)
    g.dma_start(out=compact.ap()[:, :].rearrange("(p j) two -> p (j two)", p=128),
                in_=fil_sb[:, 0:96]).then_inc(SC, 16)
    C.add(SC, 1)
    for r0 in range(0, 4097, 128):
        rn = min(128, 4097 - r0)
        for h2 in range(4):
            g.dma_start(out=routed.ap()[r0:r0 + rn, h2 * 256:(h2 + 1) * 256],
                        in_=zz_sb[0:rn, :]).then_inc(SC, 16)
            C.add(SC, 1)
    bndreg = g.alloc_register("bndreg")
    for e in range(8):
        vch(v.tensor_scalar(sel_sb[:, :], po_sb[:, e * 32:(e + 1) * 32], thall_sb[:, e:e + 1], None, ALU.is_gt))
        vch(v.tensor_tensor_scan(cum_sb[:, 0:32], ones_sb[:, 0:1].to_broadcast([128, 32]),
                                 sel_sb[:, :], 0.0, ALU.mult, ALU.add))
        vch(v.reduce_sum(cum_sb[:, 32:33], sel_sb[:, :], axis=AXX))
        t.wait_ge(EV, ev_n)
        t.matmul(ps[:, 0:4], lt_sb[:, :], cum_sb[:, 32:33].to_broadcast([128, 4]),
                 start=True, stop=True).then_inc(MM, 1)
        mm_n += 1
        v.wait_ge(MM, mm_n)
        vch(v.tensor_scalar(cum_sb[:, 0:32], cum_sb[:, 0:32], ps[:, 0:1], None, ALU.add))
        vch(v.tensor_scalar(cum_sb[:, 0:32], cum_sb[:, 0:32], float(e * CAP - 1), None, ALU.add))
        vch(v.tensor_tensor(cum_sb[:, 0:32], cum_sb[:, 0:32], sel_sb[:, :], ALU.mult))
        vch(v.tensor_scalar(cum_sb[:, 34:66], sel_sb[:, :], -1e9, 1e9, ALU.mult, ALU.add))
        vch(v.tensor_tensor(cum_sb[:, 0:32], cum_sb[:, 0:32], cum_sb[:, 34:66], ALU.add))
        vch(v.tensor_copy(pos_sb[:, :], cum_sb[:, 0:32]))
        vch(v.tensor_copy(pay_sb[:, 0:64:2], iota_sb[:, :]))
        vch(v.tensor_copy(pay_sb[:, 1:64:2], po_sb[:, e * 32:(e + 1) * 32]))
        g.wait_ge(EV, ev_n)
        g.wait_ge(SC, 16 * C.val(SC))
        g.reg_mov(bndreg, (e + 1) * CAP - 1)
        for j in range(32):
            g.indirect_dma_start(compact.ap()[:, :],
                                 IndirectOffsetOnAxis(pos_sb[:, j:j + 1], 0),
                                 pay_sb[:, 2 * j:2 * j + 2], None,
                                 bounds_check=bndreg, oob_is_err=False).then_inc(SC, 16)
        C.add(SC, 32)
        v.wait_ge(SC, 16 * C.val(SC))
    g.dma_start(out=dbg[:, :].rearrange("(p j) two -> p (j two)", p=128),
                in_=compact.ap()[:, :].rearrange("(p j) two -> p (j two)", p=128)).then_inc(SC, 16)
    C.add(SC, 1)

    if L10_PHASE < 2:
        g.wait_ge(SC, 16 * C.val(SC))
        v.wait_ge(EV, ev_n)
        for nt in range(8):
            for k in range(KD):
                for h4 in range(2):
                    g.dma_start(out=outF[k * 128:(k + 1) * 128, nt * 512 + h4 * 256:nt * 512 + (h4 + 1) * 256],
                                in_=zz_sb[:, :]).then_inc(ST0, 16)
                    C.add(ST0, 1)
        g.wait_ge(ST0, 16 * C.val(ST0))
        es.close()
        return nc

    bndreg2 = g.alloc_register("bndreg2")
    g.reg_mov(bndreg2, 4095)
    MME = {}
    for e in range(L10_NEXP if L10_PHASE >= 2 else 0):
        if e > 0:
            g.wait_ge(MM, MME[e - 1])
        for kk in range(KD):
            g.dma_start(out=wa_sb[:, kk * 4096:(kk + 1) * 4096],
                        in_=W1e[e, kk * 128:(kk + 1) * 128, :]).then_inc(LW0, 16)
        C.add(LW0, KD)
        for kk in range(32):
            g.dma_start(out=wb_sb[:, kk * 1024:(kk + 1) * 1024],
                        in_=W2e[e, kk * 128:(kk + 1) * 128, :]).then_inc(LW1, 16)
        C.add(LW1, 32)
        g.dma_start(out=bv_sb[:, 0:32], in_=b1et[e, :, :]).then_inc(LW0, 16)
        g.dma_start(out=bv_sb[:, 32:40], in_=b2et[e, :, :]).then_inc(LW0, 16)
        C.add(LW0, 2)
        sy.wait_ge(SC, 16 * C.val(SC))
        sy.dma_start(out=cidx_sb[:, :].rearrange("p (gr two) -> p gr two", two=2),
                     in_=compact.ap()[e * CAP:(e + 1) * CAP, :].rearrange("(gr p) two -> p gr two", p=128)).then_inc(LD0, 16)
        C.add(LD0, 1)
        v.wait_ge(LD0, 16 * C.val(LD0))
        if e > 0:
            v.wait_ge(SC, 16 * C.val(SC))
        vch(v.tensor_copy(idxi_sb[:, :], cidx_sb[:, 0:12:2]))
        if L10_P2CUT < 2:
            MME[e] = mm_n
            continue
        g.wait_ge(EV, ev_n)
        base_mm = mm_n
        base_sc = C.val(SC)
        base_ev = ev_n
        for gr in range(6):
            if gr >= 1:
                g.wait_ge(MM, base_mm + gr)
            g.indirect_dma_start(tokr_sb[:, :], None,
                                 xn1tm[:, :],
                                 IndirectOffsetOnAxis(idxi_sb[:, gr:gr + 1], 0),
                                 bounds_check=bndreg2, oob_is_err=False).then_inc(SC, 16)
            C.add(SC, 1)
            t.wait_ge(SC, 16 * (base_sc + gr + 1))
            if gr >= 1:
                t.wait_ge(EV, base_ev + gr)
            for kk in range(KD):
                li = t.transpose(psb[:, kk * 128:(kk + 1) * 128],
                                 tokr_sb[:, kk * 128:(kk + 1) * 128], idb_sb[:, :])
            li.then_inc(MM, 1)
            mm_n += 1
            v.wait_ge(MM, mm_n)
            li = v.tensor_copy(
                tokf_sb[:, 0:8 * CAP].rearrange("p (kk r) -> p kk r", kk=KD)[:, :, gr * 128:(gr + 1) * 128],
                psb[:, :].rearrange("p (kk r) -> p kk r", kk=KD))
            li.then_inc(EV, 1)
            ev_n += 1
        if L10_P2CUT < 3:
            MME[e] = mm_n
            continue
        t.wait_ge(EV, ev_n)
        t.wait_ge(LW0, 16 * C.val(LW0))
        t.wait_ge(LW1, 16 * C.val(LW1))
        s.wait_ge(SQ, sq_n)
        for q in range(4):
            q0 = q * 192
            sq_base = sq_n
            for m in range(32):
                bank = 3 + m % 4
                if m >= 4:
                    t.wait_ge(SQ, sq_base + m - 3)
                for kk in range(KD):
                    li = t.matmul(ps[:, bank * 512:bank * 512 + 192],
                                  wa_sb[:, kk * 4096 + m * 128:kk * 4096 + (m + 1) * 128],
                                  tokf_sb[:, kk * CAP + q0:kk * CAP + q0 + 192],
                                  start=(kk == 0), stop=(kk == KD - 1))
                li.then_inc(MM, 1)
                mm_n += 1
                s.wait_ge(MM, mm_n)
                li = s.activation(hq_sb[:, m * 192:(m + 1) * 192], ps[:, bank * 512:bank * 512 + 192],
                                  AF.Gelu_apprx_tanh, bias=bv_sb[:, m:m + 1])
                li.then_inc(SQ, 1)
                sq_n += 1
            if L10_P2CUT < 4:
                continue
            t.wait_ge(SQ, sq_n)
            for mh in range(2):
                if mh == 1:
                    t.wait_ge(EV, ev_n)
                for kk in range(32):
                    for m4 in range(4):
                        m2 = mh * 4 + m4
                        li = t.matmul(ps[:, m4 * 512:m4 * 512 + 192],
                                      wb_sb[:, kk * 1024 + m2 * 128:kk * 1024 + (m2 + 1) * 128],
                                      hq_sb[:, kk * 192:(kk + 1) * 192],
                                      start=(kk == 0), stop=(kk == 31))
                li.then_inc(MM, 1)
                mm_n += 1
                v.wait_ge(MM, mm_n)
                for m4 in range(4):
                    m2 = mh * 4 + m4
                    li = v.tensor_scalar(otf_sb[:, m2 * CAP + q0:m2 * CAP + q0 + 192],
                                         ps[:, m4 * 512:m4 * 512 + 192],
                                         bv_sb[:, 32 + m2:33 + m2], None, ALU.add)
                li.then_inc(EV, 1)
                ev_n += 1
            t.wait_ge(EV, ev_n)
        for gr in range(6 if L10_P2CUT >= 5 else 0):
            op = gr % 2
            for m2 in range(KD):
                li = t.transpose(psb[:, m2 * 128:(m2 + 1) * 128],
                                 otf_sb[:, m2 * CAP + gr * 128:m2 * CAP + (gr + 1) * 128], idb_sb[:, :])
            li.then_inc(MM, 1)
            mm_n += 1
            v.wait_ge(MM, mm_n)
            v.wait_ge(SC, 16 * C.val(SC))
            li = v.tensor_scalar(otm_sb[:, 0:1024], psb[:, :],
                                 cidx_sb[:, 2 * gr + 1:2 * gr + 2], None, ALU.mult)
            li.then_inc(EV, 1)
            ev_n += 1
            t.wait_ge(EV, ev_n)
            g.wait_ge(EV, ev_n)
            g.wait_ge(SC, 16 * C.val(SC))
            g.indirect_dma_start(routed.ap()[:, :],
                                 IndirectOffsetOnAxis(idxi_sb[:, gr:gr + 1], 0),
                                 otm_sb[:, 0:1024], None,
                                 compute_op=ALU.add).then_inc(SC, 16)
            C.add(SC, 1)
        MME[e] = mm_n

    if L10_PHASE < 3:
        g.wait_ge(SC, 16 * C.val(SC))
        g.wait_ge(MM, MME[L10_NEXP - 1])
        v.wait_ge(EV, ev_n)
        s.wait_ge(SQ, sq_n)
        for nt in range(8):
            for k in range(KD):
                for h4 in range(2):
                    g.dma_start(out=outF[k * 128:(k + 1) * 128, nt * 512 + h4 * 256:nt * 512 + (h4 + 1) * 256],
                                in_=zz_sb[:, :]).then_inc(ST0, 16)
                    C.add(ST0, 1)
        g.wait_ge(ST0, 16 * C.val(ST0))
        es.close()
        return nc

    g.wait_ge(MM, MME[L10_NEXP - 1])
    for kk in range(KD):
        g.dma_start(out=wa_sb[:, kk * 4096:(kk + 1) * 4096],
                    in_=Ws1[kk * 128:(kk + 1) * 128, :]).then_inc(LW0, 16)
    C.add(LW0, KD)
    sy.wait_ge(SQ, sq_n)
    sy.wait_ge(EV, ev_n)
    for m in range(32):
        sy.dma_start(out=bv_sb[:, m:m + 1], in_=bs1[m * 128:(m + 1) * 128, :]).then_inc(LD0, 16)
    for k in range(KD):
        sy.dma_start(out=bv_sb[:, 32 + k:33 + k], in_=bs2[k * 128:(k + 1) * 128, :]).then_inc(LD0, 16)
    C.add(LD0, 40)
    t.wait_ge(LW0, 16 * C.val(LW0))
    t.wait_ge(LD0, 16 * C.val(LD0))
    t.wait_ge(EV, ev_n)
    s.wait_ge(LD0, 16 * C.val(LD0))
    STS = {0: ST0, 1: ST1}
    for nt in range(8):
        par = nt % 2
        LDp = LD0 if par == 0 else LD1
        tok0 = nt * 512
        xo = par * KD * 512
        if nt >= 1:
            sy.wait_ge(MM, MME[L10_NEXP - 1] + nt * 32)
            sy.wait_ge(LDp, 16 * C.val(LDp))
        for k in range(KD):
            sy.dma_start(out=a_sb[:, k * 512:(k + 1) * 512],
                         in_=xn1T[k * 128:(k + 1) * 128, tok0:tok0 + 512]).then_inc(LDp, 16)
        C.add(LDp, KD)
        t.wait_ge(LDp, 16 * C.val(LDp))
        sq_base = sq_n
        for m in range(32):
            bank = m % 4
            if m >= 4:
                t.wait_ge(SQ, sq_base + m - 3)
            for kk in range(KD):
                li = t.matmul(ps[:, bank * 512:(bank + 1) * 512],
                              wa_sb[:, kk * 4096 + m * 128:kk * 4096 + (m + 1) * 128],
                              a_sb[:, kk * 512:(kk + 1) * 512],
                              start=(kk == 0), stop=(kk == KD - 1))
            li.then_inc(MM, 1)
            mm_n += 1
            hpar = m % 2
            STp = STS[hpar]
            s.wait_ge(MM, mm_n)
            if m >= 2:
                s.wait_ge(STp, 16 * C.val(STp))
            li = s.activation(hgp_sb[:, hpar * 512:(hpar + 1) * 512], ps[:, bank * 512:(bank + 1) * 512],
                              AF.Gelu_apprx_tanh, bias=bv_sb[:, m:m + 1])
            li.then_inc(SQ, 1)
            sq_n += 1
            g.wait_ge(SQ, sq_n)
            g.dma_start(out=hgs2.ap()[m * 128:(m + 1) * 128, tok0:tok0 + 512],
                        in_=hgp_sb[:, hpar * 512:(hpar + 1) * 512]).then_inc(STp, 16)
            C.add(STp, 1)
    g.wait_ge(MM, mm_n)
    g.wait_ge(ST0, 16 * C.val(ST0))
    g.wait_ge(ST1, 16 * C.val(ST1))
    sy.wait_ge(ST0, 16 * C.val(ST0))
    sy.wait_ge(ST1, 16 * C.val(ST1))
    g.wait_ge(SC, 16 * C.val(SC))
    sy.wait_ge(SC, 16 * C.val(SC))
    for r0 in range(0, 4097, 128):
        rn = min(128, 4097 - r0)
        sy.dma_start(out=rdbg[r0:r0 + rn, :], in_=routed.ap()[r0:r0 + rn, :]).then_inc(LD, 16)
        C.add(LD, 1)
    for kk in range(32):
        g.dma_start(out=wb_sb[:, kk * 1024:(kk + 1) * 1024],
                    in_=Ws2[kk * 128:(kk + 1) * 128, :]).then_inc(LW1, 16)
    C.add(LW1, 32)
    t.wait_ge(LW1, 16 * C.val(LW1))
    P3EV = {}
    for nt in range(8):
        par = nt % 2
        LDp = LD0 if par == 0 else LD1
        STp = STS[par]
        tok0 = nt * 512
        xo = par * KD * 512
        if nt >= 1:
            sy.wait_ge(EV, P3EV[nt - 1])
            sy.wait_ge(LDp, 16 * C.val(LDp))
        for k in range(KD):
            sy.dma_start(out=a_sb[:, k * 512:(k + 1) * 512],
                         in_=s1T[k * 128:(k + 1) * 128, tok0:tok0 + 512]).then_inc(LDp, 16)
        C.add(LDp, KD)
        for tg in range(4):
            if tg >= 1:
                g.wait_ge(MM, mm_n)
            g.dma_start(out=otm_sb[:, 0:1024],
                        in_=routed.ap()[tok0 + tg * 128:tok0 + (tg + 1) * 128, :]).then_inc(LDp, 16)
            C.add(LDp, 1)
            t.wait_ge(LDp, 16 * C.val(LDp))
            if tg >= 1:
                t.wait_ge(EV, ev_n)
            for k2 in range(KD):
                bank = 4 + k2 % 2
                li = t.transpose(ps[:, bank * 512:bank * 512 + 128],
                                 otm_sb[:, k2 * 128:(k2 + 1) * 128],
                                 idf_sb[:, :])
                li.then_inc(MM, 1)
                mm_n += 1
                v.wait_ge(MM, mm_n)
                li2 = v.tensor_copy(rf_sb[:, k2 * 512 + tg * 128:k2 * 512 + (tg + 1) * 128],
                                    ps[:, bank * 512:bank * 512 + 128])
                li2.then_inc(EV, 1)
                ev_n += 1
                t.wait_ge(EV, ev_n)
        for mg in range(2):
            sy.wait_ge(EV, ev_n)
            for m4 in range(4):
                m2 = mg * 4 + m4
                sy.dma_start(out=x1_sb[:, m4 * 512:(m4 + 1) * 512],
                             in_=x1T[m2 * 128:(m2 + 1) * 128, tok0:tok0 + 512]).then_inc(LD0, 16)
            C.add(LD0, 4)
            base_mm2 = mm_n
            for kk in range(32):
                hp = kk % 2
                if kk >= 2:
                    g.wait_ge(MM, base_mm2 + kk - 1)
                g.dma_start(out=hgp_sb[:, hp * 512:(hp + 1) * 512],
                            in_=hgs2.ap()[kk * 128:(kk + 1) * 128, tok0:tok0 + 512]).then_inc(LW0, 16)
                C.add(LW0, 1)
                t.wait_ge(LW0, 16 * C.val(LW0))
                for m4 in range(4):
                    m2 = mg * 4 + m4
                    li = t.matmul(ps[:, m4 * 512:(m4 + 1) * 512],
                                  wb_sb[:, kk * 1024 + m2 * 128:kk * 1024 + (m2 + 1) * 128],
                                  hgp_sb[:, hp * 512:(hp + 1) * 512],
                                  start=(kk == 0), stop=(kk == 31))
                li.then_inc(MM, 1)
                mm_n += 1
            v.wait_ge(MM, mm_n)
            v.wait_ge(LDp, 16 * C.val(LDp))
            v.wait_ge(LD0, 16 * C.val(LD0))
            if nt >= 2:
                v.wait_ge(STp, 16 * C.val(STp))
            for m4 in range(4):
                m2 = mg * 4 + m4
                vch(v.tensor_scalar(t1_sb[:], ps[:, m4 * 512:(m4 + 1) * 512],
                                    bv_sb[:, 32 + m2:33 + m2], None, ALU.add))
                vch(v.tensor_tensor(t1_sb[:], t1_sb[:], rf_sb[:, m2 * 512:(m2 + 1) * 512], ALU.add))
                s.wait_ge(EV, ev_n)
                s.activation(t2_sb[:], t1_sb[:], AF.Silu).then_inc(SQ, 1)
                sq_n += 1
                v.wait_ge(SQ, sq_n)
                vch(v.tensor_tensor(t2_sb[:], t2_sb[:], a_sb[:, m2 * 512:(m2 + 1) * 512], ALU.mult))
                li = v.tensor_tensor(x1_sb[:, m4 * 512:(m4 + 1) * 512], t2_sb[:],
                                     x1_sb[:, m4 * 512:(m4 + 1) * 512], ALU.add)
                li.then_inc(EV, 1)
                ev_n += 1
                t.wait_ge(EV, ev_n)
                g.wait_ge(EV, ev_n)
                g.dma_start(out=outF[m2 * 128:(m2 + 1) * 128, tok0:tok0 + 512],
                            in_=x1_sb[:, m4 * 512:(m4 + 1) * 512]).then_inc(STp, 16)
                C.add(STp, 1)
        P3EV[nt] = ev_n
    g.wait_ge(ST0, 16 * C.val(ST0))
    g.wait_ge(ST1, 16 * C.val(ST1))
    es.close()
    return nc



def _rot_tables():
    DH, WINDOW = 64, 256
    half = np.arange(0, DH, 2, dtype=np.float64)
    inv_freq = 1.0 / 10000.0 ** (half / DH)
    t = np.arange(S, dtype=np.float64)
    freqs = t[:, None] * inv_freq[None, :]
    freqs = np.concatenate([freqs, freqs], -1)
    base_scale = (half + 0.4 * DH) / (1.4 * DH)
    power = (t - S // 2) / (WINDOW // 2)
    scale = base_scale[None, :] ** power[:, None]
    scale = np.concatenate([scale, scale], -1)
    cs = (np.cos(freqs) * scale).astype(np.float32)
    ss = (np.sin(freqs) * scale).astype(np.float32)
    ci = (np.cos(freqs) / scale).astype(np.float32)
    si = (np.sin(freqs) / scale).astype(np.float32)
    out = []
    for a in (cs, ss, ci, si):
        fm = np.ascontiguousarray(a.T)
        out.append(np.ascontiguousarray(
            np.concatenate([fm, fm], axis=0)).astype(ml_dtypes.bfloat16))
    return out


LFA_SERIAL = False

def build_lfa():
    nc = mknc()
    xfT = nc.dram_tensor("xfT", [D, S], BF16, kind="ExternalInput").ap()
    Wqkv = nc.dram_tensor("Wqkv", [D, 3072], F32, kind="ExternalInput").ap()
    qs = nc.dram_tensor("qs", [64, 1], F32, kind="ExternalInput").ap()
    ks = nc.dram_tensor("ks", [64, 1], F32, kind="ExternalInput").ap()
    qrT = nc.dram_tensor("qrT", [D, S], BF16, kind="ExternalOutput").ap()
    krT = nc.dram_tensor("krT", [D, S], BF16, kind="ExternalOutput").ap()
    vt = nc.dram_tensor("vt", [S, D], BF16, kind="ExternalOutput").ap()

    cs, ss, ci, si = _rot_tables()
    cs_t = nc.inline_tensor(cs, name="cs_t")
    ss_t = nc.inline_tensor(ss, name="ss_t")
    ci_t = nc.inline_tensor(ci, name="ci_t")
    si_t = nc.inline_tensor(si, name="si_t")
    P = np.zeros((128, 128), np.float32)
    for b in (0, 64):
        for i in range(32):
            P[b + i + 32, b + i] = -1.0
            P[b + i, b + i + 32] = 1.0
    P_t = nc.inline_tensor(P, name="Prot")
    hsel2 = np.zeros((128, 4), np.float32)
    hsel2[0:64, 0] = 1.0
    hsel2[64:128, 1] = 1.0
    hsel2_t = nc.inline_tensor(hsel2, name="hsel2")
    hsel2T = np.zeros((2, 128), np.float32)
    hsel2T[0, 0:64] = 1.0
    hsel2T[1, 64:128] = 1.0
    hsel2T_t = nc.inline_tensor(hsel2T, name="hsel2T")

    g, t, v, s, sy = nc.gpsimd, nc.tensor, nc.vector, nc.scalar, nc.sync
    es = ExitStack()
    w_sb = es.enter_context(nc.sbuf_tensor("w_sb", [128, KD * 3072], BF16))
    tb_sb = es.enter_context(nc.sbuf_tensor("tb_sb", [128, 2 * 4 * 512], BF16))
    xs_sb = es.enter_context(nc.sbuf_tensor("xs_sb", [128, 2 * KD * 512], BF16))
    qk_sb = es.enter_context(nc.sbuf_tensor("qk_sb", [128, 2 * 16 * 512], BF16))
    rot_sb = es.enter_context(nc.sbuf_tensor("rot_sb", [128, 512], BF16))
    vt_sb = es.enter_context(nc.sbuf_tensor("vt_sb", [128, 2 * 4 * 1024], BF16))
    st_sb = es.enter_context(nc.sbuf_tensor("st_sb", [2, 3 * 512], F32))
    sq_sb = es.enter_context(nc.sbuf_tensor("sq_sb", [128, 512], F32))
    t1_sb = es.enter_context(nc.sbuf_tensor("t1_sb", [128, 512], F32))
    t2_sb = es.enter_context(nc.sbuf_tensor("t2_sb", [128, 512], F32))
    sc_sb = es.enter_context(nc.sbuf_tensor("sc_sb", [128, 2], F32))
    hs_sb = es.enter_context(nc.sbuf_tensor("hs_sb", [128, 4], F32))
    hsT_sb = es.enter_context(nc.sbuf_tensor("hsT_sb", [2, 128], F32))
    P_sb = es.enter_context(nc.sbuf_tensor("P_sb", [128, 128], BF16))
    ps = es.enter_context(nc.psum_tensor("ps", [128, 8 * 512], F32))
    LD0 = es.enter_context(nc.semaphore("LD0"))
    LD1 = es.enter_context(nc.semaphore("LD1"))
    LW = es.enter_context(nc.semaphore("LW"))
    MM = es.enter_context(nc.semaphore("MM"))
    EV = es.enter_context(nc.semaphore("EV"))
    SQ = es.enter_context(nc.semaphore("SQ"))
    ST0 = es.enter_context(nc.semaphore("ST0"))
    ST1 = es.enter_context(nc.semaphore("ST1"))
    C = Ctr()
    mm_n = ev_n = sq_n = 0
    for kk in range(KD):
        g.dma_start(out=w_sb[:, kk * 3072:(kk + 1) * 3072],
                    in_=Wqkv[kk * 128:(kk + 1) * 128, :]).then_inc(LW, 16)
    C.add(LW, KD)
    g.dma_start(out=P_sb[:, :], in_=P_t.ap()[:, :]).then_inc(LW, 16)
    C.add(LW, 1)
    sy.dma_start(out=hs_sb[:, 0:2], in_=hsel2_t.ap()[:, 0:2]).then_inc(LD0, 16)
    sy.dma_start(out=hsT_sb[0:2, :], in_=hsel2T_t.ap()[:, :]).then_inc(LD0, 16)
    sy.dma_start(out=sc_sb[0:64, 0:1], in_=qs[:, :]).then_inc(LD0, 16)
    sy.dma_start(out=sc_sb[64:128, 0:1], in_=qs[:, :]).then_inc(LD0, 16)
    sy.dma_start(out=sc_sb[0:64, 1:2], in_=ks[:, :]).then_inc(LD0, 16)
    sy.dma_start(out=sc_sb[64:128, 1:2], in_=ks[:, :]).then_inc(LD0, 16)
    C.add(LD0, 6)
    t.wait_ge(LW, 16 * C.val(LW))
    t.wait_ge(LD0, 16 * C.val(LD0))
    s.wait_ge(LD0, 16 * C.val(LD0))
    v.wait_ge(LD0, 16 * C.val(LD0))

    cons = {}
    for nt in range(8):
        if LFA_SERIAL and nt >= 1:
            t.wait_ge(EV, ev_n)
            t.wait_ge(SQ, sq_n)
            v.wait_ge(MM, mm_n)
            s.wait_ge(MM, mm_n)
            v.wait_ge(ST0, 16 * C.val(ST0))
            v.wait_ge(ST1, 16 * C.val(ST1))
            sy.wait_ge(EV, ev_n)
            sy.wait_ge(MM, mm_n)
        par = nt % 2
        LD = LD0 if par == 0 else LD1
        ST = ST0 if par == 0 else ST1
        xo = par * KD * 512
        qko = par * 16 * 512
        vo = par * 4 * 1024
        tok0 = nt * 512
        if nt >= 2:
            sy.wait_ge(EV, cons[nt - 2])
            sy.wait_ge(LD, 16 * C.val(LD))
        if nt == 0:
            v.memset(xs_sb[:, 4 * 512:4 * 512 + 1], 0.0)
            v.memset(xs_sb[:, 5 * 512:5 * 512 + 1], 0.0)
            v.memset(xs_sb[:, 6 * 512:6 * 512 + 1], 0.0)
            li = v.memset(xs_sb[:, 7 * 512:7 * 512 + 1], 0.0)
            li.then_inc(EV, 1)
            ev_n += 1
            sy.wait_ge(EV, ev_n)
        for k in range(KD):
            if k < 4:
                sy.dma_start(out=xs_sb[:, xo + k * 512:xo + (k + 1) * 512],
                             in_=xfT[k * 128:(k + 1) * 128, tok0:tok0 + 512]).then_inc(LD, 16)
            elif tok0 > 0:
                sy.dma_start(out=xs_sb[:, xo + k * 512:xo + (k + 1) * 512],
                             in_=xfT[k * 128:(k + 1) * 128, tok0 - 1:tok0 + 511]).then_inc(LD, 16)
            else:
                sy.dma_start(out=xs_sb[:, xo + k * 512 + 1:xo + (k + 1) * 512],
                             in_=xfT[k * 128:(k + 1) * 128, 0:511]).then_inc(LD, 16)
        C.add(LD, KD)
        tbo = par * 2048
        sy.dma_start(out=tb_sb[:, tbo + 0:tbo + 512], in_=cs_t.ap()[:, tok0:tok0 + 512]).then_inc(LD, 16)
        sy.dma_start(out=tb_sb[:, tbo + 512:tbo + 1024], in_=ss_t.ap()[:, tok0:tok0 + 512]).then_inc(LD, 16)
        sy.dma_start(out=tb_sb[:, tbo + 1024:tbo + 1536], in_=ci_t.ap()[:, tok0:tok0 + 512]).then_inc(LD, 16)
        sy.dma_start(out=tb_sb[:, tbo + 1536:tbo + 2048], in_=si_t.ap()[:, tok0:tok0 + 512]).then_inc(LD, 16)
        C.add(LD, 4)
        ld_tot = 16 * C.val(LD)
        t.wait_ge(LD, ld_tot)
        if nt == 0:
            t.wait_ge(EV, ev_n)
        if nt >= 2:
            v.wait_ge(ST, 16 * C.val(ST))
        for m in range(16):
            bank = m % 4
            for kk in range(KD):
                li = t.matmul(ps[:, bank * 512:(bank + 1) * 512],
                              w_sb[:, kk * 3072 + m * 128:kk * 3072 + (m + 1) * 128],
                              xs_sb[:, xo + kk * 512:xo + (kk + 1) * 512],
                              start=(kk == 0), stop=(kk == KD - 1))
            li.then_inc(MM, 1)
            mm_n += 1
            v.wait_ge(MM, mm_n)
            li = v.tensor_copy(qk_sb[:, qko + m * 512:qko + (m + 1) * 512],
                               ps[:, bank * 512:(bank + 1) * 512])
            li.then_inc(EV, 1)
            ev_n += 1
            if m >= 3:
                t.wait_ge(EV, ev_n - 3)
        for m in range(16):
            is_q = m < 8
            qo = qko + m * 512
            s.wait_ge(EV, ev_n)
            s.activation(sq_sb[:], qk_sb[:, qo:qo + 512], AF.Square).then_inc(SQ, 1)
            sq_n += 1
            t.wait_ge(SQ, sq_n)
            t.matmul(ps[0:2, 4 * 512:4 * 512 + 512], hs_sb[:, 0:2], sq_sb[:],
                     start=True, stop=True).then_inc(MM, 1)
            mm_n += 1
            s.wait_ge(MM, mm_n)
            s.activation(st_sb[0:2, 0:512], ps[0:2, 4 * 512:4 * 512 + 512], AF.Sqrt).then_inc(SQ, 1)
            sq_n += 1
            v.wait_ge(SQ, sq_n)
            v.tensor_single_scalar(st_sb[0:2, 512:1024], st_sb[0:2, 0:512], 1e-12, ALU.max).then_inc(EV, 1)
            ev_n += 1
            v.wait_ge(EV, ev_n)
            v.reciprocal(st_sb[0:2, 512:1024], st_sb[0:2, 512:1024]).then_inc(EV, 1)
            ev_n += 1
            t.wait_ge(EV, ev_n)
            t.matmul(ps[:, 5 * 512:6 * 512], hsT_sb[0:2, :], st_sb[0:2, 512:1024],
                     start=True, stop=True).then_inc(MM, 1)
            mm_n += 1
            v.wait_ge(MM, mm_n)
            v.tensor_tensor(t1_sb[:], qk_sb[:, qo:qo + 512], ps[:, 5 * 512:6 * 512], ALU.mult)
            li = v.tensor_scalar(qk_sb[:, qo:qo + 512], t1_sb[:],
                                 sc_sb[:, 0:1] if is_q else sc_sb[:, 1:2], None, ALU.mult)
            li.then_inc(EV, 1)
            ev_n += 1
            t.wait_ge(EV, ev_n)
            t.matmul(ps[:, 6 * 512:7 * 512], P_sb[:, :], qk_sb[:, qo:qo + 512],
                     start=True, stop=True).then_inc(MM, 1)
            mm_n += 1
            v.wait_ge(MM, mm_n)
            li = v.tensor_copy(rot_sb[:], ps[:, 6 * 512:7 * 512])
            li.then_inc(EV, 1)
            ev_n += 1
            cso = tbo + (0 if is_q else 1024)
            sso = tbo + (512 if is_q else 1536)
            v.wait_ge(EV, ev_n)
            v.tensor_tensor(t1_sb[:], qk_sb[:, qo:qo + 512], tb_sb[:, cso:cso + 512], ALU.mult)
            v.tensor_tensor(t2_sb[:], rot_sb[:], tb_sb[:, sso:sso + 512], ALU.mult)
            li = v.tensor_tensor(qk_sb[:, qo:qo + 512], t1_sb[:], t2_sb[:], ALU.add)
            li.then_inc(EV, 1)
            ev_n += 1
            t.wait_ge(EV, ev_n)
        for tg in range(4):
            for half in range(2):
                bank = 4 + (tg * 2 + half) % 4
                for kk in range(KD):
                    li = t.matmul(ps[:, bank * 512:(bank + 1) * 512],
                                  xs_sb[:, xo + kk * 512 + tg * 128:xo + kk * 512 + (tg + 1) * 128],
                                  w_sb[:, kk * 3072 + 2048 + half * 512:kk * 3072 + 2048 + (half + 1) * 512],
                                  start=(kk == 0), stop=(kk == KD - 1))
                li.then_inc(MM, 1)
                mm_n += 1
                v.wait_ge(MM, mm_n)
                li = v.tensor_copy(vt_sb[:, vo + tg * 1024 + half * 512:vo + tg * 1024 + (half + 1) * 512],
                                   ps[:, bank * 512:(bank + 1) * 512])
                li.then_inc(EV, 1)
                ev_n += 1
                if tg * 2 + half >= 3:
                    t.wait_ge(EV, ev_n - 3)
        cons[nt] = ev_n
        g.wait_ge(EV, ev_n)
        if nt >= 2:
            g.wait_ge(ST, 16 * C.val(ST))
        for k in range(KD):
            g.dma_start(out=qrT[k * 128:(k + 1) * 128, tok0:tok0 + 512],
                        in_=qk_sb[:, qko + k * 512:qko + (k + 1) * 512]).then_inc(ST, 16)
            g.dma_start(out=krT[k * 128:(k + 1) * 128, tok0:tok0 + 512],
                        in_=qk_sb[:, qko + (8 + k) * 512:qko + (9 + k) * 512]).then_inc(ST, 16)
        for tg in range(4):
            g.dma_start(out=vt[tok0 + tg * 128:tok0 + (tg + 1) * 128, :],
                        in_=vt_sb[:, vo + tg * 1024:vo + (tg + 1) * 1024]).then_inc(ST, 16)
        C.add(ST, 2 * KD + 4)
    g.wait_ge(ST0, 16 * C.val(ST0))
    g.wait_ge(ST1, 16 * C.val(ST1))
    es.close()
    return nc


_cache = {}

def _get(name, fn, *a):
    key = (name,) + a
    if key not in _cache:
        _cache[key] = fn(*a)
    return _cache[key]


def run_ln(xT_list):
    nc = _get("ln", build_ln)
    res = run_bass_kernel_spmd(nc, [{"xT": xT_list[c]} for c in range(NCORE)],
                               core_ids=list(range(NCORE)))
    return [r["xn_out"] for r in res.results]


def run_l0_attn(xn, w0, w1, w2):
    s_chunks = [[] for _ in range(NCORE)]
    sse = np.zeros((NCORE, 0), np.float32)
    for k in range(8):
        nck = _get("ck", build_chunk, k)
        in_maps = []
        for c in range(NCORE):
            m = {"xnT": np.ascontiguousarray(xn[c][:, k * CHUNK:(k + 1) * CHUNK]),
                 "w0": w0, "w2": w2, "w1": w1}
            if k > 0:
                m["ssep"] = np.ascontiguousarray(sse)
            in_maps.append(m)
        resk = run_bass_kernel_spmd(nck, in_maps, core_ids=list(range(NCORE)))
        for c in range(NCORE):
            s_chunks[c].append(resk.results[c]["s_out"])
        sse = np.concatenate(
            [sse, np.stack([r["sse_out"][0, 0] for r in resk.results])[:, None]], axis=1)
    s = [np.concatenate(s_chunks[c], axis=1) for c in range(NCORE)]
    return s, sse


def kernel(**inputs):
    z = inputs
    R8 = list(range(NCORE))
    x = np.asarray(z["L0_attn_x"])
    xT_h = [np.ascontiguousarray(x[c].T) for c in R8]

    nc_ln = _get("ln", build_ln)
    r = run_bass_kernel_spmd(nc_ln, [{"xT": xT_h[c]} for c in R8], core_ids=R8)
    xn_b = [np.asarray(rc["xn_out"]) for rc in r.results]
    xn_f = [np.asarray(rc["xnf_out"]) for rc in r.results]

    sT, _sse = run_l0_attn(xn_b, np.ascontiguousarray(np.asarray(z["L0_attn_w0"])),
                           np.ascontiguousarray(np.asarray(z["L0_attn_w1"])),
                           np.ascontiguousarray(np.asarray(z["L0_attn_w2"])))

    nca = _get("lfa", build_lfa)
    ncb = _get("lfb", build_lfb)
    ncc = _get("lfc", build_lfc)
    Wqkv = np.asarray(z["L0_ffn_Wqkv"])
    qsA = np.asarray(z["L0_ffn_q_scale"])
    ksA = np.asarray(z["L0_ffn_k_scale"])
    WoA = np.asarray(z["L0_ffn_Wo"])
    lngA = np.asarray(z["L0_ffn_ln_g"])
    lnbA = np.asarray(z["L0_ffn_ln_b"])
    W1A = np.asarray(z["L0_ffn_W1"])
    b1A = np.asarray(z["L0_ffn_b1"])
    W2A = np.asarray(z["L0_ffn_W2"])
    b2A = np.asarray(z["L0_ffn_b2"])
    xfb, xff = xn_b, xn_f
    for i in range(6):
        ina = [{"xfT": np.ascontiguousarray(xfb[c]),
                "Wqkv": np.ascontiguousarray(Wqkv[i]),
                "qs": np.ascontiguousarray(qsA[i].reshape(-1, 1)),
                "ks": np.ascontiguousarray(ksA[i].reshape(-1, 1))} for c in R8]
        ra = run_bass_kernel_spmd(nca, ina, core_ids=R8)
        inb = [{"qrT": np.asarray(ra.results[c]["qrT"]),
                "krT": np.asarray(ra.results[c]["krT"]),
                "vt": np.asarray(ra.results[c]["vt"])} for c in R8]
        rb = run_bass_kernel_spmd(ncb, inb, core_ids=R8)
        inc = [{"aoT": np.asarray(rb.results[c]["aoT"]),
                "xT": np.ascontiguousarray(xff[c]),
                "Wo": np.ascontiguousarray(WoA[i]),
                "ln_g": np.ascontiguousarray(lngA[i].reshape(-1, 1)),
                "ln_b": np.ascontiguousarray(lnbA[i].reshape(-1, 1)),
                "W1": np.ascontiguousarray(W1A[i]),
                "b1": np.ascontiguousarray(b1A[i].reshape(-1, 1)),
                "W2": np.ascontiguousarray(W2A[i]),
                "b2": np.ascontiguousarray(b2A[i].reshape(-1, 1))} for c in R8]
        rc_ = run_bass_kernel_spmd(ncc, inc, core_ids=R8)
        xff = [np.asarray(rc_.results[c]["x2T"]) for c in R8]
        xfb = [np.asarray(rc_.results[c]["x2Tb"]) for c in R8]

    nc9a = _get("l9a", build_l9a)
    in9a = [{"xT": xT_h[c],
             "sT": np.ascontiguousarray(sT[c]),
             "fT": np.ascontiguousarray(xfb[c]),
             "ln1_g": np.ascontiguousarray(np.asarray(z["L1_attn_ln1_g"]).reshape(-1, 1)),
             "ln1_b": np.ascontiguousarray(np.asarray(z["L1_attn_ln1_b"]).reshape(-1, 1)),
             "gate": np.ascontiguousarray(np.asarray(z["L1_ffn_gate"]))} for c in R8]
    r9a = run_bass_kernel_spmd(nc9a, in9a, core_ids=R8)
    x1T = [np.asarray(r9a.results[c]["x1T"]) for c in R8]
    xn1T = [np.asarray(r9a.results[c]["xn1T"]) for c in R8]
    yT = [np.asarray(r9a.results[c]["yT"]) for c in R8]
    probs = [np.asarray(r9a.results[c]["probs"]) for c in R8]

    nc9b = _get("l9b", build_l9b)
    in9b = [{"yT": np.ascontiguousarray(yT[c]),
             "Wq": np.ascontiguousarray(np.asarray(z["L1_attn_Wq"])),
             "bq": np.ascontiguousarray(np.asarray(z["L1_attn_bq"]).reshape(-1, 1)),
             "Wk": np.ascontiguousarray(np.asarray(z["L1_attn_Wk"])),
             "bk": np.ascontiguousarray(np.asarray(z["L1_attn_bk"]).reshape(-1, 1)),
             "Wv": np.ascontiguousarray(np.asarray(z["L1_attn_Wv"])),
             "bv": np.ascontiguousarray(np.asarray(z["L1_attn_bv"]).reshape(-1, 1)),
             "Wg": np.ascontiguousarray(np.asarray(z["L1_attn_Wg"])),
             "bg": np.ascontiguousarray(np.asarray(z["L1_attn_bg"]).reshape(-1, 1)),
             "dlog": np.ascontiguousarray(np.asarray(z["L1_attn_decay_logit"]).reshape(-1, 1))}
            for c in R8]
    r9b = run_bass_kernel_spmd(nc9b, in9b, core_ids=R8)
    outT = [np.asarray(r9b.results[c]["outT"]) for c in R8]

    nc9c = _get("l9c", build_l9c)
    in9c = [{"outT": np.ascontiguousarray(outT[c]),
             "xn1T": np.ascontiguousarray(xn1T[c]),
             "Wo": np.ascontiguousarray(np.asarray(z["L1_attn_Wo"])),
             "bo": np.ascontiguousarray(np.asarray(z["L1_attn_bo"]).reshape(-1, 1)),
             "ln2_g": np.ascontiguousarray(np.asarray(z["L1_attn_ln2_g"]).reshape(-1, 1)),
             "ln2_b": np.ascontiguousarray(np.asarray(z["L1_attn_ln2_b"]).reshape(-1, 1)),
             "W1": np.ascontiguousarray(np.asarray(z["L1_attn_W1"])),
             "b1": np.ascontiguousarray(np.asarray(z["L1_attn_b1"]).reshape(-1, 1)),
             "W2": np.ascontiguousarray(np.asarray(z["L1_attn_W2"])),
             "b2": np.ascontiguousarray(np.asarray(z["L1_attn_b2"]).reshape(-1, 1))}
            for c in R8]
    r9c = run_bass_kernel_spmd(nc9c, in9c, core_ids=R8)
    s1T = [np.asarray(r9c.results[c]["s1T"]) for c in R8]

    nc10 = _get("l10", build_l10)
    probsA = np.ascontiguousarray(np.stack(probs, 0).transpose(2, 0, 1).reshape(8, NCORE * S))
    b1et = np.ascontiguousarray(np.asarray(z["L1_ffn_b1"]).reshape(8, 32, 128).transpose(0, 2, 1))
    b2et = np.ascontiguousarray(np.asarray(z["L1_ffn_b2"]).reshape(8, 8, 128).transpose(0, 2, 1))
    in10 = [{"probsA": probsA,
             "probsO": np.ascontiguousarray(probs[c].T),
             "xn1tm": np.ascontiguousarray(xn1T[c].T),
             "xn1T": np.ascontiguousarray(xn1T[c]),
             "x1T": np.ascontiguousarray(x1T[c]),
             "s1T": np.ascontiguousarray(s1T[c]),
             "W1e": np.ascontiguousarray(np.asarray(z["L1_ffn_W1"])),
             "W2e": np.ascontiguousarray(np.asarray(z["L1_ffn_W2"])),
             "b1et": b1et, "b2et": b2et,
             "Ws1": np.ascontiguousarray(np.asarray(z["L1_ffn_Ws1"])),
             "bs1": np.ascontiguousarray(np.asarray(z["L1_ffn_bs1"]).reshape(-1, 1)),
             "Ws2": np.ascontiguousarray(np.asarray(z["L1_ffn_Ws2"])),
             "bs2": np.ascontiguousarray(np.asarray(z["L1_ffn_bs2"]).reshape(-1, 1))}
            for c in R8]
    r10 = run_bass_kernel_spmd(nc10, in10, core_ids=R8)
    out = np.stack([np.ascontiguousarray(np.asarray(r10.results[c]["outF"]).T) for c in R8])
    return out


def _lfb_masks():
    M = np.zeros((128, 4 * 512), np.float32)
    NEG = -1.0e30
    j = np.arange(512)
    for wt in range(2):
        for hf in range(2):
            blk = np.full((128, 512), NEG, np.float32)
            for p in range(128):
                i = hf * 128 + p
                if wt == 1:
                    ok = (j >= i) & (j <= i + 256)
                else:
                    ok = (j >= 256) & (j <= i + 256)
                blk[p, ok] = 0.0
            M[:, (wt * 2 + hf) * 512:(wt * 2 + hf + 1) * 512] = blk
    return M


def build_lfb():
    nc = mknc()
    qrT = nc.dram_tensor("qrT", [D, S], BF16, kind="ExternalInput").ap()
    krT = nc.dram_tensor("krT", [D, S], BF16, kind="ExternalInput").ap()
    vt = nc.dram_tensor("vt", [S, D], BF16, kind="ExternalInput").ap()
    aoT = nc.dram_tensor("aoT", [D, S], BF16, kind="ExternalOutput").ap()

    msk_t = nc.inline_tensor(_lfb_masks(), name="lfb_masks")
    idb_t = nc.inline_tensor(np.eye(128, dtype=np.float32), name="lfb_idf")

    g, t, v, s, sy = nc.gpsimd, nc.tensor, nc.vector, nc.scalar, nc.sync
    es = ExitStack()
    q_sb = es.enter_context(nc.sbuf_tensor("q_sb", [64, 2 * 16 * 256], BF16))
    k_sb = es.enter_context(nc.sbuf_tensor("k_sb", [64, 2 * 16 * 512], BF16))
    v_sb = es.enter_context(nc.sbuf_tensor("v_sb", [128, 2 * 4 * 1024], BF16))
    msk_sb = es.enter_context(nc.sbuf_tensor("msk_sb", [128, 4 * 512], F32))
    idf_sb = es.enter_context(nc.sbuf_tensor("idf_sb", [128, 128], F32))
    sm_sb = es.enter_context(nc.sbuf_tensor("sm_sb", [128, 2 * 512], F32))
    e_sb = es.enter_context(nc.sbuf_tensor("e_sb", [128, 2 * 512], F32))
    en_sb = es.enter_context(nc.sbuf_tensor("en_sb", [128, 2 * 512], F32))
    ptb_sb = es.enter_context(nc.sbuf_tensor("ptb_sb", [128, 2 * 512], BF16))
    ao_sb = es.enter_context(nc.sbuf_tensor("ao_sb", [128, 2 * 2048], BF16))
    red_sb = es.enter_context(nc.sbuf_tensor("red_sb", [128, 8], F32))
    psS = es.enter_context(nc.psum_tensor("psS", [128, 2 * 512], F32))
    psT = es.enter_context(nc.psum_tensor("psT", [128, 2 * 512], F32))
    psO = es.enter_context(nc.psum_tensor("psO", [128, 2 * 512], F32))
    LD0 = es.enter_context(nc.semaphore("LD0"))
    LD1 = es.enter_context(nc.semaphore("LD1"))
    ST0 = es.enter_context(nc.semaphore("ST0"))
    ST1 = es.enter_context(nc.semaphore("ST1"))
    MM = es.enter_context(nc.semaphore("MM"))
    EV = es.enter_context(nc.semaphore("EV"))
    SQ = es.enter_context(nc.semaphore("SQ"))
    C = Ctr()
    mm_n = ev_n = sq_n = 0

    sy.dma_start(out=msk_sb[:, :], in_=msk_t.ap()[:, :]).then_inc(LD0, 16)
    sy.dma_start(out=idf_sb[:, :], in_=idb_t.ap()[:, :]).then_inc(LD0, 16)
    C.add(LD0, 2)

    NW, NH = 16, 16
    maskadd_ev = {}
    pnorm_ev = {}
    ptevac_ev = {}
    aoevac_ev = {}
    s_mm = {}
    trans_mm = {}
    out_mm = {}
    exp_sq = {}
    ld_target = {}
    out_mm_wend = {}
    st_count = {0: 0, 1: 0}
    st_total = {}

    for wdw in range(NW):
        par = wdw % 2
        LD = LD0 if par == 0 else LD1
        ST = ST0 if par == 0 else ST1
        qo = par * 4096
        ko = par * 8192
        vo = par * 4096
        aopar = par
        if wdw >= 2:
            sy.wait_ge(MM, out_mm_wend[wdw - 2])
            sy.wait_ge(LD, 16 * C.val(LD))
        for h in range(NH):
            m = h // 2
            r0 = m * 128 + (h % 2) * 64
            sy.dma_start(out=q_sb[0:64, qo + h * 256:qo + (h + 1) * 256],
                         in_=qrT[r0:r0 + 64, wdw * 256:(wdw + 1) * 256]).then_inc(LD, 16)
            if wdw == 0:
                sy.dma_start(out=k_sb[0:64, ko + h * 512 + 256:ko + h * 512 + 512],
                             in_=krT[r0:r0 + 64, 0:256]).then_inc(LD, 16)
            else:
                sy.dma_start(out=k_sb[0:64, ko + h * 512:ko + (h + 1) * 512],
                             in_=krT[r0:r0 + 64, (wdw - 1) * 256:(wdw + 1) * 256]).then_inc(LD, 16)
        C.add(LD, 2 * NH)
        for jc in range(4):
            if wdw == 0:
                if jc < 2:
                    continue
                src0 = (jc - 2) * 128
            else:
                src0 = (wdw - 1) * 256 + jc * 128
            sy.dma_start(out=v_sb[:, vo + jc * 1024:vo + (jc + 1) * 1024],
                         in_=vt[src0:src0 + 128, :]).then_inc(LD, 16)
        C.add(LD, 2 if wdw == 0 else 4)
        ld_target[wdw] = 16 * C.val(LD)
        if wdw == 0:
            for h in range(NH):
                v.memset(k_sb[0:64, h * 512:h * 512 + 256], 0.0)
            li = v.memset(v_sb[:, 0:2048], 0.0)
            li.then_inc(EV, 1)
            ev_n += 1
            t.wait_ge(EV, ev_n)

        wt = 0 if wdw == 0 else 1
        for h in range(NH):
            for hf in range(2):
                gi = wdw * 32 + h * 2 + hf
                gp = gi % 2
                if h == 0 and hf == 0:
                    t.wait_ge(LD, ld_target[wdw])
                if gi >= 2:
                    t.wait_ge(EV, maskadd_ev[gi - 2])
                t.matmul(psS[:, gp * 512:(gp + 1) * 512],
                         q_sb[0:64, qo + h * 256 + hf * 128:qo + h * 256 + (hf + 1) * 128],
                         k_sb[0:64, ko + h * 512:ko + (h + 1) * 512],
                         start=True, stop=True).then_inc(MM, 1)
                mm_n += 1
                s_mm[gi] = mm_n
                v.wait_ge(MM, s_mm[gi])
                if gi >= 2:
                    v.wait_ge(SQ, exp_sq[gi - 2])
                v.tensor_scalar(sm_sb[:, gp * 512:(gp + 1) * 512],
                                psS[:, gp * 512:(gp + 1) * 512], 8.0, None, ALU.mult)
                mo = (wt * 2 + hf) * 512
                li = v.tensor_tensor(sm_sb[:, gp * 512:(gp + 1) * 512],
                                     sm_sb[:, gp * 512:(gp + 1) * 512],
                                     msk_sb[:, mo:mo + 512], ALU.add)
                li.then_inc(EV, 1)
                ev_n += 1
                maskadd_ev[gi] = ev_n
                v.reduce_max(red_sb[:, gp * 4:gp * 4 + 1],
                             sm_sb[:, gp * 512:(gp + 1) * 512], axis=AXX).then_inc(EV, 1)
                ev_n += 1
                v.wait_ge(EV, ev_n)
                v.tensor_scalar(red_sb[:, gp * 4 + 1:gp * 4 + 2],
                                red_sb[:, gp * 4:gp * 4 + 1], -1.0, None,
                                ALU.mult).then_inc(EV, 1)
                ev_n += 1
                s.wait_ge(EV, ev_n)
                if gi >= 2:
                    pass
                s.activation(e_sb[:, gp * 512:(gp + 1) * 512],
                             sm_sb[:, gp * 512:(gp + 1) * 512], AF.Exp,
                             bias=red_sb[:, gp * 4 + 1:gp * 4 + 2],
                             accum_out=red_sb[:, gp * 4 + 2:gp * 4 + 3]).then_inc(SQ, 1)
                sq_n += 1
                exp_sq[gi] = sq_n
                v.wait_ge(SQ, exp_sq[gi])
                v.reciprocal(red_sb[:, gp * 4 + 3:gp * 4 + 4],
                             red_sb[:, gp * 4 + 2:gp * 4 + 3]).then_inc(EV, 1)
                ev_n += 1
                v.wait_ge(EV, ev_n)
                li = v.tensor_scalar(en_sb[:, gp * 512:(gp + 1) * 512],
                                     e_sb[:, gp * 512:(gp + 1) * 512],
                                     red_sb[:, gp * 4 + 3:gp * 4 + 4], None, ALU.mult)
                li.then_inc(EV, 1)
                ev_n += 1
                pnorm_ev[gi] = ev_n
                t.wait_ge(EV, max(pnorm_ev[gi],
                                  ptevac_ev.get(gi - 2, 0)))
                for jc in range(4):
                    li = t.transpose(psT[:, gp * 512 + jc * 128:gp * 512 + (jc + 1) * 128],
                                     en_sb[:, gp * 512 + jc * 128:gp * 512 + (jc + 1) * 128],
                                     idf_sb[:, :])
                li.then_inc(MM, 1)
                mm_n += 1
                trans_mm[gi] = mm_n
                v.wait_ge(MM, trans_mm[gi])
                v.tensor_copy(ptb_sb[:, gp * 512:(gp + 1) * 512],
                              psT[:, gp * 512:(gp + 1) * 512]).then_inc(EV, 1)
                ev_n += 1
                ptevac_ev[gi] = ev_n
                t.wait_ge(EV, max(ptevac_ev[gi], aoevac_ev.get(gi - 2, 0)))
                for jc in range(4):
                    li = t.matmul(psO[0:64, gp * 512:gp * 512 + 128],
                                  v_sb[:, vo + jc * 1024 + h * 64:vo + jc * 1024 + (h + 1) * 64],
                                  ptb_sb[:, gp * 512 + jc * 128:gp * 512 + (jc + 1) * 128],
                                  start=(jc == 0), stop=(jc == 3))
                li.then_inc(MM, 1)
                mm_n += 1
                out_mm[gi] = mm_n
                v.wait_ge(MM, out_mm[gi])
                if h == 0 and hf == 0 and wdw >= 2:
                    v.wait_ge(ST, 16 * st_count[aopar])
                po = (h % 2) * 64
                v.tensor_copy(ao_sb[po:po + 64,
                                    aopar * 2048 + (h // 2) * 256 + hf * 128:
                                    aopar * 2048 + (h // 2) * 256 + (hf + 1) * 128],
                              psO[0:64, gp * 512:gp * 512 + 128]).then_inc(EV, 1)
                ev_n += 1
                aoevac_ev[gi] = ev_n
        out_mm_wend[wdw] = mm_n
        sy.wait_ge(EV, ev_n)
        sy.wait_ge(ST, 16 * st_count[aopar])
        for m in range(KD):
            sy.dma_start(out=aoT[m * 128:(m + 1) * 128, wdw * 256:(wdw + 1) * 256],
                         in_=ao_sb[:, aopar * 2048 + m * 256:aopar * 2048 + (m + 1) * 256]
                         ).then_inc(ST, 16)
        st_count[aopar] += KD
    sy.wait_ge(ST0, 16 * st_count[0])
    sy.wait_ge(ST1, 16 * st_count[1])
    es.close()
    return nc



LFC_NP = 22
LFC_DBG = False

def _lfc_w(i):
    return min(128, 2730 - i * 128)


def build_lfc():
    nc = mknc()
    aoT = nc.dram_tensor("aoT", [D, S], BF16, kind="ExternalInput").ap()
    xT = nc.dram_tensor("xT", [D, S], F32, kind="ExternalInput").ap()
    Wo = nc.dram_tensor("Wo", [D, D], F32, kind="ExternalInput").ap()
    ln_g = nc.dram_tensor("ln_g", [D, 1], F32, kind="ExternalInput").ap()
    ln_b = nc.dram_tensor("ln_b", [D, 1], F32, kind="ExternalInput").ap()
    W1 = nc.dram_tensor("W1", [D, 5460], F32, kind="ExternalInput").ap()
    b1 = nc.dram_tensor("b1", [5460, 1], F32, kind="ExternalInput").ap()
    W2 = nc.dram_tensor("W2", [2730, D], F32, kind="ExternalInput").ap()
    b2 = nc.dram_tensor("b2", [D, 1], F32, kind="ExternalInput").ap()
    x2T = nc.dram_tensor("x2T", [D, S], F32, kind="ExternalOutput").ap()
    x2Tb = nc.dram_tensor("x2Tb", [D, S], BF16, kind="ExternalOutput").ap()
    _ik = "ExternalOutput" if LFC_DBG else "Internal"
    x1s = nc.dram_tensor("x1s", [D, S], F32, kind=_ik)
    hsrc = nc.dram_tensor("hsrc", [D, S], BF16, kind=_ik)
    hgs = nc.dram_tensor("hgs", [LFC_NP * 128, S], BF16, kind=_ik)
    ones128 = nc.inline_tensor(np.ones((128, 1), np.float32), name="ones128")
    ones1x128 = nc.inline_tensor(np.ones((1, 128), np.float32), name="ones1x128")
    g, t, v, s, sy = nc.gpsimd, nc.tensor, nc.vector, nc.scalar, nc.sync
    es = ExitStack()
    pool = ConstPool(nc, es, [1e-5])
    w_sb = es.enter_context(nc.sbuf_tensor("w_sb", [128, 8 * 2816], BF16))
    a_sb = es.enter_context(nc.sbuf_tensor("a_sb", [128, 2 * KD * 512], BF16))
    x_sb = es.enter_context(nc.sbuf_tensor("x_sb", [128, KD * 512], F32))
    x1_sb = es.enter_context(nc.sbuf_tensor("x1_sb", [128, 2 * KD * 512], F32))
    xsh_sb = es.enter_context(nc.sbuf_tensor("xsh_sb", [128, KD * 512], F32))
    h_sb = es.enter_context(nc.sbuf_tensor("h_sb", [128, 2 * KD * 512], BF16))
    out_sb = es.enter_context(nc.sbuf_tensor("out_sb", [128, KD * 512], F32))
    gg_sb = es.enter_context(nc.sbuf_tensor("gg_sb", [128, 2 * 512], F32))
    hh_sb = es.enter_context(nc.sbuf_tensor("hh_sb", [128, 2 * 512], BF16))
    sq_sb = es.enter_context(nc.sbuf_tensor("sq_sb", [128, 512], F32))
    t1_sb = es.enter_context(nc.sbuf_tensor("t1_sb", [128, 512], F32))
    st_sb = es.enter_context(nc.sbuf_tensor("st_sb", [1, 4 * 512], F32))
    bvec_sb = es.enter_context(nc.sbuf_tensor("bvec_sb", [128, 96], F32))
    pcol_sb = es.enter_context(nc.sbuf_tensor("pcol_sb", [128, 4], F32))
    ones_sb = es.enter_context(nc.sbuf_tensor("ones_sb", [128, 129], F32))
    ps = es.enter_context(nc.psum_tensor("ps", [128, 8 * 512], F32))
    LD0 = es.enter_context(nc.semaphore("LD0"))
    LD1 = es.enter_context(nc.semaphore("LD1"))
    LW = es.enter_context(nc.semaphore("LW"))
    MM = es.enter_context(nc.semaphore("MM"))
    EV = es.enter_context(nc.semaphore("EV"))
    SQ = es.enter_context(nc.semaphore("SQ"))
    ST0 = es.enter_context(nc.semaphore("ST0"))
    ST1 = es.enter_context(nc.semaphore("ST1"))
    C = Ctr()
    mm_n = ev_n = sq_n = 0

    sy.dma_start(out=ones_sb[:, 0:1], in_=ones128.ap()[:, :]).then_inc(LD0, 16)
    sy.dma_start(out=ones_sb[0:1, 1:129], in_=ones1x128.ap()[:, :]).then_inc(LD0, 16)
    C.add(LD0, 2)
    for k in range(KD):
        sy.dma_start(out=bvec_sb[:, k:k + 1], in_=ln_g[k * 128:(k + 1) * 128, :]).then_inc(LD0, 16)
        sy.dma_start(out=bvec_sb[:, 8 + k:9 + k], in_=ln_b[k * 128:(k + 1) * 128, :]).then_inc(LD0, 16)
        sy.dma_start(out=bvec_sb[:, 16 + k:17 + k], in_=b2[k * 128:(k + 1) * 128, :]).then_inc(LD0, 16)
    for i in range(LFC_NP):
        w = _lfc_w(i)
        sy.dma_start(out=bvec_sb[0:w, 24 + i:25 + i], in_=b1[i * 128:i * 128 + w, :]).then_inc(LD0, 16)
        sy.dma_start(out=bvec_sb[0:w, 46 + i:47 + i], in_=b1[2730 + i * 128:2730 + i * 128 + w, :]).then_inc(LD0, 16)
    C.add(LD0, 3 * KD + 2 * LFC_NP)
    for kk in range(KD):
        g.dma_start(out=w_sb[:, kk * 1024:(kk + 1) * 1024],
                    in_=Wo[kk * 128:(kk + 1) * 128, :]).then_inc(LW, 16)
    C.add(LW, KD)

    def ln_stats(src_sb):
        nonlocal mm_n, ev_n, sq_n
        for k in range(KD):
            li = t.matmul(ps[0:1, 0:512], ones_sb[:, 0:1],
                          src_sb[:, k * 512:(k + 1) * 512],
                          start=(k == 0), stop=(k == KD - 1))
        li.then_inc(MM, 1)
        mm_n += 1
        for k in range(KD):
            if k > 0:
                s.wait_ge(MM, mm_n)
            s.activation(sq_sb[:, :], src_sb[:, k * 512:(k + 1) * 512],
                         AF.Square).then_inc(SQ, 1)
            sq_n += 1
            t.wait_ge(SQ, sq_n)
            t.matmul(ps[0:1, 512:1024], ones_sb[:, 0:1], sq_sb[:, :],
                     start=(k == 0), stop=(k == KD - 1)).then_inc(MM, 1)
            mm_n += 1
        v.wait_ge(MM, mm_n)
        v.tensor_scalar(st_sb[0:1, 0:512], ps[0:1, 0:512], 1.0 / D, None, ALU.mult)
        v.tensor_scalar(st_sb[0:1, 512:1024], ps[0:1, 512:1024], 1.0 / D, None, ALU.mult)
        v.tensor_tensor(st_sb[0:1, 1024:1536], st_sb[0:1, 0:512], st_sb[0:1, 0:512], ALU.mult)
        v.tensor_tensor(st_sb[0:1, 1024:1536], st_sb[0:1, 512:1024], st_sb[0:1, 1024:1536], ALU.subtract).then_inc(EV, 1)
        ev_n += 1
        s.wait_ge(EV, ev_n)
        s.activation(st_sb[0:1, 1024:1536], st_sb[0:1, 1024:1536], AF.Sqrt,
                     bias=pool.ap(1e-5, rows=1)).then_inc(SQ, 1)
        sq_n += 1
        v.wait_ge(SQ, sq_n)
        v.reciprocal(st_sb[0:1, 1024:1536], st_sb[0:1, 1024:1536])
        v.tensor_tensor(st_sb[0:1, 1536:2048], st_sb[0:1, 0:512], st_sb[0:1, 1024:1536], ALU.mult).then_inc(EV, 1)
        ev_n += 1
        t.wait_ge(EV, ev_n)
        t.matmul(ps[:, 2 * 512:3 * 512], ones_sb[0:1, 1:129], st_sb[0:1, 1024:1536], start=True, stop=True)
        li = t.matmul(ps[:, 3 * 512:4 * 512], ones_sb[0:1, 1:129], st_sb[0:1, 1536:2048], start=True, stop=True)
        li.then_inc(MM, 1)
        mm_n += 1
        v.wait_ge(MM, mm_n)

    s.wait_ge(pool.CN, 1)
    t.wait_ge(LW, 16 * C.val(LW))
    t.wait_ge(LD0, 16 * C.val(LD0))
    v.wait_ge(LD0, 16 * C.val(LD0))
    s.wait_ge(LD0, 16 * C.val(LD0))

    cons = {}
    x1ev = {}
    for nt in range(8):
        par = nt % 2
        LD = LD0 if par == 0 else LD1
        ST = ST0 if par == 0 else ST1
        xo = par * KD * 512
        tok0 = nt * 512
        if nt >= 2:
            sy.wait_ge(EV, cons[nt - 2])
            sy.wait_ge(LD, 16 * C.val(LD))
        for k in range(KD):
            sy.dma_start(out=a_sb[:, xo + k * 512:xo + (k + 1) * 512],
                         in_=aoT[k * 128:(k + 1) * 128, tok0:tok0 + 512]).then_inc(LD, 16)
        C.add(LD, KD)
        if nt >= 1:
            sy.wait_ge(EV, x1ev[nt - 1])
        for k in range(KD):
            sy.dma_start(out=x_sb[:, k * 512:(k + 1) * 512],
                         in_=xT[k * 128:(k + 1) * 128, tok0:tok0 + 512]).then_inc(LD, 16)
        C.add(LD, KD)
        t.wait_ge(LD, 16 * C.val(LD))
        v.wait_ge(LD, 16 * C.val(LD))
        for m in range(KD):
            bank = 4 + m % 4
            for kk in range(KD):
                li = t.matmul(ps[:, bank * 512:(bank + 1) * 512],
                              w_sb[:, kk * 1024 + m * 128:kk * 1024 + (m + 1) * 128],
                              a_sb[:, xo + kk * 512:xo + (kk + 1) * 512],
                              start=(kk == 0), stop=(kk == KD - 1))
            li.then_inc(MM, 1)
            mm_n += 1
            v.wait_ge(MM, mm_n)
            li = v.tensor_tensor(x1_sb[:, xo + m * 512:xo + (m + 1) * 512],
                                 ps[:, bank * 512:(bank + 1) * 512],
                                 x_sb[:, m * 512:(m + 1) * 512], ALU.add)
            li.then_inc(EV, 1)
            ev_n += 1
            if m >= 3:
                t.wait_ge(EV, ev_n - 3)
        x1ev[nt] = ev_n
        g.wait_ge(EV, ev_n)
        for k in range(KD):
            g.dma_start(out=x1s.ap()[k * 128:(k + 1) * 128, tok0:tok0 + 512],
                        in_=x1_sb[:, xo + k * 512:xo + (k + 1) * 512]).then_inc(ST, 16)
        C.add(ST, KD)
        for k in range(4):
            v.tensor_copy(xsh_sb[:, k * 512:(k + 1) * 512],
                          x1_sb[:, xo + k * 512:xo + (k + 1) * 512])
        for k in range(4, KD):
            v.tensor_copy(xsh_sb[:, k * 512 + 1:(k + 1) * 512],
                          x1_sb[:, xo + k * 512:xo + (k + 1) * 512 - 1])
            if nt == 0:
                v.memset(xsh_sb[:, k * 512:k * 512 + 1], 0.0)
            else:
                v.tensor_copy(xsh_sb[:, k * 512:k * 512 + 1], pcol_sb[:, k - 4:k - 3])
        li = v.tensor_copy(xsh_sb[:, 0:1], xsh_sb[:, 0:1])
        li.then_inc(EV, 1)
        ev_n += 1
        v.wait_ge(EV, ev_n)
        for k in range(4, KD):
            li = v.tensor_copy(pcol_sb[:, k - 4:k - 3],
                               x1_sb[:, xo + (k + 1) * 512 - 1:xo + (k + 1) * 512])
        li.then_inc(EV, 1)
        ev_n += 1
        s.wait_ge(EV, ev_n)
        t.wait_ge(EV, ev_n)
        ln_stats(xsh_sb)
        if nt >= 2:
            v.wait_ge(ST, 16 * C.val(ST))
        for k in range(KD):
            v.tensor_tensor(t1_sb[:], xsh_sb[:, k * 512:(k + 1) * 512], ps[:, 2 * 512:3 * 512], ALU.mult)
            v.tensor_tensor(t1_sb[:], t1_sb[:], ps[:, 3 * 512:4 * 512], ALU.subtract)
            li = v.tensor_scalar(h_sb[:, xo + k * 512:xo + (k + 1) * 512], t1_sb[:],
                                 bvec_sb[:, k:k + 1], bvec_sb[:, 8 + k:9 + k], ALU.mult, ALU.add)
        li.then_inc(EV, 1)
        ev_n += 1
        cons[nt] = ev_n
        g.wait_ge(EV, ev_n)
        for k in range(KD):
            g.dma_start(out=hsrc.ap()[k * 128:(k + 1) * 128, tok0:tok0 + 512],
                        in_=h_sb[:, xo + k * 512:xo + (k + 1) * 512]).then_inc(ST, 16)
        C.add(ST, KD)
    c1_mm = mm_n

    pair_ev = []
    pair_mm = []
    for hp in range(2):
        pairs = list(range(11 * hp, min(11 * (hp + 1), LFC_NP)))
        g.wait_ge(MM, mm_n)
        g.wait_ge(LW, 16 * C.val(LW))
        for kk in range(KD):
            for ii, i in enumerate(pairs):
                w = _lfc_w(i)
                g.dma_start(out=w_sb[:, kk * 2816 + ii * 256:kk * 2816 + ii * 256 + w],
                            in_=W1[kk * 128:(kk + 1) * 128, i * 128:i * 128 + w]).then_inc(LW, 16)
                g.dma_start(out=w_sb[:, kk * 2816 + ii * 256 + 128:kk * 2816 + ii * 256 + 128 + w],
                            in_=W1[kk * 128:(kk + 1) * 128, 2730 + i * 128:2730 + i * 128 + w]).then_inc(LW, 16)
        C.add(LW, 2 * KD * len(pairs))
        t.wait_ge(LW, 16 * C.val(LW))
        for nt in range(8):
            par = nt % 2
            LD = LD0 if par == 0 else LD1
            xo = par * KD * 512
            tok0 = nt * 512
            sy.wait_ge(ST0, 16 * C.val(ST0))
            sy.wait_ge(ST1, 16 * C.val(ST1))
            if hp == 0 and nt < 2:
                sy.wait_ge(MM, c1_mm)
            elif len(pair_mm) > len(pairs):
                sy.wait_ge(MM, pair_mm[-(len(pairs) + 1)])
            sy.wait_ge(LD, 16 * C.val(LD))
            for k in range(KD):
                sy.dma_start(out=a_sb[:, xo + k * 512:xo + (k + 1) * 512],
                             in_=hsrc.ap()[k * 128:(k + 1) * 128, tok0:tok0 + 512]).then_inc(LD, 16)
            C.add(LD, KD)
            t.wait_ge(LD, 16 * C.val(LD))
            for ii, i in enumerate(pairs):
                w = _lfc_w(i)
                gidx = len(pair_ev)
                if gidx == 0:
                    t.wait_ge(EV, cons[7])
                bA = gidx % 2
                bG = 2 + gidx % 2
                hp2 = gidx % 2
                STh = ST0 if hp2 == 0 else ST1
                if gidx >= 2:
                    t.wait_ge(EV, pair_ev[gidx - 2][0])
                    t.wait_ge(SQ, pair_ev[gidx - 2][1])
                for kk in range(KD):
                    t.matmul(ps[0:w, bA * 512:bA * 512 + 512],
                             w_sb[:, kk * 2816 + ii * 256:kk * 2816 + ii * 256 + w],
                             a_sb[:, xo + kk * 512:xo + (kk + 1) * 512],
                             start=(kk == 0), stop=(kk == KD - 1))
                for kk in range(KD):
                    li = t.matmul(ps[0:w, bG * 512:bG * 512 + 512],
                                  w_sb[:, kk * 2816 + ii * 256 + 128:kk * 2816 + ii * 256 + 128 + w],
                                  a_sb[:, xo + kk * 512:xo + (kk + 1) * 512],
                                  start=(kk == 0), stop=(kk == KD - 1))
                li.then_inc(MM, 1)
                mm_n += 1
                pair_mm.append(mm_n)
                s.wait_ge(MM, mm_n)
                if gidx >= 2:
                    s.wait_ge(EV, pair_ev[gidx - 2][0])
                s.activation(gg_sb[0:w, hp2 * 512:hp2 * 512 + 512],
                             ps[0:w, bG * 512:bG * 512 + 512], AF.Gelu,
                             bias=bvec_sb[0:w, 46 + i:47 + i]).then_inc(SQ, 1)
                sq_n += 1
                v.wait_ge(SQ, sq_n)
                v.wait_ge(STh, 16 * C.val(STh))
                v.tensor_scalar(t1_sb[0:w, :], ps[0:w, bA * 512:bA * 512 + 512],
                                bvec_sb[0:w, 24 + i:25 + i], None, ALU.add)
                li = v.tensor_tensor(hh_sb[0:w, hp2 * 512:hp2 * 512 + 512],
                                     t1_sb[0:w, :], gg_sb[0:w, hp2 * 512:hp2 * 512 + 512],
                                     ALU.mult)
                li.then_inc(EV, 1)
                ev_n += 1
                pair_ev.append((ev_n, sq_n))
                g.wait_ge(EV, ev_n)
                g.dma_start(out=hgs.ap()[i * 128:i * 128 + w, tok0:tok0 + 512],
                            in_=hh_sb[0:w, hp2 * 512:hp2 * 512 + 512]).then_inc(STh, 16)
                C.add(STh, 1)
    c2a_mm = mm_n

    g.wait_ge(MM, mm_n)
    g.wait_ge(ST0, 16 * C.val(ST0))
    g.wait_ge(ST1, 16 * C.val(ST1))
    sy.wait_ge(ST0, 16 * C.val(ST0))
    sy.wait_ge(ST1, 16 * C.val(ST1))
    g.wait_ge(LW, 16 * C.val(LW))
    for kk in range(LFC_NP):
        w = _lfc_w(kk)
        g.dma_start(out=w_sb[0:w, kk * 1024:(kk + 1) * 1024],
                    in_=W2[kk * 128:kk * 128 + w, :]).then_inc(LW, 16)
    C.add(LW, LFC_NP)
    t.wait_ge(LW, 16 * C.val(LW))
    t.wait_ge(EV, pair_ev[-1][0])
    t.wait_ge(SQ, pair_ev[-1][1])
    evprev = {}
    for nt in range(8):
        par = nt % 2
        LD = LD0 if par == 0 else LD1
        ST = ST0 if par == 0 else ST1
        STo = ST1 if par == 0 else ST0
        xo = par * KD * 512
        tok0 = nt * 512
        if nt >= 2:
            sy.wait_ge(EV, evprev[nt - 2])
            sy.wait_ge(LD, 16 * C.val(LD))
        for k in range(KD):
            sy.dma_start(out=x1_sb[:, xo + k * 512:xo + (k + 1) * 512],
                         in_=x1s.ap()[k * 128:(k + 1) * 128, tok0:tok0 + 512]).then_inc(LD, 16)
        C.add(LD, KD)
        tile_mm_base = mm_n
        for kk in range(LFC_NP):
            w = _lfc_w(kk)
            hp = 0
            if kk >= 1:
                g.wait_ge(MM, tile_mm_base + kk)
            g.dma_start(out=a_sb[0:w, xo + hp * 512:xo + hp * 512 + 512],
                        in_=hgs.ap()[kk * 128:kk * 128 + w, tok0:tok0 + 512]).then_inc(LW, 16)
            C.add(LW, 1)
            t.wait_ge(LW, 16 * C.val(LW))
            for m in range(KD):
                li = t.matmul(ps[:, m * 512:(m + 1) * 512],
                              w_sb[0:w, kk * 1024 + m * 128:kk * 1024 + (m + 1) * 128],
                              a_sb[0:w, xo + hp * 512:xo + hp * 512 + 512],
                              start=(kk == 0), stop=(kk == LFC_NP - 1))
            li.then_inc(MM, 1)
            mm_n += 1
        v.wait_ge(MM, mm_n)
        v.wait_ge(LD, 16 * C.val(LD))
        if nt >= 1:
            v.wait_ge(STo, 16 * C.val(STo))
            v.wait_ge(ST, 16 * C.val(ST))
        for m in range(KD):
            v.tensor_scalar(t1_sb[:], ps[:, m * 512:(m + 1) * 512],
                            bvec_sb[:, 16 + m:17 + m], None, ALU.add)
            li = v.tensor_tensor(out_sb[:, m * 512:(m + 1) * 512], t1_sb[:],
                                 x1_sb[:, xo + m * 512:xo + (m + 1) * 512], ALU.add)
        li.then_inc(EV, 1)
        ev_n += 1
        evprev[nt] = ev_n
        t.wait_ge(EV, ev_n)
        g.wait_ge(EV, ev_n)
        for k in range(KD):
            g.dma_start(out=x2T[k * 128:(k + 1) * 128, tok0:tok0 + 512],
                        in_=out_sb[:, k * 512:(k + 1) * 512]).then_inc(ST, 16)
            g.dma_start(out=x2Tb[k * 128:(k + 1) * 128, tok0:tok0 + 512],
                        in_=out_sb[:, k * 512:(k + 1) * 512]).then_inc(ST, 16)
        C.add(ST, 2 * KD)
    g.wait_ge(ST0, 16 * C.val(ST0))
    g.wait_ge(ST1, 16 * C.val(ST1))
    es.close()
    return nc


# version 02 (final)
# speedup vs baseline: 1.2910x; 1.2910x over previous
import math
from contextlib import ExitStack
import numpy as np
import ml_dtypes
import concourse.bass as bass
import concourse.mybir as mybir
from concourse.bass import ds
from concourse.bass_core_rust import IndirectOffsetOnAxis
from concourse.bass_utils import run_bass_kernel_spmd

F32 = mybir.dt.float32
BF16 = mybir.dt.bfloat16
I32 = mybir.dt.int32
AF = mybir.ActivationFunctionType
ALU = mybir.AluOpType
AXX = mybir.AxisListType.X

D = 1024
S = 4096
NCORE = 8
KD = D // 128
CHUNK = 512
LR = 0.001


RACE_DETECT = False

def mknc():
    return bass.Bass("TRN2", target_bir_lowering=False,
                     detect_race_conditions=RACE_DETECT)


class Ctr:
    def __init__(self):
        self.c = {}
    def add(self, sem, n=1):
        self.c[sem.name] = self.c.get(sem.name, 0) + n
        return self.c[sem.name]
    def val(self, sem):
        return self.c.get(sem.name, 0)



class ConstPool:
    def __init__(self, nc, es, values, sem_name="CN"):
        self.nc = nc
        self.vals = list(values)
        self.tile = es.enter_context(nc.sbuf_tensor("constpool", [128, max(1, len(self.vals))], F32))
        self.CN = es.enter_context(nc.semaphore(sem_name))
        for i, val in enumerate(self.vals):
            nc.vector.memset(self.tile[:, i:i + 1], float(val))
        nc.vector.nop().then_inc(self.CN, 1)
    def ap(self, value, rows=128):
        i = self.vals.index(value)
        return self.tile[0:rows, i:i + 1]



def build_ln():
    nc = mknc()
    xT = nc.dram_tensor("xT", [D, S], F32, kind="ExternalInput").ap()
    xn_out = nc.dram_tensor("xn_out", [D, S], BF16, kind="ExternalOutput").ap()
    xnf_out = nc.dram_tensor("xnf_out", [D, S], F32, kind="ExternalOutput").ap()
    ones128 = nc.inline_tensor(np.ones((128, 1), np.float32), name="ones128")
    ones1x128 = nc.inline_tensor(np.ones((1, 128), np.float32), name="ones1x128")
    g, t, v, s, sy = nc.gpsimd, nc.tensor, nc.vector, nc.scalar, nc.sync
    es = ExitStack()
    pool = ConstPool(nc, es, [1e-5])
    x_sb = es.enter_context(nc.sbuf_tensor("x_sb", [128, 2 * KD * 512], F32))
    sq_sb = es.enter_context(nc.sbuf_tensor("sq_sb", [128, KD * 512], F32))
    xn_sb = es.enter_context(nc.sbuf_tensor("xn_sb", [128, 2 * KD * 512], F32))
    t1_sb = es.enter_context(nc.sbuf_tensor("t1_sb", [128, 512], F32))
    st_sb = es.enter_context(nc.sbuf_tensor("st_sb", [1, 4 * 512], F32))
    ones_sb = es.enter_context(nc.sbuf_tensor("ones_sb", [128, 129], F32))
    ps = es.enter_context(nc.psum_tensor("ps", [128, 8 * 512], F32))
    LD0 = es.enter_context(nc.semaphore("LD0"))
    LD1 = es.enter_context(nc.semaphore("LD1"))
    MM = es.enter_context(nc.semaphore("MM"))
    EV = es.enter_context(nc.semaphore("EV"))
    SQ = es.enter_context(nc.semaphore("SQ"))
    ST0 = es.enter_context(nc.semaphore("ST0"))
    ST1 = es.enter_context(nc.semaphore("ST1"))
    if True:
        C = Ctr()
        sy.dma_start(out=ones_sb[:, 0:1], in_=ones128.ap()[:, :]).then_inc(LD0, 16)
        sy.dma_start(out=ones_sb[0:1, 1:129], in_=ones1x128.ap()[:, :]).then_inc(LD0, 16)
        C.add(LD0, 2)
        mm_n = ev_n = sq_n = 0
        var_done = {}
        apply_done = {}
        for nt in range(8):
            par = nt % 2
            LD = LD0 if par == 0 else LD1
            xo = par * KD * 512
            tok0 = nt * 512
            if nt >= 2:
                sy.wait_ge(EV, apply_done[nt - 2])
                sy.wait_ge(LD, 16 * C.val(LD))
            for k in range(KD):
                sy.dma_start(out=x_sb[:, xo + k * 512:xo + (k + 1) * 512],
                             in_=xT[k * 128:(k + 1) * 128, tok0:tok0 + 512]).then_inc(LD, 16)
            C.add(LD, KD)
            ld_tot = 16 * C.val(LD)
            s.wait_ge(LD, ld_tot)
            for k in range(KD):
                li = s.activation(sq_sb[:, k * 512:(k + 1) * 512],
                             x_sb[:, xo + k * 512:xo + (k + 1) * 512], AF.Square)
            li.then_inc(SQ, 1)
            sq_n += 1
            t.wait_ge(LD, ld_tot)
            if nt >= 1:
                t.wait_ge(EV, var_done[nt - 1])
            for k in range(KD):
                t.matmul(ps[0:1, 0:512], ones_sb[:, 0:1],
                         x_sb[:, xo + k * 512:xo + (k + 1) * 512],
                         start=(k == 0), stop=(k == KD - 1))
            t.wait_ge(SQ, sq_n)
            for k in range(KD):
                li = t.matmul(ps[0:1, 512:1024], ones_sb[:, 0:1], sq_sb[:, k * 512:(k + 1) * 512],
                         start=(k == 0), stop=(k == KD - 1))
            li.then_inc(MM, 1)
            mm_n += 1
            v.wait_ge(MM, mm_n)
            v.tensor_scalar(st_sb[0:1, 0:512], ps[0:1, 0:512], 1.0 / D, None, ALU.mult)
            v.tensor_scalar(st_sb[0:1, 512:1024], ps[0:1, 512:1024], 1.0 / D, None, ALU.mult)
            v.tensor_tensor(st_sb[0:1, 1024:1536], st_sb[0:1, 0:512], st_sb[0:1, 0:512], ALU.mult)
            v.tensor_tensor(st_sb[0:1, 1024:1536], st_sb[0:1, 512:1024], st_sb[0:1, 1024:1536], ALU.subtract).then_inc(EV, 1)
            ev_n += 1
            var_done[nt] = ev_n
            s.wait_ge(EV, ev_n)
            if nt == 0:
                s.wait_ge(pool.CN, 1)
            s.activation(st_sb[0:1, 1024:1536], st_sb[0:1, 1024:1536], AF.Sqrt, bias=pool.ap(1e-5, rows=1)).then_inc(SQ, 1)
            sq_n += 1
            v.wait_ge(SQ, sq_n)
            v.reciprocal(st_sb[0:1, 1024:1536], st_sb[0:1, 1024:1536])
            v.tensor_tensor(st_sb[0:1, 1536:2048], st_sb[0:1, 0:512], st_sb[0:1, 1024:1536], ALU.mult).then_inc(EV, 1)
            ev_n += 1
            t.wait_ge(EV, ev_n)
            t.matmul(ps[:, 2 * 512:3 * 512], ones_sb[0:1, 1:129], st_sb[0:1, 1024:1536], start=True, stop=True)
            t.matmul(ps[:, 3 * 512:4 * 512], ones_sb[0:1, 1:129], st_sb[0:1, 1536:2048], start=True, stop=True).then_inc(MM, 1)
            mm_n += 1
            v.wait_ge(MM, mm_n)
            ST = ST0 if par == 0 else ST1
            if nt >= 2:
                v.wait_ge(ST, 16 * C.val(ST))
            for k in range(KD):
                v.tensor_tensor(t1_sb[:], x_sb[:, xo + k * 512:xo + (k + 1) * 512], ps[:, 2 * 512:3 * 512], ALU.mult)
                li = v.tensor_tensor(xn_sb[:, xo + k * 512:xo + (k + 1) * 512], t1_sb[:], ps[:, 3 * 512:4 * 512], ALU.subtract)
            li.then_inc(EV, 1)
            ev_n += 1
            apply_done[nt] = ev_n
            g.wait_ge(EV, ev_n)
            for k in range(KD):
                g.dma_start(out=xn_out[k * 128:(k + 1) * 128, tok0:tok0 + 512],
                            in_=xn_sb[:, xo + k * 512:xo + (k + 1) * 512]).then_inc(ST, 16)
                g.dma_start(out=xnf_out[k * 128:(k + 1) * 128, tok0:tok0 + 512],
                            in_=xn_sb[:, xo + k * 512:xo + (k + 1) * 512]).then_inc(ST, 16)
            C.add(ST, 2 * KD)
        g.wait_ge(ST0, 16 * C.val(ST0))
        g.wait_ge(ST1, 16 * C.val(ST1))
    es.close()
    return nc



def build_chunk(kchunk):
    nc = mknc()
    xnT = nc.dram_tensor("xnT", [D, CHUNK], BF16, kind="ExternalInput").ap()
    w0 = nc.dram_tensor("w0", [D, 4096], F32, kind="ExternalInput").ap()
    w2 = nc.dram_tensor("w2", [D, 4096], F32, kind="ExternalInput").ap()
    w1 = nc.dram_tensor("w1", [4096, D], F32, kind="ExternalInput").ap()
    if kchunk > 0:
        ssep = nc.dram_tensor("ssep", [8, kchunk], F32, kind="ExternalInput").ap()
    s_out = nc.dram_tensor("s_out", [D, CHUNK], BF16, kind="ExternalOutput").ap()
    sse_out = nc.dram_tensor("sse_out", [1, 1], F32, kind="ExternalOutput").ap()

    ones128 = nc.inline_tensor(np.ones((128, 1), np.float32), name="ones128")
    ones1x128 = nc.inline_tensor(np.ones((1, 128), np.float32), name="ones1x128")
    ones8 = nc.inline_tensor(np.ones((8, 1), np.float32), name="ones8")
    g, t, v, s, sy = nc.gpsimd, nc.tensor, nc.vector, nc.scalar, nc.sync
    es = ExitStack()
    xn_sb = es.enter_context(nc.sbuf_tensor("xn_sb", [128, KD * CHUNK], BF16))
    w_sb = es.enter_context(nc.sbuf_tensor("w_sb", [128, 8 * 4096], BF16))
    p_sb = es.enter_context(nc.sbuf_tensor("p_sb", [128, 32 * CHUNK], BF16))
    q_sb = es.enter_context(nc.sbuf_tensor("q_sb", [128, 32 * CHUNK], BF16))
    t1_sb = es.enter_context(nc.sbuf_tensor("t1_sb", [128, 512], F32))
    t2_sb = es.enter_context(nc.sbuf_tensor("t2_sb", [128, 512], F32))
    diff_sb = es.enter_context(nc.sbuf_tensor("diff_sb", [128, 512], F32))
    junk_sb = es.enter_context(nc.sbuf_tensor("junk_sb", [128, 512], F32))
    acc_sb = es.enter_context(nc.sbuf_tensor("acc_sb", [128, 8], F32))
    sse_sb = es.enter_context(nc.sbuf_tensor("sse_sb", [1, 8], F32))
    ones_sb = es.enter_context(nc.sbuf_tensor("ones_sb", [128, 129], F32))
    o8_sb = es.enter_context(nc.sbuf_tensor("o8_sb", [8, 1], F32))
    ssep_sb = es.enter_context(nc.sbuf_tensor("ssep_sb", [8, 8], F32))
    dsc_sb = es.enter_context(nc.sbuf_tensor("dsc_sb", [1, 16], F32))
    dbc_sb = es.enter_context(nc.sbuf_tensor("dbc_sb", [128, 1], F32))
    ps = es.enter_context(nc.psum_tensor("ps", [128, 8 * 512], F32))
    LD = es.enter_context(nc.semaphore("LD"))
    LW = es.enter_context(nc.semaphore("LW"))
    MM = es.enter_context(nc.semaphore("MM"))
    EV = es.enter_context(nc.semaphore("EV"))
    SQ = es.enter_context(nc.semaphore("SQ"))
    ST = es.enter_context(nc.semaphore("ST"))
    if True:
        C = Ctr()
        mm_n = ev_n = sq_n = 0
        sy.dma_start(out=ones_sb[:, 0:1], in_=ones128.ap()[:, :]).then_inc(LD, 16)
        sy.dma_start(out=ones_sb[0:1, 1:129], in_=ones1x128.ap()[:, :]).then_inc(LD, 16)
        sy.dma_start(out=o8_sb[:, :], in_=ones8.ap()[:, :]).then_inc(LD, 16)
        C.add(LD, 3)
        for k in range(KD):
            sy.dma_start(out=xn_sb[:, k * CHUNK:(k + 1) * CHUNK],
                         in_=xnT[k * 128:(k + 1) * 128, :]).then_inc(LD, 16)
        C.add(LD, KD)

        if kchunk > 0:
            v.memset(ssep_sb[:, :], 0.0).then_inc(EV, 1)
            ev_n += 1
            sy.wait_ge(EV, ev_n)
            sy.dma_start(out=ssep_sb[0:8, 0:kchunk], in_=ssep[:, :]).then_inc(LD, 16)
            C.add(LD, 1)
        t.wait_ge(LD, 16 * C.val(LD))
        if kchunk > 0:
            t.matmul(ps[0:1, 0:8], o8_sb[0:8, 0:1], ssep_sb[0:8, 0:8], start=True, stop=True).then_inc(MM, 1)
            mm_n += 1
            v.wait_ge(MM, mm_n)
            v.tensor_scalar(dsc_sb[0:1, 0:kchunk], ps[0:1, 0:kchunk],
                            -LR / (NCORE * CHUNK * D), 1.0, ALU.mult, ALU.add).then_inc(EV, 1)
            ev_n += 1
            v.wait_ge(EV, ev_n)
            v.tensor_copy(dsc_sb[0:1, 8:9], dsc_sb[0:1, 0:1]).then_inc(EV, 1)
            ev_n += 1
            for j in range(1, kchunk):
                v.wait_ge(EV, ev_n)
                v.tensor_tensor(dsc_sb[0:1, 8:9], dsc_sb[0:1, 8:9], dsc_sb[0:1, j:j + 1], ALU.mult).then_inc(EV, 1)
                ev_n += 1
            v.wait_ge(EV, ev_n)
            li = v.tensor_copy(dsc_sb[0:1, 8:12], dsc_sb[0:1, 8:9].to_broadcast([1, 4]))
            li.then_inc(EV, 1)
            ev_n += 1
            t.wait_ge(EV, ev_n)
            t.matmul(ps[:, 0:4], ones_sb[0:1, 1:129], dsc_sb[0:1, 8:12], start=True, stop=True).then_inc(MM, 1)
            mm_n += 1
            v.wait_ge(MM, mm_n)
            v.tensor_copy(dbc_sb[:, 0:1], ps[:, 0:1]).then_inc(EV, 1)
            ev_n += 1
            t.wait_ge(EV, ev_n)

        def load_w_kchunks(src, n_k, width):
            for k in range(n_k):
                g.dma_start(out=w_sb[:, k * width:(k + 1) * width],
                            in_=src[k * 128:(k + 1) * 128, :]).then_inc(LW, 16)
            C.add(LW, n_k)

        def proj_32m(dst_sb):
            nonlocal mm_n, ev_n
            t.wait_ge(LW, 16 * C.val(LW))
            for m in range(32):
                bank = m % 4
                for k in range(KD):
                    ins = t.matmul(ps[:, bank * 512:(bank + 1) * 512],
                                   w_sb[:, k * 4096 + m * 128:k * 4096 + (m + 1) * 128],
                                   xn_sb[:, k * CHUNK:k * CHUNK + 512],
                                   start=(k == 0), stop=(k == KD - 1))
                ins.then_inc(MM, 1)
                mm_n += 1
                v.wait_ge(MM, mm_n)
                v.tensor_copy(dst_sb[:, m * 512:(m + 1) * 512], ps[:, bank * 512:(bank + 1) * 512]).then_inc(EV, 1)
                ev_n += 1
                if m >= 3:
                    t.wait_ge(EV, ev_n - 3)

        load_w_kchunks(w0, KD, 4096)
        proj_32m(p_sb)
        g.wait_ge(MM, mm_n)
        load_w_kchunks(w2, KD, 4096)
        proj_32m(q_sb)

        s.wait_ge(EV, ev_n)
        for m in range(32):
            if kchunk > 0:
                s.activation(t1_sb[:], p_sb[:, m * 512:(m + 1) * 512], AF.Silu, scale=dbc_sb[:, 0:1]).then_inc(SQ, 1)
            else:
                s.activation(t1_sb[:], p_sb[:, m * 512:(m + 1) * 512], AF.Silu).then_inc(SQ, 1)
            sq_n += 1
            v.wait_ge(SQ, sq_n)
            if kchunk > 0:
                v.tensor_scalar(t2_sb[:], q_sb[:, m * 512:(m + 1) * 512], dbc_sb[:, 0:1], None, ALU.mult)
                li = v.tensor_tensor(p_sb[:, m * 512:(m + 1) * 512], t1_sb[:], t2_sb[:], ALU.mult)
            else:
                li = v.tensor_tensor(p_sb[:, m * 512:(m + 1) * 512], t1_sb[:],
                                q_sb[:, m * 512:(m + 1) * 512], ALU.mult)
            li.then_inc(EV, 1)
            ev_n += 1
            s.wait_ge(EV, ev_n)

        g.wait_ge(MM, mm_n)
        for k in range(32):
            g.dma_start(out=w_sb[:, k * 1024:(k + 1) * 1024],
                        in_=w1[k * 128:(k + 1) * 128, :]).then_inc(LW, 16)
        C.add(LW, 32)
        t.wait_ge(LW, 16 * C.val(LW))
        t.wait_ge(EV, ev_n)
        for m in range(8):
            for k in range(32):
                ins = t.matmul(ps[:, m * 512:(m + 1) * 512],
                               w_sb[:, k * 1024 + m * 128:k * 1024 + (m + 1) * 128],
                               p_sb[:, k * 512:(k + 1) * 512],
                               start=(k == 0), stop=(k == 31))
        ins.then_inc(MM, 1)
        mm_n += 1
        s.wait_ge(MM, mm_n)
        v.wait_ge(MM, mm_n)
        for m in range(8):
            if kchunk > 0:
                s.activation(p_sb[:, m * 512:(m + 1) * 512], ps[:, m * 512:(m + 1) * 512], AF.Copy,
                             scale=dbc_sb[:, 0:1]).then_inc(SQ, 1)
            else:
                s.activation(p_sb[:, m * 512:(m + 1) * 512], ps[:, m * 512:(m + 1) * 512], AF.Copy).then_inc(SQ, 1)
            sq_n += 1
            v.wait_ge(SQ, sq_n)
            v.tensor_tensor(diff_sb[:], p_sb[:, m * 512:(m + 1) * 512],
                            xn_sb[:, m * CHUNK:(m + 1) * CHUNK], ALU.subtract).then_inc(EV, 1)
            ev_n += 1
            s.wait_ge(EV, ev_n)
            s.activation(junk_sb[:], diff_sb[:], AF.Square, accum_out=acc_sb[:, m:m + 1]).then_inc(SQ, 1)
            sq_n += 1
        t.wait_ge(SQ, sq_n)
        t.matmul(ps[0:1, 0:8], ones_sb[:, 0:1], acc_sb[:], start=True, stop=True).then_inc(MM, 1)
        mm_n += 1
        v.wait_ge(MM, mm_n)
        v.reduce_sum(sse_sb[0:1, 0:1], ps[0:1, 0:8], axis=AXX).then_inc(EV, 1)
        ev_n += 1
        g.wait_ge(EV, ev_n)
        g.wait_ge(SQ, sq_n)
        for m in range(8):
            g.dma_start(out=s_out[m * 128:(m + 1) * 128, :],
                        in_=p_sb[:, m * 512:(m + 1) * 512]).then_inc(ST, 16)
        g.dma_start(out=sse_out[:, :], in_=sse_sb[0:1, 0:1]).then_inc(ST, 16)
        C.add(ST, 9)
        g.wait_ge(ST, 16 * C.val(ST))
    es.close()
    return nc





def build_l9a():
    nc = mknc()
    xT = nc.dram_tensor("xT", [D, S], F32, kind="ExternalInput").ap()
    sT = nc.dram_tensor("sT", [D, S], BF16, kind="ExternalInput").ap()
    fT = nc.dram_tensor("fT", [D, S], BF16, kind="ExternalInput").ap()
    ln1_g = nc.dram_tensor("ln1_g", [D, 1], F32, kind="ExternalInput").ap()
    ln1_b = nc.dram_tensor("ln1_b", [D, 1], F32, kind="ExternalInput").ap()
    gate = nc.dram_tensor("gate", [D, 8], F32, kind="ExternalInput").ap()
    x1T = nc.dram_tensor("x1T", [D, S], F32, kind="ExternalOutput").ap()
    xn1T = nc.dram_tensor("xn1T", [D, S], BF16, kind="ExternalOutput").ap()
    yT = nc.dram_tensor("yT", [D, S], BF16, kind="ExternalOutput").ap()
    probs = nc.dram_tensor("probs", [S, 8], F32, kind="ExternalOutput").ap()
    ones128 = nc.inline_tensor(np.ones((128, 1), np.float32), name="ones128")
    ones1x128 = nc.inline_tensor(np.ones((1, 128), np.float32), name="ones1x128")
    g, t, v, s, sy = nc.gpsimd, nc.tensor, nc.vector, nc.scalar, nc.sync
    es = ExitStack()
    pool = ConstPool(nc, es, [1e-5])
    x_sb = es.enter_context(nc.sbuf_tensor("x_sb", [128, 2 * KD * 512], F32))
    s_sb = es.enter_context(nc.sbuf_tensor("s_sb", [128, 2 * KD * 512], BF16))
    f_sb = es.enter_context(nc.sbuf_tensor("f_sb", [128, 2 * KD * 512], BF16))
    x1_sb = es.enter_context(nc.sbuf_tensor("x1_sb", [128, 2 * KD * 512], F32))
    xn1_sb = es.enter_context(nc.sbuf_tensor("xn1_sb", [128, 2 * KD * 512], BF16))
    y_sb = es.enter_context(nc.sbuf_tensor("y_sb", [128, 2 * KD * 512], BF16))
    sq_sb = es.enter_context(nc.sbuf_tensor("sq_sb", [128, KD * 512], F32))
    t1_sb = es.enter_context(nc.sbuf_tensor("t1_sb", [128, 512], F32))
    st_sb = es.enter_context(nc.sbuf_tensor("st_sb", [1, 4 * 512], F32))
    lnp_sb = es.enter_context(nc.sbuf_tensor("lnp_sb", [128, 16], F32))
    gate_sb = es.enter_context(nc.sbuf_tensor("gate_sb", [128, KD * 8], BF16))
    sm_sb = es.enter_context(nc.sbuf_tensor("sm_sb", [128, 16], F32))
    pr_sb = es.enter_context(nc.sbuf_tensor("pr_sb", [128, 8], F32))
    ones_sb = es.enter_context(nc.sbuf_tensor("ones_sb", [128, 129], F32))
    ones_bsb = es.enter_context(nc.sbuf_tensor("ones_bsb", [128, 1], BF16))
    ps = es.enter_context(nc.psum_tensor("ps", [128, 8 * 512], F32))
    LD0 = es.enter_context(nc.semaphore("LD0"))
    LD1 = es.enter_context(nc.semaphore("LD1"))
    MM = es.enter_context(nc.semaphore("MM"))
    EV = es.enter_context(nc.semaphore("EV"))
    SQ = es.enter_context(nc.semaphore("SQ"))
    ST0 = es.enter_context(nc.semaphore("ST0"))
    ST1 = es.enter_context(nc.semaphore("ST1"))
    GL = es.enter_context(nc.semaphore("GL"))
    C = Ctr()
    mm_n = ev_n = sq_n = 0
    g.dma_start(out=ones_bsb[:, 0:1], in_=ones128.ap()[:, :]).then_inc(GL, 16)
    C.add(GL, 1)
    sy.dma_start(out=ones_sb[:, 0:1], in_=ones128.ap()[:, :]).then_inc(LD0, 16)
    sy.dma_start(out=ones_sb[0:1, 1:129], in_=ones1x128.ap()[:, :]).then_inc(LD0, 16)
    for k in range(KD):
        sy.dma_start(out=lnp_sb[:, k:k + 1], in_=ln1_g[k * 128:(k + 1) * 128, :]).then_inc(LD0, 16)
        sy.dma_start(out=lnp_sb[:, 8 + k:9 + k], in_=ln1_b[k * 128:(k + 1) * 128, :]).then_inc(LD0, 16)
        g.dma_start(out=gate_sb[:, k * 8:(k + 1) * 8], in_=gate[k * 128:(k + 1) * 128, :]).then_inc(GL, 16)
    C.add(LD0, 2 + 2 * KD)
    C.add(GL, KD)

    def ln_stats(src_sb, xo, n_sq_off=0):
        nonlocal mm_n, ev_n, sq_n
        o1 = ones_sb[:, 0:1] if src_sb.dtype == F32 else ones_bsb[:, 0:1]
        s.wait_ge(EV, ev_n)
        for k in range(KD):
            li = s.activation(sq_sb[:, k * 512:(k + 1) * 512],
                              src_sb[:, xo + k * 512:xo + (k + 1) * 512], AF.Square)
        li.then_inc(SQ, 1)
        sq_n += 1
        for k in range(KD):
            t.matmul(ps[0:1, 0:512], o1,
                     src_sb[:, xo + k * 512:xo + (k + 1) * 512],
                     start=(k == 0), stop=(k == KD - 1))
        t.wait_ge(SQ, sq_n)
        for k in range(KD):
            li = t.matmul(ps[0:1, 512:1024], ones_sb[:, 0:1], sq_sb[:, k * 512:(k + 1) * 512],
                          start=(k == 0), stop=(k == KD - 1))
        li.then_inc(MM, 1)
        mm_n += 1
        v.wait_ge(MM, mm_n)
        v.tensor_scalar(st_sb[0:1, 0:512], ps[0:1, 0:512], 1.0 / D, None, ALU.mult)
        v.tensor_scalar(st_sb[0:1, 512:1024], ps[0:1, 512:1024], 1.0 / D, None, ALU.mult)
        v.tensor_tensor(st_sb[0:1, 1024:1536], st_sb[0:1, 0:512], st_sb[0:1, 0:512], ALU.mult)
        v.tensor_tensor(st_sb[0:1, 1024:1536], st_sb[0:1, 512:1024], st_sb[0:1, 1024:1536], ALU.subtract).then_inc(EV, 1)
        ev_n += 1
        s.wait_ge(EV, ev_n)
        s.activation(st_sb[0:1, 1024:1536], st_sb[0:1, 1024:1536], AF.Sqrt, bias=pool.ap(1e-5, rows=1)).then_inc(SQ, 1)
        sq_n += 1
        v.wait_ge(SQ, sq_n)
        v.reciprocal(st_sb[0:1, 1024:1536], st_sb[0:1, 1024:1536])
        v.tensor_tensor(st_sb[0:1, 1536:2048], st_sb[0:1, 0:512], st_sb[0:1, 1024:1536], ALU.mult).then_inc(EV, 1)
        ev_n += 1
        t.wait_ge(EV, ev_n)
        t.matmul(ps[:, 2 * 512:3 * 512], ones_sb[0:1, 1:129], st_sb[0:1, 1024:1536], start=True, stop=True)
        li = t.matmul(ps[:, 3 * 512:4 * 512], ones_sb[0:1, 1:129], st_sb[0:1, 1536:2048], start=True, stop=True)
        li.then_inc(MM, 1)
        mm_n += 1
        v.wait_ge(MM, mm_n)

    s.wait_ge(pool.CN, 1)
    cons_done = {}
    store_done = {}
    for nt in range(8):
        par = nt % 2
        LD = LD0 if par == 0 else LD1
        ST = ST0 if par == 0 else ST1
        xo = par * KD * 512
        tok0 = nt * 512
        if nt >= 2:
            sy.wait_ge(EV, cons_done[nt - 2])
            sy.wait_ge(LD, 16 * C.val(LD))
            g.wait_ge(EV, cons_done[nt - 2])
        for k in range(KD):
            sy.dma_start(out=x_sb[:, xo + k * 512:xo + (k + 1) * 512],
                         in_=xT[k * 128:(k + 1) * 128, tok0:tok0 + 512]).then_inc(LD, 16)
            sy.dma_start(out=s_sb[:, xo + k * 512:xo + (k + 1) * 512],
                         in_=sT[k * 128:(k + 1) * 128, tok0:tok0 + 512]).then_inc(LD, 16)
            g.dma_start(out=f_sb[:, xo + k * 512:xo + (k + 1) * 512],
                        in_=fT[k * 128:(k + 1) * 128, tok0:tok0 + 512]).then_inc(GL, 16)
        C.add(LD, 2 * KD)
        C.add(GL, KD)
        ld_tot = 16 * C.val(LD)
        s.wait_ge(LD, ld_tot)
        s.wait_ge(GL, 16 * C.val(GL))
        v.wait_ge(LD, ld_tot)
        v.wait_ge(GL, 16 * C.val(GL))
        if nt >= 2:
            v.wait_ge(ST, 16 * C.val(ST))
        for k in range(KD):
            s.activation(t1_sb[:], f_sb[:, xo + k * 512:xo + (k + 1) * 512], AF.Silu).then_inc(SQ, 1)
            sq_n += 1
            v.wait_ge(SQ, sq_n)
            v.tensor_tensor(t1_sb[:], t1_sb[:], s_sb[:, xo + k * 512:xo + (k + 1) * 512], ALU.mult)
            li = v.tensor_tensor(x1_sb[:, xo + k * 512:xo + (k + 1) * 512], t1_sb[:],
                                 x_sb[:, xo + k * 512:xo + (k + 1) * 512], ALU.add)
            li.then_inc(EV, 1)
            ev_n += 1
            s.wait_ge(EV, ev_n)
        t.wait_ge(EV, ev_n)
        ln_stats(x1_sb, xo)
        for k in range(KD):
            v.tensor_tensor(t1_sb[:], x1_sb[:, xo + k * 512:xo + (k + 1) * 512], ps[:, 2 * 512:3 * 512], ALU.mult)
            li = v.tensor_tensor(xn1_sb[:, xo + k * 512:xo + (k + 1) * 512], t1_sb[:], ps[:, 3 * 512:4 * 512], ALU.subtract)
        li.then_inc(EV, 1)
        ev_n += 1
        t.wait_ge(EV, ev_n)
        ln_stats(xn1_sb, xo)
        for k in range(KD):
            v.tensor_tensor(t1_sb[:], xn1_sb[:, xo + k * 512:xo + (k + 1) * 512], ps[:, 2 * 512:3 * 512], ALU.mult)
            v.tensor_tensor(t1_sb[:], t1_sb[:], ps[:, 3 * 512:4 * 512], ALU.subtract)
            li = v.tensor_scalar(y_sb[:, xo + k * 512:xo + (k + 1) * 512], t1_sb[:],
                                 lnp_sb[:, k:k + 1], lnp_sb[:, 8 + k:9 + k], ALU.mult, ALU.add)
        li.then_inc(EV, 1)
        ev_n += 1
        cons_done[nt] = ev_n
        t.wait_ge(EV, ev_n)
        for tg in range(4):
            for k in range(KD):
                li = t.matmul(ps[:, 4 * 512 + tg * 8:4 * 512 + (tg + 1) * 8],
                              xn1_sb[:, xo + k * 512 + tg * 128:xo + k * 512 + (tg + 1) * 128],
                              gate_sb[:, k * 8:(k + 1) * 8],
                              start=(k == 0), stop=(k == KD - 1))
        li.then_inc(MM, 1)
        mm_n += 1
        v.wait_ge(MM, mm_n)
        for tg in range(4):
            pslice = ps[:, 4 * 512 + tg * 8:4 * 512 + (tg + 1) * 8]
            v.tensor_copy(sm_sb[:, 0:8], pslice).then_inc(EV, 1)
            ev_n += 1
            v.wait_ge(EV, ev_n)
            v.reduce_max(sm_sb[:, 8:9], sm_sb[:, 0:8], axis=AXX).then_inc(EV, 1)
            ev_n += 1
            v.wait_ge(EV, ev_n)
            v.tensor_scalar(sm_sb[:, 0:8], sm_sb[:, 0:8], sm_sb[:, 8:9], None, ALU.subtract).then_inc(EV, 1)
            ev_n += 1
            s.wait_ge(EV, ev_n)
            s.activation(sm_sb[:, 0:8], sm_sb[:, 0:8], AF.Exp).then_inc(SQ, 1)
            sq_n += 1
            v.wait_ge(SQ, sq_n)
            v.reduce_sum(sm_sb[:, 8:9], sm_sb[:, 0:8], axis=AXX).then_inc(EV, 1)
            ev_n += 1
            v.wait_ge(EV, ev_n)
            v.reciprocal(sm_sb[:, 9:10], sm_sb[:, 8:9]).then_inc(EV, 1)
            ev_n += 1
            v.wait_ge(EV, ev_n)
            v.tensor_scalar(pr_sb[:, 0:8], sm_sb[:, 0:8], sm_sb[:, 9:10], None, ALU.mult).then_inc(EV, 1)
            ev_n += 1
            g.wait_ge(EV, ev_n)
            g.dma_start(out=probs[tok0 + tg * 128:tok0 + (tg + 1) * 128, :],
                        in_=pr_sb[:, 0:8]).then_inc(ST, 16)
            C.add(ST, 1)
            v.wait_ge(ST, 16 * C.val(ST))
        g.wait_ge(EV, cons_done[nt])
        for k in range(KD):
            g.dma_start(out=x1T[k * 128:(k + 1) * 128, tok0:tok0 + 512],
                        in_=x1_sb[:, xo + k * 512:xo + (k + 1) * 512]).then_inc(ST, 16)
            g.dma_start(out=xn1T[k * 128:(k + 1) * 128, tok0:tok0 + 512],
                        in_=xn1_sb[:, xo + k * 512:xo + (k + 1) * 512]).then_inc(ST, 16)
            g.dma_start(out=yT[k * 128:(k + 1) * 128, tok0:tok0 + 512],
                        in_=y_sb[:, xo + k * 512:xo + (k + 1) * 512]).then_inc(ST, 16)
        C.add(ST, 3 * KD)
    g.wait_ge(ST0, 16 * C.val(ST0))
    g.wait_ge(ST1, 16 * C.val(ST1))
    es.close()
    return nc



def build_l9b():
    nc = mknc()
    yT = nc.dram_tensor("yT", [D, S], BF16, kind="ExternalInput").ap()
    Ws = {}
    bs = {}
    for P in ("q", "k", "v", "g"):
        Ws[P] = nc.dram_tensor("W" + P, [D, D], F32, kind="ExternalInput").ap()
        bs[P] = nc.dram_tensor("b" + P, [D, 1], F32, kind="ExternalInput").ap()
    dlog = nc.dram_tensor("dlog", [16, 1], F32, kind="ExternalInput").ap()
    outT = nc.dram_tensor("outT", [D, S], BF16, kind="ExternalOutput").ap()

    mask16 = np.zeros((16, 8), np.float32)
    for h in range(16):
        mask16[h, h // 2] = 1.0
    hsel = np.zeros((16, 128), np.float32)
    for h in range(16):
        for p in range(128):
            if p // 64 == h % 2:
                hsel[h, p] = 1.0
    mask16_t = nc.inline_tensor(mask16, name="mask16")
    hsel_t = nc.inline_tensor(hsel, name="hsel")

    g, t, v, s, sy = nc.gpsimd, nc.tensor, nc.vector, nc.scalar, nc.sync
    es = ExitStack()
    y_sb = es.enter_context(nc.sbuf_tensor("y_sb", [128, KD * S], BF16))
    wp_sb = es.enter_context(nc.sbuf_tensor("wp_sb", [128, 4 * KD * 128], BF16))
    bias_sb = es.enter_context(nc.sbuf_tensor("bias_sb", [128, 32], F32))
    qe_sb = es.enter_context(nc.sbuf_tensor("qe_sb", [128, 512], F32))
    ke_sb = es.enter_context(nc.sbuf_tensor("ke_sb", [128, 512], F32))
    ve_sb = es.enter_context(nc.sbuf_tensor("ve_sb", [128, 512], F32))
    ge_sb = es.enter_context(nc.sbuf_tensor("ge_sb", [128, 512], F32))
    kv_sb = es.enter_context(nc.sbuf_tensor("kv_sb", [128, S], F32))
    qg_sb = es.enter_context(nc.sbuf_tensor("qg_sb", [128, S], F32))
    state_sb = es.enter_context(nc.sbuf_tensor("state_sb", [128, S], F32))
    out_sb = es.enter_context(nc.sbuf_tensor("out_sb", [128, 2 * S], BF16))
    dl_sb = es.enter_context(nc.sbuf_tensor("dl_sb", [16, 2], F32))
    m16_sb = es.enter_context(nc.sbuf_tensor("m16_sb", [16, 8], F32))
    hsel_sb = es.enter_context(nc.sbuf_tensor("hsel_sb", [16, 128], F32))
    rhs_sb = es.enter_context(nc.sbuf_tensor("rhs_sb", [16, 8], F32))
    dec_sb = es.enter_context(nc.sbuf_tensor("dec_sb", [128, 8], F32))
    ps = es.enter_context(nc.psum_tensor("ps", [128, 8 * 512], F32))
    LD = es.enter_context(nc.semaphore("LD"))
    LW = es.enter_context(nc.semaphore("LW"))
    MM = es.enter_context(nc.semaphore("MM"))
    EV = es.enter_context(nc.semaphore("EV"))
    SQ = es.enter_context(nc.semaphore("SQ"))
    ST0 = es.enter_context(nc.semaphore("ST0"))
    ST1 = es.enter_context(nc.semaphore("ST1"))
    C = Ctr()
    mm_n = ev_n = sq_n = 0
    for k in range(KD):
        for nt2 in range(8):
            sy.dma_start(out=y_sb[:, k * S + nt2 * 512:k * S + (nt2 + 1) * 512],
                         in_=yT[k * 128:(k + 1) * 128, nt2 * 512:(nt2 + 1) * 512]).then_inc(LD, 16)
    C.add(LD, KD * 8)
    sy.dma_start(out=dl_sb[0:16, 0:1], in_=dlog[:, :]).then_inc(LD, 16)
    sy.dma_start(out=m16_sb[:, :], in_=mask16_t.ap()[:, :]).then_inc(LD, 16)
    sy.dma_start(out=hsel_sb[:, :], in_=hsel_t.ap()[:, :]).then_inc(LD, 16)
    C.add(LD, 3)
    for Pi, P in enumerate(("q", "k", "v", "g")):
        for c in range(8):
            sy.dma_start(out=bias_sb[:, Pi * 8 + c:Pi * 8 + c + 1],
                         in_=bs[P][c * 128:(c + 1) * 128, :]).then_inc(LD, 16)
    C.add(LD, 32)
    s.wait_ge(LD, 16 * C.val(LD))
    s.activation(dl_sb[0:16, 1:2], dl_sb[0:16, 0:1], AF.Sigmoid).then_inc(SQ, 1)
    sq_n += 1
    v.wait_ge(SQ, sq_n)
    v.tensor_scalar(rhs_sb[0:16, :], m16_sb[0:16, :], dl_sb[0:16, 1:2], None, ALU.mult).then_inc(EV, 1)
    ev_n += 1
    t.wait_ge(EV, ev_n)
    t.matmul(ps[:, 7 * 512:7 * 512 + 8], hsel_sb[0:16, :], rhs_sb[0:16, :], start=True, stop=True).then_inc(MM, 1)
    mm_n += 1
    v.wait_ge(MM, mm_n)
    v.tensor_copy(dec_sb[:, :], ps[:, 7 * 512:7 * 512 + 8]).then_inc(EV, 1)
    ev_n += 1

    t.wait_ge(LD, 16 * C.val(LD))
    for c in range(8):
        if c > 0:
            g.wait_ge(MM, mm_n)
        for Pi, P in enumerate(("q", "k", "v", "g")):
            g.dma_start(out=wp_sb[:, Pi * KD * 128:(Pi + 1) * KD * 128].rearrange("p (kk m) -> p kk m", kk=KD),
                        in_=Ws[P][:, c * 128:(c + 1) * 128].rearrange("(kk p) m -> p kk m", p=128)).then_inc(LW, 16)
        C.add(LW, 4)
        t.wait_ge(LW, 16 * C.val(LW))
        cstart_sq = sq_n
        for nt in range(8):
            for Pi in range(4):
                for kk in range(KD):
                    li = t.matmul(ps[:, Pi * 512:(Pi + 1) * 512],
                                  wp_sb[:, Pi * KD * 128 + kk * 128:Pi * KD * 128 + (kk + 1) * 128],
                                  y_sb[:, kk * S + nt * 512:kk * S + (nt + 1) * 512],
                                  start=(kk == 0), stop=(kk == KD - 1))
            li.then_inc(MM, 1)
            mm_n += 1
            s.wait_ge(MM, mm_n)
            li = s.activation(ge_sb[:], ps[:, 1536:2048], AF.Sigmoid, bias=bias_sb[:, 3 * 8 + c:3 * 8 + c + 1])
            li.then_inc(SQ, 1)
            sq_n += 1
            v.wait_ge(MM, mm_n)
            v.tensor_scalar(qe_sb[:], ps[:, 0:512], bias_sb[:, 0 * 8 + c:0 * 8 + c + 1], None, ALU.add)
            v.tensor_scalar(ke_sb[:], ps[:, 512:1024], bias_sb[:, 1 * 8 + c:1 * 8 + c + 1], None, ALU.add)
            v.tensor_scalar(ve_sb[:], ps[:, 1024:1536], bias_sb[:, 2 * 8 + c:2 * 8 + c + 1], None, ALU.add)
            v.tensor_tensor(kv_sb[:, nt * 512:(nt + 1) * 512], ke_sb[:], ve_sb[:], ALU.mult)
            v.wait_ge(SQ, sq_n)
            li = v.tensor_tensor(qg_sb[:, nt * 512:(nt + 1) * 512], qe_sb[:], ge_sb[:], ALU.mult)
            li.then_inc(EV, 1)
            ev_n += 1
            t.wait_ge(SQ, sq_n)
            t.wait_ge(EV, ev_n)
        v.tensor_tensor_scan(state_sb[:, :], dec_sb[:, c:c + 1].to_broadcast([128, S]),
                             kv_sb[:, :], 0.0, ALU.mult, ALU.add)
        par = c % 2
        ST = ST0 if par == 0 else ST1
        if c >= 2:
            v.wait_ge(ST, 16 * C.val(ST))
        oo = par * S
        li = v.scalar_tensor_tensor(out_sb[:, oo:oo + S], state_sb[:, :], 0.125, qg_sb[:, :],
                                    ALU.mult, ALU.mult)
        li.then_inc(EV, 1)
        ev_n += 1
        g.wait_ge(EV, ev_n)
        g.dma_start(out=outT[c * 128:(c + 1) * 128, :], in_=out_sb[:, oo:oo + S]).then_inc(ST, 16)
        C.add(ST, 1)
    g.wait_ge(ST0, 16 * C.val(ST0))
    g.wait_ge(ST1, 16 * C.val(ST1))
    es.close()
    return nc



def build_l9c():
    nc = mknc()
    outT = nc.dram_tensor("outT", [D, S], BF16, kind="ExternalInput").ap()
    xn1T = nc.dram_tensor("xn1T", [D, S], BF16, kind="ExternalInput").ap()
    Wo = nc.dram_tensor("Wo", [D, D], F32, kind="ExternalInput").ap()
    bo = nc.dram_tensor("bo", [D, 1], F32, kind="ExternalInput").ap()
    ln2_g = nc.dram_tensor("ln2_g", [D, 1], F32, kind="ExternalInput").ap()
    ln2_b = nc.dram_tensor("ln2_b", [D, 1], F32, kind="ExternalInput").ap()
    W1 = nc.dram_tensor("W1", [D, 4096], F32, kind="ExternalInput").ap()
    b1 = nc.dram_tensor("b1", [4096, 1], F32, kind="ExternalInput").ap()
    W2 = nc.dram_tensor("W2", [4096, D], F32, kind="ExternalInput").ap()
    b2 = nc.dram_tensor("b2", [D, 1], F32, kind="ExternalInput").ap()
    s1T = nc.dram_tensor("s1T", [D, S], BF16, kind="ExternalOutput").ap()
    x2s = nc.dram_tensor("x2s", [D, S], F32, kind="Internal")
    hsrc = nc.dram_tensor("hsrc", [D, S], BF16, kind="Internal")
    hgs = nc.dram_tensor("hgs", [4096, S], BF16, kind="Internal")
    ones128 = nc.inline_tensor(np.ones((128, 1), np.float32), name="ones128")
    ones1x128 = nc.inline_tensor(np.ones((1, 128), np.float32), name="ones1x128")
    g, t, v, s, sy = nc.gpsimd, nc.tensor, nc.vector, nc.scalar, nc.sync
    es = ExitStack()
    pool = ConstPool(nc, es, [1e-5])
    w_sb = es.enter_context(nc.sbuf_tensor("w_sb", [128, 32 * 1024], BF16))
    wo_sb = es.enter_context(nc.sbuf_tensor("wo_sb", [128, KD * 1024], BF16))
    a_sb = es.enter_context(nc.sbuf_tensor("a_sb", [128, 2 * KD * 512], BF16))
    b_sb = es.enter_context(nc.sbuf_tensor("b_sb", [128, 2 * KD * 512], BF16))
    x2_sb = es.enter_context(nc.sbuf_tensor("x2_sb", [128, 2 * KD * 512], F32))
    h_sb = es.enter_context(nc.sbuf_tensor("h_sb", [128, 2 * KD * 512], BF16))
    hg_sb = es.enter_context(nc.sbuf_tensor("hg_sb", [128, 2 * 512], BF16))
    sq_sb = es.enter_context(nc.sbuf_tensor("sq_sb", [128, KD * 512], F32))
    t1_sb = es.enter_context(nc.sbuf_tensor("t1_sb", [128, 512], F32))
    st_sb = es.enter_context(nc.sbuf_tensor("st_sb", [1, 4 * 512], F32))
    bvec_sb = es.enter_context(nc.sbuf_tensor("bvec_sb", [128, 64], F32))
    ones_sb = es.enter_context(nc.sbuf_tensor("ones_sb", [128, 129], F32))
    ones_bsb = es.enter_context(nc.sbuf_tensor("ones_bsb", [128, 1], BF16))
    ps = es.enter_context(nc.psum_tensor("ps", [128, 8 * 512], F32))
    LD0 = es.enter_context(nc.semaphore("LD0"))
    LD1 = es.enter_context(nc.semaphore("LD1"))
    LW = es.enter_context(nc.semaphore("LW"))
    MM = es.enter_context(nc.semaphore("MM"))
    EV = es.enter_context(nc.semaphore("EV"))
    SQ = es.enter_context(nc.semaphore("SQ"))
    ST0 = es.enter_context(nc.semaphore("ST0"))
    ST1 = es.enter_context(nc.semaphore("ST1"))
    C = Ctr()
    mm_n = ev_n = sq_n = 0
    g.dma_start(out=ones_bsb[:, 0:1], in_=ones128.ap()[:, :]).then_inc(LW, 16)
    C.add(LW, 1)
    sy.dma_start(out=ones_sb[:, 0:1], in_=ones128.ap()[:, :]).then_inc(LD0, 16)
    sy.dma_start(out=ones_sb[0:1, 1:129], in_=ones1x128.ap()[:, :]).then_inc(LD0, 16)
    C.add(LD0, 2)
    for k in range(KD):
        sy.dma_start(out=bvec_sb[:, k:k + 1], in_=bo[k * 128:(k + 1) * 128, :]).then_inc(LD0, 16)
        sy.dma_start(out=bvec_sb[:, 8 + k:9 + k], in_=ln2_g[k * 128:(k + 1) * 128, :]).then_inc(LD0, 16)
        sy.dma_start(out=bvec_sb[:, 16 + k:17 + k], in_=ln2_b[k * 128:(k + 1) * 128, :]).then_inc(LD0, 16)
        sy.dma_start(out=bvec_sb[:, 56 + k:57 + k], in_=b2[k * 128:(k + 1) * 128, :]).then_inc(LD0, 16)
    for m in range(32):
        sy.dma_start(out=bvec_sb[:, 24 + m:25 + m], in_=b1[m * 128:(m + 1) * 128, :]).then_inc(LD0, 16)
    C.add(LD0, 4 * KD + 32)
    for kk in range(KD):
        g.dma_start(out=wo_sb[:, kk * 1024:(kk + 1) * 1024],
                    in_=Wo[kk * 128:(kk + 1) * 128, :]).then_inc(LW, 16)
    C.add(LW, KD)

    def ln_stats2(src_sb, xo):
        nonlocal mm_n, ev_n, sq_n
        o1 = ones_sb[:, 0:1] if src_sb.dtype == F32 else ones_bsb[:, 0:1]
        s.wait_ge(EV, ev_n)
        for k in range(KD):
            li = s.activation(sq_sb[:, k * 512:(k + 1) * 512],
                              src_sb[:, xo + k * 512:xo + (k + 1) * 512], AF.Square)
        li.then_inc(SQ, 1)
        sq_n += 1
        for k in range(KD):
            t.matmul(ps[0:1, 0:512], o1, src_sb[:, xo + k * 512:xo + (k + 1) * 512],
                     start=(k == 0), stop=(k == KD - 1))
        t.wait_ge(SQ, sq_n)
        for k in range(KD):
            li = t.matmul(ps[0:1, 512:1024], ones_sb[:, 0:1], sq_sb[:, k * 512:(k + 1) * 512],
                          start=(k == 0), stop=(k == KD - 1))
        li.then_inc(MM, 1)
        mm_n += 1
        v.wait_ge(MM, mm_n)
        v.tensor_scalar(st_sb[0:1, 0:512], ps[0:1, 0:512], 1.0 / D, None, ALU.mult)
        v.tensor_scalar(st_sb[0:1, 512:1024], ps[0:1, 512:1024], 1.0 / D, None, ALU.mult)
        v.tensor_tensor(st_sb[0:1, 1024:1536], st_sb[0:1, 0:512], st_sb[0:1, 0:512], ALU.mult)
        v.tensor_tensor(st_sb[0:1, 1024:1536], st_sb[0:1, 512:1024], st_sb[0:1, 1024:1536], ALU.subtract).then_inc(EV, 1)
        ev_n += 1
        s.wait_ge(EV, ev_n)
        s.activation(st_sb[0:1, 1024:1536], st_sb[0:1, 1024:1536], AF.Sqrt, bias=pool.ap(1e-5, rows=1)).then_inc(SQ, 1)
        sq_n += 1
        v.wait_ge(SQ, sq_n)
        v.reciprocal(st_sb[0:1, 1024:1536], st_sb[0:1, 1024:1536])
        v.tensor_tensor(st_sb[0:1, 1536:2048], st_sb[0:1, 0:512], st_sb[0:1, 1024:1536], ALU.mult).then_inc(EV, 1)
        ev_n += 1
        t.wait_ge(EV, ev_n)
        t.matmul(ps[:, 2 * 512:3 * 512], ones_sb[0:1, 1:129], st_sb[0:1, 1024:1536], start=True, stop=True)
        li = t.matmul(ps[:, 3 * 512:4 * 512], ones_sb[0:1, 1:129], st_sb[0:1, 1536:2048], start=True, stop=True)
        li.then_inc(MM, 1)
        mm_n += 1
        v.wait_ge(MM, mm_n)

    s.wait_ge(pool.CN, 1)
    mm_n_c2a_gate = {}
    c2b_cons = {}
    t.wait_ge(LW, 16 * C.val(LW))
    t.wait_ge(LD0, 16 * C.val(LD0))
    cons = {}
    for nt in range(8):
        par = nt % 2
        LD = LD0 if par == 0 else LD1
        ST = ST0 if par == 0 else ST1
        xo = par * KD * 512
        tok0 = nt * 512
        if nt >= 2:
            sy.wait_ge(EV, cons[nt - 2])
            sy.wait_ge(LD, 16 * C.val(LD))
        for k in range(KD):
            sy.dma_start(out=a_sb[:, xo + k * 512:xo + (k + 1) * 512],
                         in_=outT[k * 128:(k + 1) * 128, tok0:tok0 + 512]).then_inc(LD, 16)
            sy.dma_start(out=b_sb[:, xo + k * 512:xo + (k + 1) * 512],
                         in_=xn1T[k * 128:(k + 1) * 128, tok0:tok0 + 512]).then_inc(LD, 16)
        C.add(LD, 2 * KD)
        t.wait_ge(LD, 16 * C.val(LD))
        v.wait_ge(LD, 16 * C.val(LD))
        if nt >= 2:
            v.wait_ge(ST, 16 * C.val(ST))
        for m in range(KD):
            bank = 4 + m % 4
            for kk in range(KD):
                li = t.matmul(ps[:, bank * 512:(bank + 1) * 512],
                              wo_sb[:, kk * 1024 + m * 128:kk * 1024 + (m + 1) * 128],
                              a_sb[:, xo + kk * 512:xo + (kk + 1) * 512],
                              start=(kk == 0), stop=(kk == KD - 1))
            li.then_inc(MM, 1)
            mm_n += 1
            v.wait_ge(MM, mm_n)
            v.tensor_scalar(t1_sb[:], ps[:, bank * 512:(bank + 1) * 512],
                            bvec_sb[:, m:m + 1], None, ALU.add)
            li = v.tensor_tensor(x2_sb[:, xo + m * 512:xo + (m + 1) * 512], t1_sb[:],
                                 b_sb[:, xo + m * 512:xo + (m + 1) * 512], ALU.add)
            li.then_inc(EV, 1)
            ev_n += 1
            if m >= 3:
                t.wait_ge(EV, ev_n - 3)
        t.wait_ge(EV, ev_n)
        ln_stats2(x2_sb, xo)
        for k in range(KD):
            v.tensor_tensor(t1_sb[:], x2_sb[:, xo + k * 512:xo + (k + 1) * 512], ps[:, 2 * 512:3 * 512], ALU.mult)
            v.tensor_tensor(t1_sb[:], t1_sb[:], ps[:, 3 * 512:4 * 512], ALU.subtract)
            li = v.tensor_scalar(h_sb[:, xo + k * 512:xo + (k + 1) * 512], t1_sb[:],
                                 bvec_sb[:, 8 + k:9 + k], bvec_sb[:, 16 + k:17 + k], ALU.mult, ALU.add)
        li.then_inc(EV, 1)
        ev_n += 1
        cons[nt] = ev_n
        g.wait_ge(EV, ev_n)
        for k in range(KD):
            g.dma_start(out=x2s.ap()[k * 128:(k + 1) * 128, tok0:tok0 + 512],
                        in_=x2_sb[:, xo + k * 512:xo + (k + 1) * 512]).then_inc(ST, 16)
            g.dma_start(out=hsrc.ap()[k * 128:(k + 1) * 128, tok0:tok0 + 512],
                        in_=h_sb[:, xo + k * 512:xo + (k + 1) * 512]).then_inc(ST, 16)
        C.add(ST, 2 * KD)

    g.wait_ge(MM, mm_n)
    for kk in range(KD):
        g.dma_start(out=w_sb[:, kk * 4096:(kk + 1) * 4096],
                    in_=W1[kk * 128:(kk + 1) * 128, :]).then_inc(LW, 16)
    C.add(LW, KD)
    t.wait_ge(LW, 16 * C.val(LW))
    hg_ev = {}
    for nt in range(8):
        par = nt % 2
        LD = LD0 if par == 0 else LD1
        ST = ST0 if par == 0 else ST1
        xo = par * KD * 512
        tok0 = nt * 512
        sy.wait_ge(ST, 16 * C.val(ST))
        if nt >= 2:
            sy.wait_ge(MM, mm_n_c2a_gate[nt - 2])
            sy.wait_ge(LD, 16 * C.val(LD))
        for k in range(KD):
            sy.dma_start(out=a_sb[:, xo + k * 512:xo + (k + 1) * 512],
                         in_=hsrc.ap()[k * 128:(k + 1) * 128, tok0:tok0 + 512]).then_inc(LD, 16)
        C.add(LD, KD)
        t.wait_ge(LD, 16 * C.val(LD))
        for m in range(32):
            bank = m % 4
            for kk in range(KD):
                li = t.matmul(ps[:, bank * 512:(bank + 1) * 512],
                              w_sb[:, kk * 4096 + m * 128:kk * 4096 + (m + 1) * 128],
                              a_sb[:, xo + kk * 512:xo + (kk + 1) * 512],
                              start=(kk == 0), stop=(kk == KD - 1))
            li.then_inc(MM, 1)
            mm_n += 1
            hpar = m % 2
            s.wait_ge(MM, mm_n)
            if m >= 2:
                s.wait_ge(ST0 if hpar == 0 else ST1, 16 * C.val(ST0 if hpar == 0 else ST1))
            li = s.activation(hg_sb[:, hpar * 512:(hpar + 1) * 512], ps[:, bank * 512:(bank + 1) * 512],
                              AF.Gelu, bias=bvec_sb[:, 24 + m:25 + m])
            li.then_inc(SQ, 1)
            sq_n += 1
            g.wait_ge(SQ, sq_n)
            g.dma_start(out=hgs.ap()[m * 128:(m + 1) * 128, tok0:tok0 + 512],
                        in_=hg_sb[:, hpar * 512:(hpar + 1) * 512]).then_inc(ST0 if hpar == 0 else ST1, 16)
            C.add(ST0 if hpar == 0 else ST1, 1)
            if m >= 3:
                t.wait_ge(SQ, sq_n - 3)
        mm_n_c2a_gate[nt] = mm_n
    g.wait_ge(MM, mm_n)
    g.wait_ge(ST0, 16 * C.val(ST0))
    g.wait_ge(ST1, 16 * C.val(ST1))
    sy.wait_ge(ST0, 16 * C.val(ST0))
    sy.wait_ge(ST1, 16 * C.val(ST1))
    for kk in range(32):
        g.dma_start(out=w_sb[:, kk * 1024:(kk + 1) * 1024],
                    in_=W2[kk * 128:(kk + 1) * 128, :]).then_inc(LW, 16)
    C.add(LW, 32)
    t.wait_ge(LW, 16 * C.val(LW))
    for nt in range(8):
        par = nt % 2
        LD = LD0 if par == 0 else LD1
        ST = ST0 if par == 0 else ST1
        xo = par * KD * 512
        tok0 = nt * 512
        if nt >= 2:
            sy.wait_ge(EV, c2b_cons[nt - 2])
            sy.wait_ge(LD, 16 * C.val(LD))
        for k in range(KD):
            sy.dma_start(out=x2_sb[:, xo + k * 512:xo + (k + 1) * 512],
                         in_=x2s.ap()[k * 128:(k + 1) * 128, tok0:tok0 + 512]).then_inc(LD, 16)
        C.add(LD, KD)
        tile_mm_base = mm_n
        for kk in range(32):
            hp = 0
            if kk >= 1:
                sy.wait_ge(MM, tile_mm_base + kk)
            sy.dma_start(out=b_sb[:, xo + hp * 512:xo + (hp + 1) * 512],
                         in_=hgs.ap()[kk * 128:(kk + 1) * 128, tok0:tok0 + 512]).then_inc(LD, 16)
            C.add(LD, 1)
            t.wait_ge(LD, 16 * C.val(LD))
            for m in range(KD):
                li = t.matmul(ps[:, m * 512:(m + 1) * 512],
                              w_sb[:, kk * 1024 + m * 128:kk * 1024 + (m + 1) * 128],
                              b_sb[:, xo + hp * 512:xo + (hp + 1) * 512],
                              start=(kk == 0), stop=(kk == 31))
            li.then_inc(MM, 1)
            mm_n += 1
        v.wait_ge(MM, mm_n)
        v.wait_ge(LD, 16 * C.val(LD))
        if nt >= 2:
            v.wait_ge(ST, 16 * C.val(ST))
        for m in range(KD):
            v.tensor_scalar(t1_sb[:], ps[:, m * 512:(m + 1) * 512],
                            bvec_sb[:, 56 + m:57 + m], None, ALU.add)
            li = v.tensor_tensor(h_sb[:, xo + m * 512:xo + (m + 1) * 512], t1_sb[:],
                                 x2_sb[:, xo + m * 512:xo + (m + 1) * 512], ALU.add)
        li.then_inc(EV, 1)
        ev_n += 1
        c2b_cons[nt] = ev_n
        t.wait_ge(EV, ev_n)
        g.wait_ge(EV, ev_n)
        for k in range(KD):
            g.dma_start(out=s1T[k * 128:(k + 1) * 128, tok0:tok0 + 512],
                        in_=h_sb[:, xo + k * 512:xo + (k + 1) * 512]).then_inc(ST, 16)
        C.add(ST, KD)
    g.wait_ge(ST0, 16 * C.val(ST0))
    g.wait_ge(ST1, 16 * C.val(ST1))
    es.close()
    return nc



CAP = 768
NBIS = 40
L10_PHASE = 3
L10_NEXP = 8
L10_P2CUT = 5

def build_l10():
    nc = mknc()
    probsA = nc.dram_tensor("probsA", [8, 32768], F32, kind="ExternalInput").ap()
    probsO = nc.dram_tensor("probsO", [8, 4096], F32, kind="ExternalInput").ap()
    xn1tm = nc.dram_tensor("xn1tm", [4096, 1024], BF16, kind="ExternalInput").ap()
    xn1T = nc.dram_tensor("xn1T", [D, S], BF16, kind="ExternalInput").ap()
    x1T = nc.dram_tensor("x1T", [D, S], F32, kind="ExternalInput").ap()
    s1T = nc.dram_tensor("s1T", [D, S], BF16, kind="ExternalInput").ap()
    W1e = nc.dram_tensor("W1e", [8, 1024, 4096], F32, kind="ExternalInput").ap()
    W2e = nc.dram_tensor("W2e", [8, 4096, 1024], F32, kind="ExternalInput").ap()
    b1et = nc.dram_tensor("b1et", [8, 128, 32], F32, kind="ExternalInput").ap()
    b2et = nc.dram_tensor("b2et", [8, 128, 8], F32, kind="ExternalInput").ap()
    Ws1 = nc.dram_tensor("Ws1", [D, 4096], F32, kind="ExternalInput").ap()
    bs1 = nc.dram_tensor("bs1", [4096, 1], F32, kind="ExternalInput").ap()
    Ws2 = nc.dram_tensor("Ws2", [4096, D], F32, kind="ExternalInput").ap()
    bs2 = nc.dram_tensor("bs2", [D, 1], F32, kind="ExternalInput").ap()
    outF = nc.dram_tensor("outF", [D, S], F32, kind="ExternalOutput").ap()
    dbg = nc.dram_tensor("dbg", [8 * CAP, 2], F32, kind="ExternalOutput").ap()
    rdbg = nc.dram_tensor("rdbg", [4097, 1024], F32, kind="ExternalOutput").ap()
    routed = nc.dram_tensor("routed", [4097, 1024], F32, kind="Internal")
    compact = nc.dram_tensor("compact", [8 * CAP, 2], F32, kind="Internal")
    hgs2 = nc.dram_tensor("hgs2", [4096, S], BF16, kind="Internal")

    esel8 = np.zeros((8, 128), np.float32)
    for e in range(8):
        esel8[e, e * 16:(e + 1) * 16] = 1.0
    eselT = np.ascontiguousarray(esel8.T)
    eye8 = np.eye(8, dtype=np.float32)
    on8 = np.ones((8, 128), np.float32)
    ltri = np.tril(np.ones((128, 128), np.float32), -1)
    ident = np.eye(128, dtype=np.float32)
    esel8_t = nc.inline_tensor(esel8, name="esel8")
    eselT_t = nc.inline_tensor(eselT, name="eselT")
    eye8_t = nc.inline_tensor(eye8, name="eye8")
    on8_t = nc.inline_tensor(on8, name="on8")
    ltri_t = nc.inline_tensor(ltri, name="ltri")
    idb_t = nc.inline_tensor(ident, name="idb")
    ones128 = nc.inline_tensor(np.ones((128, 1), np.float32), name="ones128")
    ones1x128 = nc.inline_tensor(np.ones((1, 128), np.float32), name="ones1x128")

    g, t, v, s, sy = nc.gpsimd, nc.tensor, nc.vector, nc.scalar, nc.sync
    es = ExitStack()
    pool = ConstPool(nc, es, [1e-5])
    wa_sb = es.enter_context(nc.sbuf_tensor("wa_sb", [128, 8 * 4096], BF16))
    wb_sb = es.enter_context(nc.sbuf_tensor("wb_sb", [128, 32 * 1024], BF16))
    tokf_sb = es.enter_context(nc.sbuf_tensor("tokf_sb", [128, 8 * CAP], BF16))
    hq_sb = es.enter_context(nc.sbuf_tensor("hq_sb", [128, 32 * 192], BF16))
    otf_sb = es.enter_context(nc.sbuf_tensor("otf_sb", [128, 8 * CAP], BF16))
    pa_sb = es.enter_context(nc.sbuf_tensor("pa_sb", [128, 2048], F32))
    x1_sb = es.enter_context(nc.sbuf_tensor("x1_sb", [128, 4 * 512], F32))
    a_sb = es.enter_context(nc.sbuf_tensor("a_sb", [128, KD * 512], BF16))
    otm_sb = es.enter_context(nc.sbuf_tensor("otm_sb", [128, 1024], F32))
    tokr_sb = es.enter_context(nc.sbuf_tensor("tokr_sb", [128, 1024], BF16))
    zz_sb = es.enter_context(nc.sbuf_tensor("zz_sb", [128, 256], F32))
    hgp_sb = es.enter_context(nc.sbuf_tensor("hgp_sb", [128, 2 * 512], BF16))
    t1_sb = es.enter_context(nc.sbuf_tensor("t1_sb", [128, 512], F32))
    t2_sb = es.enter_context(nc.sbuf_tensor("t2_sb", [128, 512], F32))
    po_sb = es.enter_context(nc.sbuf_tensor("po_sb", [128, 8 * 32], F32))
    bis_sb = es.enter_context(nc.sbuf_tensor("bis_sb", [8, 16], F32))
    thbc_sb = es.enter_context(nc.sbuf_tensor("thbc_sb", [128, 4], F32))
    thall_sb = es.enter_context(nc.sbuf_tensor("thall_sb", [128, 8], F32))
    sel_sb = es.enter_context(nc.sbuf_tensor("sel_sb", [128, 32], F32))
    cum_sb = es.enter_context(nc.sbuf_tensor("cum_sb", [128, 72], F32))
    iota_sb = es.enter_context(nc.sbuf_tensor("iota_sb", [128, 32], F32))
    pay_sb = es.enter_context(nc.sbuf_tensor("pay_sb", [128, 64], F32))
    fil_sb = es.enter_context(nc.sbuf_tensor("fil_sb", [128, 96], F32))
    pos_sb = es.enter_context(nc.sbuf_tensor("pos_sb", [128, 32], I32))
    ones_sb = es.enter_context(nc.sbuf_tensor("ones_sb", [128, 129], F32))
    lt_sb = es.enter_context(nc.sbuf_tensor("lt_sb", [128, 128], F32))
    es8_sb = es.enter_context(nc.sbuf_tensor("es8_sb", [8, 264], F32))
    esT_sb = es.enter_context(nc.sbuf_tensor("esT_sb", [128, 8], F32))
    idb_sb = es.enter_context(nc.sbuf_tensor("idb_sb", [128, 128], BF16))
    idf_sb = es.enter_context(nc.sbuf_tensor("idf_sb", [128, 128], F32))
    cidx_sb = es.enter_context(nc.sbuf_tensor("cidx_sb", [128, 12], F32))
    idxi_sb = es.enter_context(nc.sbuf_tensor("idxi_sb", [128, 6], I32))
    bv_sb = es.enter_context(nc.sbuf_tensor("bv_sb", [128, 64], F32))
    rf_sb = pa_sb[:, :].bitcast(BF16)
    ps = es.enter_context(nc.psum_tensor("ps", [128, 7 * 512], F32))
    psb = es.enter_context(nc.psum_tensor("psb", [128, 1024], BF16))
    LD = es.enter_context(nc.semaphore("LD"))
    LD0 = es.enter_context(nc.semaphore("LD0"))
    LD1 = es.enter_context(nc.semaphore("LD1"))
    LW0 = es.enter_context(nc.semaphore("LW0"))
    LW1 = es.enter_context(nc.semaphore("LW1"))
    MM = es.enter_context(nc.semaphore("MM"))
    EV = es.enter_context(nc.semaphore("EV"))
    SQ = es.enter_context(nc.semaphore("SQ"))
    SC = es.enter_context(nc.semaphore("SC"))
    ST0 = es.enter_context(nc.semaphore("ST0"))
    ST1 = es.enter_context(nc.semaphore("ST1"))
    C = Ctr()
    mm_n = ev_n = sq_n = 0

    sy.dma_start(out=ones_sb[:, 0:1], in_=ones128.ap()[:, :]).then_inc(LD, 16)
    sy.dma_start(out=ones_sb[0:1, 1:129], in_=ones1x128.ap()[:, :]).then_inc(LD, 16)
    sy.dma_start(out=lt_sb[:, :], in_=ltri_t.ap()[:, :]).then_inc(LD, 16)
    sy.dma_start(out=es8_sb[0:8, 0:128], in_=esel8_t.ap()[:, :]).then_inc(LD, 16)
    sy.dma_start(out=es8_sb[0:8, 128:136], in_=eye8_t.ap()[:, :]).then_inc(LD, 16)
    sy.dma_start(out=es8_sb[0:8, 136:264], in_=on8_t.ap()[:, :]).then_inc(LD, 16)
    sy.dma_start(out=esT_sb[:, :], in_=eselT_t.ap()[:, :]).then_inc(LD, 16)
    g.dma_start(out=idb_sb[:, :], in_=idb_t.ap()[:, :]).then_inc(SC, 16)
    sy.dma_start(out=idf_sb[:, :], in_=idb_t.ap()[:, :]).then_inc(LD, 16)
    for e in range(8):
        sy.dma_start(out=pa_sb[e * 16:(e + 1) * 16, :],
                     in_=probsA[e:e + 1, :].rearrange("one (tc j) -> (one tc) j", tc=16)).then_inc(LD, 16)
        sy.dma_start(out=po_sb[:, e * 32:(e + 1) * 32],
                     in_=probsO[e:e + 1, :].rearrange("one (p j) -> (one p) j", p=128)).then_inc(LD, 16)
    C.add(LD, 8 + 16)
    C.add(SC, 1)
    g.iota(iota_sb[:, :], pattern=[[1, 32]], base=0, channel_multiplier=32,
           allow_small_or_imprecise_dtypes=True).then_inc(EV, 1)
    ev_n += 1

    def vch(op):
        nonlocal ev_n
        op.then_inc(EV, 1)
        ev_n += 1
        v.wait_ge(EV, ev_n)

    v.memset(bis_sb[0:8, 0:1], 0.0)
    v.memset(bis_sb[0:8, 1:2], 1.0)
    vch(v.memset(bis_sb[0:8, 2:3], 0.5))
    v.wait_ge(LD, 16 * C.val(LD))
    s.wait_ge(LD, 16 * C.val(LD))
    t.wait_ge(LD, 16 * C.val(LD))
    s.wait_ge(pool.CN, 1)
    for it in range(NBIS):
        vch(v.tensor_copy(bis_sb[0:8, 4:8], bis_sb[0:8, 2:3].to_broadcast([8, 4])))
        t.wait_ge(EV, ev_n)
        t.matmul(ps[:, 0:4], es8_sb[0:8, 0:128], bis_sb[0:8, 4:8], start=True, stop=True).then_inc(MM, 1)
        mm_n += 1
        v.wait_ge(MM, mm_n)
        vch(v.tensor_scalar(thbc_sb[:, 0:1], ps[:, 0:1], -1.0, None, ALU.mult))
        s.wait_ge(EV, ev_n)
        s.activation(otf_sb[:, 0:2048], pa_sb[:, :], AF.Sign, bias=thbc_sb[:, 0:1],
                     accum_out=thbc_sb[:, 1:2]).then_inc(SQ, 1)
        sq_n += 1
        t.wait_ge(SQ, sq_n)
        t.matmul(ps[0:8, 512:520], esT_sb[:, :], thbc_sb[:, 1:2].to_broadcast([128, 8]),
                 start=True, stop=True).then_inc(MM, 1)
        mm_n += 1
        v.wait_ge(MM, mm_n)
        vch(v.tensor_scalar(bis_sb[0:8, 3:4], ps[0:8, 512:513], 0.5, 16384.0, ALU.mult, ALU.add))
        vch(v.tensor_single_scalar(bis_sb[0:8, 4:5], bis_sb[0:8, 3:4], 4096.0, ALU.is_ge))
        vch(v.tensor_tensor(bis_sb[0:8, 5:6], bis_sb[0:8, 2:3], bis_sb[0:8, 0:1], ALU.subtract))
        vch(v.tensor_tensor(bis_sb[0:8, 5:6], bis_sb[0:8, 5:6], bis_sb[0:8, 4:5], ALU.mult))
        vch(v.tensor_tensor(bis_sb[0:8, 0:1], bis_sb[0:8, 0:1], bis_sb[0:8, 5:6], ALU.add))
        vch(v.tensor_tensor(bis_sb[0:8, 6:7], bis_sb[0:8, 2:3], bis_sb[0:8, 1:2], ALU.subtract))
        vch(v.tensor_tensor(bis_sb[0:8, 7:8], bis_sb[0:8, 6:7], bis_sb[0:8, 4:5], ALU.mult))
        vch(v.tensor_tensor(bis_sb[0:8, 6:7], bis_sb[0:8, 6:7], bis_sb[0:8, 7:8], ALU.subtract))
        vch(v.tensor_tensor(bis_sb[0:8, 1:2], bis_sb[0:8, 1:2], bis_sb[0:8, 6:7], ALU.add))
        vch(v.tensor_tensor(bis_sb[0:8, 2:3], bis_sb[0:8, 0:1], bis_sb[0:8, 1:2], ALU.add))
        vch(v.tensor_scalar(bis_sb[0:8, 2:3], bis_sb[0:8, 2:3], 0.5, None, ALU.mult))
    vch(v.tensor_scalar(bis_sb[0:8, 8:16], es8_sb[0:8, 128:136], bis_sb[0:8, 0:1], None, ALU.mult))
    t.wait_ge(EV, ev_n)
    t.matmul(ps[:, 0:8], es8_sb[0:8, 136:264], bis_sb[0:8, 8:16], start=True, stop=True).then_inc(MM, 1)
    mm_n += 1
    v.wait_ge(MM, mm_n)
    vch(v.tensor_copy(thall_sb[:, :], ps[:, 0:8]))

    v.memset(zz_sb[:, :], 0.0)
    v.memset(fil_sb[:, :], 0.0)
    vch(v.memset(fil_sb[:, 0:96:2], 4096.0))
    g.wait_ge(EV, ev_n)
    g.dma_start(out=compact.ap()[:, :].rearrange("(p j) two -> p (j two)", p=128),
                in_=fil_sb[:, 0:96]).then_inc(SC, 16)
    C.add(SC, 1)
    for r0 in range(0, 4097, 128):
        rn = min(128, 4097 - r0)
        for h2 in range(4):
            g.dma_start(out=routed.ap()[r0:r0 + rn, h2 * 256:(h2 + 1) * 256],
                        in_=zz_sb[0:rn, :]).then_inc(SC, 16)
            C.add(SC, 1)
    bndreg = g.alloc_register("bndreg")
    for e in range(8):
        vch(v.tensor_scalar(sel_sb[:, :], po_sb[:, e * 32:(e + 1) * 32], thall_sb[:, e:e + 1], None, ALU.is_gt))
        vch(v.tensor_tensor_scan(cum_sb[:, 0:32], ones_sb[:, 0:1].to_broadcast([128, 32]),
                                 sel_sb[:, :], 0.0, ALU.mult, ALU.add))
        vch(v.reduce_sum(cum_sb[:, 32:33], sel_sb[:, :], axis=AXX))
        t.wait_ge(EV, ev_n)
        t.matmul(ps[:, 0:4], lt_sb[:, :], cum_sb[:, 32:33].to_broadcast([128, 4]),
                 start=True, stop=True).then_inc(MM, 1)
        mm_n += 1
        v.wait_ge(MM, mm_n)
        vch(v.tensor_scalar(cum_sb[:, 0:32], cum_sb[:, 0:32], ps[:, 0:1], None, ALU.add))
        vch(v.tensor_scalar(cum_sb[:, 0:32], cum_sb[:, 0:32], float(e * CAP - 1), None, ALU.add))
        vch(v.tensor_tensor(cum_sb[:, 0:32], cum_sb[:, 0:32], sel_sb[:, :], ALU.mult))
        vch(v.tensor_scalar(cum_sb[:, 34:66], sel_sb[:, :], -1e9, 1e9, ALU.mult, ALU.add))
        vch(v.tensor_tensor(cum_sb[:, 0:32], cum_sb[:, 0:32], cum_sb[:, 34:66], ALU.add))
        vch(v.tensor_copy(pos_sb[:, :], cum_sb[:, 0:32]))
        vch(v.tensor_copy(pay_sb[:, 0:64:2], iota_sb[:, :]))
        vch(v.tensor_copy(pay_sb[:, 1:64:2], po_sb[:, e * 32:(e + 1) * 32]))
        g.wait_ge(EV, ev_n)
        g.wait_ge(SC, 16 * C.val(SC))
        g.reg_mov(bndreg, (e + 1) * CAP - 1)
        for j in range(32):
            g.indirect_dma_start(compact.ap()[:, :],
                                 IndirectOffsetOnAxis(pos_sb[:, j:j + 1], 0),
                                 pay_sb[:, 2 * j:2 * j + 2], None,
                                 bounds_check=bndreg, oob_is_err=False).then_inc(SC, 16)
        C.add(SC, 32)
        v.wait_ge(SC, 16 * C.val(SC))
    g.dma_start(out=dbg[:, :].rearrange("(p j) two -> p (j two)", p=128),
                in_=compact.ap()[:, :].rearrange("(p j) two -> p (j two)", p=128)).then_inc(SC, 16)
    C.add(SC, 1)

    if L10_PHASE < 2:
        g.wait_ge(SC, 16 * C.val(SC))
        v.wait_ge(EV, ev_n)
        for nt in range(8):
            for k in range(KD):
                for h4 in range(2):
                    g.dma_start(out=outF[k * 128:(k + 1) * 128, nt * 512 + h4 * 256:nt * 512 + (h4 + 1) * 256],
                                in_=zz_sb[:, :]).then_inc(ST0, 16)
                    C.add(ST0, 1)
        g.wait_ge(ST0, 16 * C.val(ST0))
        es.close()
        return nc

    bndreg2 = g.alloc_register("bndreg2")
    g.reg_mov(bndreg2, 4095)
    MME = {}
    for e in range(L10_NEXP if L10_PHASE >= 2 else 0):
        if e > 0:
            g.wait_ge(MM, MME[e - 1])
        for kk in range(KD):
            g.dma_start(out=wa_sb[:, kk * 4096:(kk + 1) * 4096],
                        in_=W1e[e, kk * 128:(kk + 1) * 128, :]).then_inc(LW0, 16)
        C.add(LW0, KD)
        for kk in range(32):
            g.dma_start(out=wb_sb[:, kk * 1024:(kk + 1) * 1024],
                        in_=W2e[e, kk * 128:(kk + 1) * 128, :]).then_inc(LW1, 16)
        C.add(LW1, 32)
        g.dma_start(out=bv_sb[:, 0:32], in_=b1et[e, :, :]).then_inc(LW0, 16)
        g.dma_start(out=bv_sb[:, 32:40], in_=b2et[e, :, :]).then_inc(LW0, 16)
        C.add(LW0, 2)
        sy.wait_ge(SC, 16 * C.val(SC))
        sy.dma_start(out=cidx_sb[:, :].rearrange("p (gr two) -> p gr two", two=2),
                     in_=compact.ap()[e * CAP:(e + 1) * CAP, :].rearrange("(gr p) two -> p gr two", p=128)).then_inc(LD0, 16)
        C.add(LD0, 1)
        v.wait_ge(LD0, 16 * C.val(LD0))
        if e > 0:
            v.wait_ge(SC, 16 * C.val(SC))
        vch(v.tensor_copy(idxi_sb[:, :], cidx_sb[:, 0:12:2]))
        if L10_P2CUT < 2:
            MME[e] = mm_n
            continue
        g.wait_ge(EV, ev_n)
        base_mm = mm_n
        base_sc = C.val(SC)
        base_ev = ev_n
        for gr in range(6):
            if gr >= 1:
                g.wait_ge(MM, base_mm + gr)
            g.indirect_dma_start(tokr_sb[:, :], None,
                                 xn1tm[:, :],
                                 IndirectOffsetOnAxis(idxi_sb[:, gr:gr + 1], 0),
                                 bounds_check=bndreg2, oob_is_err=False).then_inc(SC, 16)
            C.add(SC, 1)
            t.wait_ge(SC, 16 * (base_sc + gr + 1))
            if gr >= 1:
                t.wait_ge(EV, base_ev + gr)
            for kk in range(KD):
                li = t.transpose(psb[:, kk * 128:(kk + 1) * 128],
                                 tokr_sb[:, kk * 128:(kk + 1) * 128], idb_sb[:, :])
            li.then_inc(MM, 1)
            mm_n += 1
            v.wait_ge(MM, mm_n)
            li = v.tensor_copy(
                tokf_sb[:, 0:8 * CAP].rearrange("p (kk r) -> p kk r", kk=KD)[:, :, gr * 128:(gr + 1) * 128],
                psb[:, :].rearrange("p (kk r) -> p kk r", kk=KD))
            li.then_inc(EV, 1)
            ev_n += 1
        if L10_P2CUT < 3:
            MME[e] = mm_n
            continue
        t.wait_ge(EV, ev_n)
        t.wait_ge(LW0, 16 * C.val(LW0))
        t.wait_ge(LW1, 16 * C.val(LW1))
        s.wait_ge(SQ, sq_n)
        for q in range(4):
            q0 = q * 192
            sq_base = sq_n
            for m in range(32):
                bank = 3 + m % 4
                if m >= 4:
                    t.wait_ge(SQ, sq_base + m - 3)
                for kk in range(KD):
                    li = t.matmul(ps[:, bank * 512:bank * 512 + 192],
                                  wa_sb[:, kk * 4096 + m * 128:kk * 4096 + (m + 1) * 128],
                                  tokf_sb[:, kk * CAP + q0:kk * CAP + q0 + 192],
                                  start=(kk == 0), stop=(kk == KD - 1))
                li.then_inc(MM, 1)
                mm_n += 1
                s.wait_ge(MM, mm_n)
                li = s.activation(hq_sb[:, m * 192:(m + 1) * 192], ps[:, bank * 512:bank * 512 + 192],
                                  AF.Gelu_apprx_tanh, bias=bv_sb[:, m:m + 1])
                li.then_inc(SQ, 1)
                sq_n += 1
            if L10_P2CUT < 4:
                continue
            t.wait_ge(SQ, sq_n)
            for mh in range(2):
                if mh == 1:
                    t.wait_ge(EV, ev_n)
                for kk in range(32):
                    for m4 in range(4):
                        m2 = mh * 4 + m4
                        li = t.matmul(ps[:, m4 * 512:m4 * 512 + 192],
                                      wb_sb[:, kk * 1024 + m2 * 128:kk * 1024 + (m2 + 1) * 128],
                                      hq_sb[:, kk * 192:(kk + 1) * 192],
                                      start=(kk == 0), stop=(kk == 31))
                li.then_inc(MM, 1)
                mm_n += 1
                v.wait_ge(MM, mm_n)
                for m4 in range(4):
                    m2 = mh * 4 + m4
                    li = v.tensor_scalar(otf_sb[:, m2 * CAP + q0:m2 * CAP + q0 + 192],
                                         ps[:, m4 * 512:m4 * 512 + 192],
                                         bv_sb[:, 32 + m2:33 + m2], None, ALU.add)
                li.then_inc(EV, 1)
                ev_n += 1
            t.wait_ge(EV, ev_n)
        for gr in range(6 if L10_P2CUT >= 5 else 0):
            op = gr % 2
            for m2 in range(KD):
                li = t.transpose(psb[:, m2 * 128:(m2 + 1) * 128],
                                 otf_sb[:, m2 * CAP + gr * 128:m2 * CAP + (gr + 1) * 128], idb_sb[:, :])
            li.then_inc(MM, 1)
            mm_n += 1
            v.wait_ge(MM, mm_n)
            v.wait_ge(SC, 16 * C.val(SC))
            li = v.tensor_scalar(otm_sb[:, 0:1024], psb[:, :],
                                 cidx_sb[:, 2 * gr + 1:2 * gr + 2], None, ALU.mult)
            li.then_inc(EV, 1)
            ev_n += 1
            t.wait_ge(EV, ev_n)
            g.wait_ge(EV, ev_n)
            g.wait_ge(SC, 16 * C.val(SC))
            g.indirect_dma_start(routed.ap()[:, :],
                                 IndirectOffsetOnAxis(idxi_sb[:, gr:gr + 1], 0),
                                 otm_sb[:, 0:1024], None,
                                 compute_op=ALU.add).then_inc(SC, 16)
            C.add(SC, 1)
        MME[e] = mm_n

    if L10_PHASE < 3:
        g.wait_ge(SC, 16 * C.val(SC))
        g.wait_ge(MM, MME[L10_NEXP - 1])
        v.wait_ge(EV, ev_n)
        s.wait_ge(SQ, sq_n)
        for nt in range(8):
            for k in range(KD):
                for h4 in range(2):
                    g.dma_start(out=outF[k * 128:(k + 1) * 128, nt * 512 + h4 * 256:nt * 512 + (h4 + 1) * 256],
                                in_=zz_sb[:, :]).then_inc(ST0, 16)
                    C.add(ST0, 1)
        g.wait_ge(ST0, 16 * C.val(ST0))
        es.close()
        return nc

    g.wait_ge(MM, MME[L10_NEXP - 1])
    for kk in range(KD):
        g.dma_start(out=wa_sb[:, kk * 4096:(kk + 1) * 4096],
                    in_=Ws1[kk * 128:(kk + 1) * 128, :]).then_inc(LW0, 16)
    C.add(LW0, KD)
    sy.wait_ge(SQ, sq_n)
    sy.wait_ge(EV, ev_n)
    for m in range(32):
        sy.dma_start(out=bv_sb[:, m:m + 1], in_=bs1[m * 128:(m + 1) * 128, :]).then_inc(LD0, 16)
    for k in range(KD):
        sy.dma_start(out=bv_sb[:, 32 + k:33 + k], in_=bs2[k * 128:(k + 1) * 128, :]).then_inc(LD0, 16)
    C.add(LD0, 40)
    t.wait_ge(LW0, 16 * C.val(LW0))
    t.wait_ge(LD0, 16 * C.val(LD0))
    t.wait_ge(EV, ev_n)
    s.wait_ge(LD0, 16 * C.val(LD0))
    STS = {0: ST0, 1: ST1}
    for nt in range(8):
        par = nt % 2
        LDp = LD0 if par == 0 else LD1
        tok0 = nt * 512
        xo = par * KD * 512
        if nt >= 1:
            sy.wait_ge(MM, MME[L10_NEXP - 1] + nt * 32)
            sy.wait_ge(LDp, 16 * C.val(LDp))
        for k in range(KD):
            sy.dma_start(out=a_sb[:, k * 512:(k + 1) * 512],
                         in_=xn1T[k * 128:(k + 1) * 128, tok0:tok0 + 512]).then_inc(LDp, 16)
        C.add(LDp, KD)
        t.wait_ge(LDp, 16 * C.val(LDp))
        sq_base = sq_n
        for m in range(32):
            bank = m % 4
            if m >= 4:
                t.wait_ge(SQ, sq_base + m - 3)
            for kk in range(KD):
                li = t.matmul(ps[:, bank * 512:(bank + 1) * 512],
                              wa_sb[:, kk * 4096 + m * 128:kk * 4096 + (m + 1) * 128],
                              a_sb[:, kk * 512:(kk + 1) * 512],
                              start=(kk == 0), stop=(kk == KD - 1))
            li.then_inc(MM, 1)
            mm_n += 1
            hpar = m % 2
            STp = STS[hpar]
            s.wait_ge(MM, mm_n)
            if m >= 2:
                s.wait_ge(STp, 16 * C.val(STp))
            li = s.activation(hgp_sb[:, hpar * 512:(hpar + 1) * 512], ps[:, bank * 512:(bank + 1) * 512],
                              AF.Gelu_apprx_tanh, bias=bv_sb[:, m:m + 1])
            li.then_inc(SQ, 1)
            sq_n += 1
            g.wait_ge(SQ, sq_n)
            g.dma_start(out=hgs2.ap()[m * 128:(m + 1) * 128, tok0:tok0 + 512],
                        in_=hgp_sb[:, hpar * 512:(hpar + 1) * 512]).then_inc(STp, 16)
            C.add(STp, 1)
    g.wait_ge(MM, mm_n)
    g.wait_ge(ST0, 16 * C.val(ST0))
    g.wait_ge(ST1, 16 * C.val(ST1))
    sy.wait_ge(ST0, 16 * C.val(ST0))
    sy.wait_ge(ST1, 16 * C.val(ST1))
    g.wait_ge(SC, 16 * C.val(SC))
    sy.wait_ge(SC, 16 * C.val(SC))
    for r0 in range(0, 4097, 128):
        rn = min(128, 4097 - r0)
        sy.dma_start(out=rdbg[r0:r0 + rn, :], in_=routed.ap()[r0:r0 + rn, :]).then_inc(LD, 16)
        C.add(LD, 1)
    for kk in range(32):
        g.dma_start(out=wb_sb[:, kk * 1024:(kk + 1) * 1024],
                    in_=Ws2[kk * 128:(kk + 1) * 128, :]).then_inc(LW1, 16)
    C.add(LW1, 32)
    t.wait_ge(LW1, 16 * C.val(LW1))
    P3EV = {}
    for nt in range(8):
        par = nt % 2
        LDp = LD0 if par == 0 else LD1
        STp = STS[par]
        tok0 = nt * 512
        xo = par * KD * 512
        if nt >= 1:
            sy.wait_ge(EV, P3EV[nt - 1])
            sy.wait_ge(LDp, 16 * C.val(LDp))
        for k in range(KD):
            sy.dma_start(out=a_sb[:, k * 512:(k + 1) * 512],
                         in_=s1T[k * 128:(k + 1) * 128, tok0:tok0 + 512]).then_inc(LDp, 16)
        C.add(LDp, KD)
        for tg in range(4):
            if tg >= 1:
                g.wait_ge(MM, mm_n)
            g.dma_start(out=otm_sb[:, 0:1024],
                        in_=routed.ap()[tok0 + tg * 128:tok0 + (tg + 1) * 128, :]).then_inc(LDp, 16)
            C.add(LDp, 1)
            t.wait_ge(LDp, 16 * C.val(LDp))
            if tg >= 1:
                t.wait_ge(EV, ev_n)
            for k2 in range(KD):
                bank = 4 + k2 % 2
                li = t.transpose(ps[:, bank * 512:bank * 512 + 128],
                                 otm_sb[:, k2 * 128:(k2 + 1) * 128],
                                 idf_sb[:, :])
                li.then_inc(MM, 1)
                mm_n += 1
                v.wait_ge(MM, mm_n)
                li2 = v.tensor_copy(rf_sb[:, k2 * 512 + tg * 128:k2 * 512 + (tg + 1) * 128],
                                    ps[:, bank * 512:bank * 512 + 128])
                li2.then_inc(EV, 1)
                ev_n += 1
                t.wait_ge(EV, ev_n)
        for mg in range(2):
            sy.wait_ge(EV, ev_n)
            for m4 in range(4):
                m2 = mg * 4 + m4
                sy.dma_start(out=x1_sb[:, m4 * 512:(m4 + 1) * 512],
                             in_=x1T[m2 * 128:(m2 + 1) * 128, tok0:tok0 + 512]).then_inc(LD0, 16)
            C.add(LD0, 4)
            base_mm2 = mm_n
            for kk in range(32):
                hp = kk % 2
                if kk >= 2:
                    g.wait_ge(MM, base_mm2 + kk - 1)
                g.dma_start(out=hgp_sb[:, hp * 512:(hp + 1) * 512],
                            in_=hgs2.ap()[kk * 128:(kk + 1) * 128, tok0:tok0 + 512]).then_inc(LW0, 16)
                C.add(LW0, 1)
                t.wait_ge(LW0, 16 * C.val(LW0))
                for m4 in range(4):
                    m2 = mg * 4 + m4
                    li = t.matmul(ps[:, m4 * 512:(m4 + 1) * 512],
                                  wb_sb[:, kk * 1024 + m2 * 128:kk * 1024 + (m2 + 1) * 128],
                                  hgp_sb[:, hp * 512:(hp + 1) * 512],
                                  start=(kk == 0), stop=(kk == 31))
                li.then_inc(MM, 1)
                mm_n += 1
            v.wait_ge(MM, mm_n)
            v.wait_ge(LDp, 16 * C.val(LDp))
            v.wait_ge(LD0, 16 * C.val(LD0))
            if nt >= 2:
                v.wait_ge(STp, 16 * C.val(STp))
            for m4 in range(4):
                m2 = mg * 4 + m4
                vch(v.tensor_scalar(t1_sb[:], ps[:, m4 * 512:(m4 + 1) * 512],
                                    bv_sb[:, 32 + m2:33 + m2], None, ALU.add))
                vch(v.tensor_tensor(t1_sb[:], t1_sb[:], rf_sb[:, m2 * 512:(m2 + 1) * 512], ALU.add))
                s.wait_ge(EV, ev_n)
                s.activation(t2_sb[:], t1_sb[:], AF.Silu).then_inc(SQ, 1)
                sq_n += 1
                v.wait_ge(SQ, sq_n)
                vch(v.tensor_tensor(t2_sb[:], t2_sb[:], a_sb[:, m2 * 512:(m2 + 1) * 512], ALU.mult))
                li = v.tensor_tensor(x1_sb[:, m4 * 512:(m4 + 1) * 512], t2_sb[:],
                                     x1_sb[:, m4 * 512:(m4 + 1) * 512], ALU.add)
                li.then_inc(EV, 1)
                ev_n += 1
                t.wait_ge(EV, ev_n)
                g.wait_ge(EV, ev_n)
                g.dma_start(out=outF[m2 * 128:(m2 + 1) * 128, tok0:tok0 + 512],
                            in_=x1_sb[:, m4 * 512:(m4 + 1) * 512]).then_inc(STp, 16)
                C.add(STp, 1)
        P3EV[nt] = ev_n
    g.wait_ge(ST0, 16 * C.val(ST0))
    g.wait_ge(ST1, 16 * C.val(ST1))
    es.close()
    return nc



def _rot_tables():
    DH, WINDOW = 64, 256
    half = np.arange(0, DH, 2, dtype=np.float64)
    inv_freq = 1.0 / 10000.0 ** (half / DH)
    t = np.arange(S, dtype=np.float64)
    freqs = t[:, None] * inv_freq[None, :]
    freqs = np.concatenate([freqs, freqs], -1)
    base_scale = (half + 0.4 * DH) / (1.4 * DH)
    power = (t - S // 2) / (WINDOW // 2)
    scale = base_scale[None, :] ** power[:, None]
    scale = np.concatenate([scale, scale], -1)
    cs = (np.cos(freqs) * scale).astype(np.float32)
    ss = (np.sin(freqs) * scale).astype(np.float32)
    ci = (np.cos(freqs) / scale).astype(np.float32)
    si = (np.sin(freqs) / scale).astype(np.float32)
    out = []
    for a in (cs, ss, ci, si):
        fm = np.ascontiguousarray(a.T)
        out.append(np.ascontiguousarray(
            np.concatenate([fm, fm], axis=0)).astype(ml_dtypes.bfloat16))
    return out


LFA_SERIAL = False

def build_lfa():
    nc = mknc()
    xfT = nc.dram_tensor("xfT", [D, S], BF16, kind="ExternalInput").ap()
    Wqkv = nc.dram_tensor("Wqkv", [D, 3072], F32, kind="ExternalInput").ap()
    qs = nc.dram_tensor("qs", [64, 1], F32, kind="ExternalInput").ap()
    ks = nc.dram_tensor("ks", [64, 1], F32, kind="ExternalInput").ap()
    qrT = nc.dram_tensor("qrT", [D, S], BF16, kind="ExternalOutput").ap()
    krT = nc.dram_tensor("krT", [D, S], BF16, kind="ExternalOutput").ap()
    vt = nc.dram_tensor("vt", [S, D], BF16, kind="ExternalOutput").ap()

    cs, ss, ci, si = _rot_tables()
    cs_t = nc.inline_tensor(cs, name="cs_t")
    ss_t = nc.inline_tensor(ss, name="ss_t")
    ci_t = nc.inline_tensor(ci, name="ci_t")
    si_t = nc.inline_tensor(si, name="si_t")
    P = np.zeros((128, 128), np.float32)
    for b in (0, 64):
        for i in range(32):
            P[b + i + 32, b + i] = -1.0
            P[b + i, b + i + 32] = 1.0
    P_t = nc.inline_tensor(P, name="Prot")
    hsel2 = np.zeros((128, 4), np.float32)
    hsel2[0:64, 0] = 1.0
    hsel2[64:128, 1] = 1.0
    hsel2_t = nc.inline_tensor(hsel2, name="hsel2")
    hsel2T = np.zeros((2, 128), np.float32)
    hsel2T[0, 0:64] = 1.0
    hsel2T[1, 64:128] = 1.0
    hsel2T_t = nc.inline_tensor(hsel2T, name="hsel2T")

    g, t, v, s, sy = nc.gpsimd, nc.tensor, nc.vector, nc.scalar, nc.sync
    es = ExitStack()
    w_sb = es.enter_context(nc.sbuf_tensor("w_sb", [128, KD * 3072], BF16))
    tb_sb = es.enter_context(nc.sbuf_tensor("tb_sb", [128, 2 * 4 * 512], BF16))
    xs_sb = es.enter_context(nc.sbuf_tensor("xs_sb", [128, 2 * KD * 512], BF16))
    qk_sb = es.enter_context(nc.sbuf_tensor("qk_sb", [128, 2 * 16 * 512], BF16))
    rot_sb = es.enter_context(nc.sbuf_tensor("rot_sb", [128, 512], BF16))
    vt_sb = es.enter_context(nc.sbuf_tensor("vt_sb", [128, 2 * 4 * 1024], BF16))
    st_sb = es.enter_context(nc.sbuf_tensor("st_sb", [2, 3 * 512], F32))
    sq_sb = es.enter_context(nc.sbuf_tensor("sq_sb", [128, 512], F32))
    t1_sb = es.enter_context(nc.sbuf_tensor("t1_sb", [128, 512], F32))
    t2_sb = es.enter_context(nc.sbuf_tensor("t2_sb", [128, 512], F32))
    sc_sb = es.enter_context(nc.sbuf_tensor("sc_sb", [128, 2], F32))
    hs_sb = es.enter_context(nc.sbuf_tensor("hs_sb", [128, 4], F32))
    hsT_sb = es.enter_context(nc.sbuf_tensor("hsT_sb", [2, 128], F32))
    P_sb = es.enter_context(nc.sbuf_tensor("P_sb", [128, 128], BF16))
    ps = es.enter_context(nc.psum_tensor("ps", [128, 8 * 512], F32))
    LD0 = es.enter_context(nc.semaphore("LD0"))
    LD1 = es.enter_context(nc.semaphore("LD1"))
    LW = es.enter_context(nc.semaphore("LW"))
    MM = es.enter_context(nc.semaphore("MM"))
    EV = es.enter_context(nc.semaphore("EV"))
    SQ = es.enter_context(nc.semaphore("SQ"))
    ST0 = es.enter_context(nc.semaphore("ST0"))
    ST1 = es.enter_context(nc.semaphore("ST1"))
    C = Ctr()
    mm_n = ev_n = sq_n = 0
    for kk in range(KD):
        g.dma_start(out=w_sb[:, kk * 3072:(kk + 1) * 3072],
                    in_=Wqkv[kk * 128:(kk + 1) * 128, :]).then_inc(LW, 16)
    C.add(LW, KD)
    g.dma_start(out=P_sb[:, :], in_=P_t.ap()[:, :]).then_inc(LW, 16)
    C.add(LW, 1)
    sy.dma_start(out=hs_sb[:, 0:2], in_=hsel2_t.ap()[:, 0:2]).then_inc(LD0, 16)
    sy.dma_start(out=hsT_sb[0:2, :], in_=hsel2T_t.ap()[:, :]).then_inc(LD0, 16)
    sy.dma_start(out=sc_sb[0:64, 0:1], in_=qs[:, :]).then_inc(LD0, 16)
    sy.dma_start(out=sc_sb[64:128, 0:1], in_=qs[:, :]).then_inc(LD0, 16)
    sy.dma_start(out=sc_sb[0:64, 1:2], in_=ks[:, :]).then_inc(LD0, 16)
    sy.dma_start(out=sc_sb[64:128, 1:2], in_=ks[:, :]).then_inc(LD0, 16)
    C.add(LD0, 6)
    t.wait_ge(LW, 16 * C.val(LW))
    t.wait_ge(LD0, 16 * C.val(LD0))
    s.wait_ge(LD0, 16 * C.val(LD0))
    v.wait_ge(LD0, 16 * C.val(LD0))

    cons = {}
    for nt in range(8):
        if LFA_SERIAL and nt >= 1:
            t.wait_ge(EV, ev_n)
            t.wait_ge(SQ, sq_n)
            v.wait_ge(MM, mm_n)
            s.wait_ge(MM, mm_n)
            v.wait_ge(ST0, 16 * C.val(ST0))
            v.wait_ge(ST1, 16 * C.val(ST1))
            sy.wait_ge(EV, ev_n)
            sy.wait_ge(MM, mm_n)
        par = nt % 2
        LD = LD0 if par == 0 else LD1
        ST = ST0 if par == 0 else ST1
        xo = par * KD * 512
        qko = par * 16 * 512
        vo = par * 4 * 1024
        tok0 = nt * 512
        if nt >= 2:
            sy.wait_ge(EV, cons[nt - 2])
            sy.wait_ge(LD, 16 * C.val(LD))
        if nt == 0:
            v.memset(xs_sb[:, 4 * 512:4 * 512 + 1], 0.0)
            v.memset(xs_sb[:, 5 * 512:5 * 512 + 1], 0.0)
            v.memset(xs_sb[:, 6 * 512:6 * 512 + 1], 0.0)
            li = v.memset(xs_sb[:, 7 * 512:7 * 512 + 1], 0.0)
            li.then_inc(EV, 1)
            ev_n += 1
            sy.wait_ge(EV, ev_n)
        for k in range(KD):
            if k < 4:
                sy.dma_start(out=xs_sb[:, xo + k * 512:xo + (k + 1) * 512],
                             in_=xfT[k * 128:(k + 1) * 128, tok0:tok0 + 512]).then_inc(LD, 16)
            elif tok0 > 0:
                sy.dma_start(out=xs_sb[:, xo + k * 512:xo + (k + 1) * 512],
                             in_=xfT[k * 128:(k + 1) * 128, tok0 - 1:tok0 + 511]).then_inc(LD, 16)
            else:
                sy.dma_start(out=xs_sb[:, xo + k * 512 + 1:xo + (k + 1) * 512],
                             in_=xfT[k * 128:(k + 1) * 128, 0:511]).then_inc(LD, 16)
        C.add(LD, KD)
        tbo = par * 2048
        sy.dma_start(out=tb_sb[:, tbo + 0:tbo + 512], in_=cs_t.ap()[:, tok0:tok0 + 512]).then_inc(LD, 16)
        sy.dma_start(out=tb_sb[:, tbo + 512:tbo + 1024], in_=ss_t.ap()[:, tok0:tok0 + 512]).then_inc(LD, 16)
        sy.dma_start(out=tb_sb[:, tbo + 1024:tbo + 1536], in_=ci_t.ap()[:, tok0:tok0 + 512]).then_inc(LD, 16)
        sy.dma_start(out=tb_sb[:, tbo + 1536:tbo + 2048], in_=si_t.ap()[:, tok0:tok0 + 512]).then_inc(LD, 16)
        C.add(LD, 4)
        ld_tot = 16 * C.val(LD)
        t.wait_ge(LD, ld_tot)
        if nt == 0:
            t.wait_ge(EV, ev_n)
        if nt >= 2:
            v.wait_ge(ST, 16 * C.val(ST))
        for m in range(16):
            bank = m % 4
            for kk in range(KD):
                li = t.matmul(ps[:, bank * 512:(bank + 1) * 512],
                              w_sb[:, kk * 3072 + m * 128:kk * 3072 + (m + 1) * 128],
                              xs_sb[:, xo + kk * 512:xo + (kk + 1) * 512],
                              start=(kk == 0), stop=(kk == KD - 1))
            li.then_inc(MM, 1)
            mm_n += 1
            v.wait_ge(MM, mm_n)
            li = v.tensor_copy(qk_sb[:, qko + m * 512:qko + (m + 1) * 512],
                               ps[:, bank * 512:(bank + 1) * 512])
            li.then_inc(EV, 1)
            ev_n += 1
            if m >= 3:
                t.wait_ge(EV, ev_n - 3)
        for m in range(16):
            is_q = m < 8
            qo = qko + m * 512
            s.wait_ge(EV, ev_n)
            s.activation(sq_sb[:], qk_sb[:, qo:qo + 512], AF.Square).then_inc(SQ, 1)
            sq_n += 1
            t.wait_ge(SQ, sq_n)
            t.matmul(ps[0:2, 4 * 512:4 * 512 + 512], hs_sb[:, 0:2], sq_sb[:],
                     start=True, stop=True).then_inc(MM, 1)
            mm_n += 1
            s.wait_ge(MM, mm_n)
            s.activation(st_sb[0:2, 0:512], ps[0:2, 4 * 512:4 * 512 + 512], AF.Sqrt).then_inc(SQ, 1)
            sq_n += 1
            v.wait_ge(SQ, sq_n)
            v.tensor_single_scalar(st_sb[0:2, 512:1024], st_sb[0:2, 0:512], 1e-12, ALU.max).then_inc(EV, 1)
            ev_n += 1
            v.wait_ge(EV, ev_n)
            v.reciprocal(st_sb[0:2, 512:1024], st_sb[0:2, 512:1024]).then_inc(EV, 1)
            ev_n += 1
            t.wait_ge(EV, ev_n)
            t.matmul(ps[:, 5 * 512:6 * 512], hsT_sb[0:2, :], st_sb[0:2, 512:1024],
                     start=True, stop=True).then_inc(MM, 1)
            mm_n += 1
            v.wait_ge(MM, mm_n)
            v.tensor_tensor(t1_sb[:], qk_sb[:, qo:qo + 512], ps[:, 5 * 512:6 * 512], ALU.mult)
            li = v.tensor_scalar(qk_sb[:, qo:qo + 512], t1_sb[:],
                                 sc_sb[:, 0:1] if is_q else sc_sb[:, 1:2], None, ALU.mult)
            li.then_inc(EV, 1)
            ev_n += 1
            t.wait_ge(EV, ev_n)
            t.matmul(ps[:, 6 * 512:7 * 512], P_sb[:, :], qk_sb[:, qo:qo + 512],
                     start=True, stop=True).then_inc(MM, 1)
            mm_n += 1
            v.wait_ge(MM, mm_n)
            li = v.tensor_copy(rot_sb[:], ps[:, 6 * 512:7 * 512])
            li.then_inc(EV, 1)
            ev_n += 1
            cso = tbo + (0 if is_q else 1024)
            sso = tbo + (512 if is_q else 1536)
            v.wait_ge(EV, ev_n)
            v.tensor_tensor(t1_sb[:], qk_sb[:, qo:qo + 512], tb_sb[:, cso:cso + 512], ALU.mult)
            v.tensor_tensor(t2_sb[:], rot_sb[:], tb_sb[:, sso:sso + 512], ALU.mult)
            li = v.tensor_tensor(qk_sb[:, qo:qo + 512], t1_sb[:], t2_sb[:], ALU.add)
            li.then_inc(EV, 1)
            ev_n += 1
            t.wait_ge(EV, ev_n)
        for tg in range(4):
            for half in range(2):
                bank = 4 + (tg * 2 + half) % 4
                for kk in range(KD):
                    li = t.matmul(ps[:, bank * 512:(bank + 1) * 512],
                                  xs_sb[:, xo + kk * 512 + tg * 128:xo + kk * 512 + (tg + 1) * 128],
                                  w_sb[:, kk * 3072 + 2048 + half * 512:kk * 3072 + 2048 + (half + 1) * 512],
                                  start=(kk == 0), stop=(kk == KD - 1))
                li.then_inc(MM, 1)
                mm_n += 1
                v.wait_ge(MM, mm_n)
                li = v.tensor_copy(vt_sb[:, vo + tg * 1024 + half * 512:vo + tg * 1024 + (half + 1) * 512],
                                   ps[:, bank * 512:(bank + 1) * 512])
                li.then_inc(EV, 1)
                ev_n += 1
                if tg * 2 + half >= 3:
                    t.wait_ge(EV, ev_n - 3)
        cons[nt] = ev_n
        g.wait_ge(EV, ev_n)
        if nt >= 2:
            g.wait_ge(ST, 16 * C.val(ST))
        for k in range(KD):
            g.dma_start(out=qrT[k * 128:(k + 1) * 128, tok0:tok0 + 512],
                        in_=qk_sb[:, qko + k * 512:qko + (k + 1) * 512]).then_inc(ST, 16)
            g.dma_start(out=krT[k * 128:(k + 1) * 128, tok0:tok0 + 512],
                        in_=qk_sb[:, qko + (8 + k) * 512:qko + (9 + k) * 512]).then_inc(ST, 16)
        for tg in range(4):
            g.dma_start(out=vt[tok0 + tg * 128:tok0 + (tg + 1) * 128, :],
                        in_=vt_sb[:, vo + tg * 1024:vo + (tg + 1) * 1024]).then_inc(ST, 16)
        C.add(ST, 2 * KD + 4)
    g.wait_ge(ST0, 16 * C.val(ST0))
    g.wait_ge(ST1, 16 * C.val(ST1))
    es.close()
    return nc


_cache = {}

def _get(name, fn, *a):
    key = (name,) + a
    if key not in _cache:
        _cache[key] = fn(*a)
    return _cache[key]


def run_ln(xT_list):
    nc = _get("ln", build_ln)
    res = run_bass_kernel_spmd(nc, [{"xT": xT_list[c]} for c in range(NCORE)],
                               core_ids=list(range(NCORE)))
    return [r["xn_out"] for r in res.results]


def run_l0_attn(xn, w0, w1, w2):
    s_chunks = [[] for _ in range(NCORE)]
    sse = np.zeros((NCORE, 0), np.float32)
    for k in range(8):
        nck = _get("ck", build_chunk, k)
        in_maps = []
        for c in range(NCORE):
            m = {"xnT": np.ascontiguousarray(xn[c][:, k * CHUNK:(k + 1) * CHUNK]),
                 "w0": w0, "w2": w2, "w1": w1}
            if k > 0:
                m["ssep"] = np.ascontiguousarray(sse)
            in_maps.append(m)
        resk = run_bass_kernel_spmd(nck, in_maps, core_ids=list(range(NCORE)))
        for c in range(NCORE):
            s_chunks[c].append(resk.results[c]["s_out"])
        sse = np.concatenate(
            [sse, np.stack([r["sse_out"][0, 0] for r in resk.results])[:, None]], axis=1)
    s = [np.concatenate(s_chunks[c], axis=1) for c in range(NCORE)]
    return s, sse


def kernel(**inputs):
    z = inputs
    R8 = list(range(NCORE))
    x = np.asarray(z["L0_attn_x"])
    xT_h = [np.ascontiguousarray(x[c].T) for c in R8]

    nc_ln = _get("ln", build_ln)
    r = run_bass_kernel_spmd(nc_ln, [{"xT": xT_h[c]} for c in R8], core_ids=R8)
    xn_b = [np.asarray(rc["xn_out"]) for rc in r.results]
    xn_f = [np.asarray(rc["xnf_out"]) for rc in r.results]

    sT, _sse = run_l0_attn(xn_b, np.ascontiguousarray(np.asarray(z["L0_attn_w0"])),
                           np.ascontiguousarray(np.asarray(z["L0_attn_w1"])),
                           np.ascontiguousarray(np.asarray(z["L0_attn_w2"])))

    nca = _get("lfa", build_lfa)
    ncb = _get("lfb", build_lfb)
    ncc = _get("lfc", build_lfc)
    Wqkv = np.asarray(z["L0_ffn_Wqkv"])
    qsA = np.asarray(z["L0_ffn_q_scale"])
    ksA = np.asarray(z["L0_ffn_k_scale"])
    WoA = np.asarray(z["L0_ffn_Wo"])
    lngA = np.asarray(z["L0_ffn_ln_g"])
    lnbA = np.asarray(z["L0_ffn_ln_b"])
    W1A = np.asarray(z["L0_ffn_W1"])
    b1A = np.asarray(z["L0_ffn_b1"])
    W2A = np.asarray(z["L0_ffn_W2"])
    b2A = np.asarray(z["L0_ffn_b2"])
    xfb, xff = xn_b, xn_f
    for i in range(6):
        ina = [{"xfT": np.ascontiguousarray(xfb[c]),
                "Wqkv": np.ascontiguousarray(Wqkv[i]),
                "qs": np.ascontiguousarray(qsA[i].reshape(-1, 1)),
                "ks": np.ascontiguousarray(ksA[i].reshape(-1, 1))} for c in R8]
        ra = run_bass_kernel_spmd(nca, ina, core_ids=R8)
        inb = [{"qrT": np.asarray(ra.results[c]["qrT"]),
                "krT": np.asarray(ra.results[c]["krT"]),
                "vt": np.asarray(ra.results[c]["vt"])} for c in R8]
        rb = run_bass_kernel_spmd(ncb, inb, core_ids=R8)
        inc = [{"aoT": np.asarray(rb.results[c]["aoT"]),
                "xT": np.ascontiguousarray(xff[c]),
                "Wo": np.ascontiguousarray(WoA[i]),
                "ln_g": np.ascontiguousarray(lngA[i].reshape(-1, 1)),
                "ln_b": np.ascontiguousarray(lnbA[i].reshape(-1, 1)),
                "W1": np.ascontiguousarray(W1A[i]),
                "b1": np.ascontiguousarray(b1A[i].reshape(-1, 1)),
                "W2": np.ascontiguousarray(W2A[i]),
                "b2": np.ascontiguousarray(b2A[i].reshape(-1, 1))} for c in R8]
        rc_ = run_bass_kernel_spmd(ncc, inc, core_ids=R8)
        xff = [np.asarray(rc_.results[c]["x2T"]) for c in R8]
        xfb = [np.asarray(rc_.results[c]["x2Tb"]) for c in R8]

    nc9a = _get("l9a", build_l9a)
    in9a = [{"xT": xT_h[c],
             "sT": np.ascontiguousarray(sT[c]),
             "fT": np.ascontiguousarray(xfb[c]),
             "ln1_g": np.ascontiguousarray(np.asarray(z["L1_attn_ln1_g"]).reshape(-1, 1)),
             "ln1_b": np.ascontiguousarray(np.asarray(z["L1_attn_ln1_b"]).reshape(-1, 1)),
             "gate": np.ascontiguousarray(np.asarray(z["L1_ffn_gate"]))} for c in R8]
    r9a = run_bass_kernel_spmd(nc9a, in9a, core_ids=R8)
    x1T = [np.asarray(r9a.results[c]["x1T"]) for c in R8]
    xn1T = [np.asarray(r9a.results[c]["xn1T"]) for c in R8]
    yT = [np.asarray(r9a.results[c]["yT"]) for c in R8]
    probs = [np.asarray(r9a.results[c]["probs"]) for c in R8]

    nc9b = _get("l9b", build_l9b)
    in9b = [{"yT": np.ascontiguousarray(yT[c]),
             "Wq": np.ascontiguousarray(np.asarray(z["L1_attn_Wq"])),
             "bq": np.ascontiguousarray(np.asarray(z["L1_attn_bq"]).reshape(-1, 1)),
             "Wk": np.ascontiguousarray(np.asarray(z["L1_attn_Wk"])),
             "bk": np.ascontiguousarray(np.asarray(z["L1_attn_bk"]).reshape(-1, 1)),
             "Wv": np.ascontiguousarray(np.asarray(z["L1_attn_Wv"])),
             "bv": np.ascontiguousarray(np.asarray(z["L1_attn_bv"]).reshape(-1, 1)),
             "Wg": np.ascontiguousarray(np.asarray(z["L1_attn_Wg"])),
             "bg": np.ascontiguousarray(np.asarray(z["L1_attn_bg"]).reshape(-1, 1)),
             "dlog": np.ascontiguousarray(np.asarray(z["L1_attn_decay_logit"]).reshape(-1, 1))}
            for c in R8]
    r9b = run_bass_kernel_spmd(nc9b, in9b, core_ids=R8)
    outT = [np.asarray(r9b.results[c]["outT"]) for c in R8]

    nc9c = _get("l9c", build_l9c)
    in9c = [{"outT": np.ascontiguousarray(outT[c]),
             "xn1T": np.ascontiguousarray(xn1T[c]),
             "Wo": np.ascontiguousarray(np.asarray(z["L1_attn_Wo"])),
             "bo": np.ascontiguousarray(np.asarray(z["L1_attn_bo"]).reshape(-1, 1)),
             "ln2_g": np.ascontiguousarray(np.asarray(z["L1_attn_ln2_g"]).reshape(-1, 1)),
             "ln2_b": np.ascontiguousarray(np.asarray(z["L1_attn_ln2_b"]).reshape(-1, 1)),
             "W1": np.ascontiguousarray(np.asarray(z["L1_attn_W1"])),
             "b1": np.ascontiguousarray(np.asarray(z["L1_attn_b1"]).reshape(-1, 1)),
             "W2": np.ascontiguousarray(np.asarray(z["L1_attn_W2"])),
             "b2": np.ascontiguousarray(np.asarray(z["L1_attn_b2"]).reshape(-1, 1))}
            for c in R8]
    r9c = run_bass_kernel_spmd(nc9c, in9c, core_ids=R8)
    s1T = [np.asarray(r9c.results[c]["s1T"]) for c in R8]

    nc10 = _get("l10", build_l10)
    probsA = np.ascontiguousarray(np.stack(probs, 0).transpose(2, 0, 1).reshape(8, NCORE * S))
    b1et = np.ascontiguousarray(np.asarray(z["L1_ffn_b1"]).reshape(8, 32, 128).transpose(0, 2, 1))
    b2et = np.ascontiguousarray(np.asarray(z["L1_ffn_b2"]).reshape(8, 8, 128).transpose(0, 2, 1))
    in10 = [{"probsA": probsA,
             "probsO": np.ascontiguousarray(probs[c].T),
             "xn1tm": np.ascontiguousarray(xn1T[c].T),
             "xn1T": np.ascontiguousarray(xn1T[c]),
             "x1T": np.ascontiguousarray(x1T[c]),
             "s1T": np.ascontiguousarray(s1T[c]),
             "W1e": np.ascontiguousarray(np.asarray(z["L1_ffn_W1"])),
             "W2e": np.ascontiguousarray(np.asarray(z["L1_ffn_W2"])),
             "b1et": b1et, "b2et": b2et,
             "Ws1": np.ascontiguousarray(np.asarray(z["L1_ffn_Ws1"])),
             "bs1": np.ascontiguousarray(np.asarray(z["L1_ffn_bs1"]).reshape(-1, 1)),
             "Ws2": np.ascontiguousarray(np.asarray(z["L1_ffn_Ws2"])),
             "bs2": np.ascontiguousarray(np.asarray(z["L1_ffn_bs2"]).reshape(-1, 1))}
            for c in R8]
    r10 = run_bass_kernel_spmd(nc10, in10, core_ids=R8)
    out = np.stack([np.ascontiguousarray(np.asarray(r10.results[c]["outF"]).T) for c in R8])
    return out


def _lfb_masks():
    M = np.zeros((128, 4 * 512), np.float32)
    NEG = -1.0e30
    j = np.arange(512)
    for wt in range(2):
        for hf in range(2):
            blk = np.full((128, 512), NEG, np.float32)
            for p in range(128):
                i = hf * 128 + p
                if wt == 1:
                    ok = (j >= i) & (j <= i + 256)
                else:
                    ok = (j >= 256) & (j <= i + 256)
                blk[p, ok] = 0.0
            M[:, (wt * 2 + hf) * 512:(wt * 2 + hf + 1) * 512] = blk
    return M


def build_lfb():
    nc = mknc()
    qrT = nc.dram_tensor("qrT", [D, S], BF16, kind="ExternalInput").ap()
    krT = nc.dram_tensor("krT", [D, S], BF16, kind="ExternalInput").ap()
    vt = nc.dram_tensor("vt", [S, D], BF16, kind="ExternalInput").ap()
    aoT = nc.dram_tensor("aoT", [D, S], BF16, kind="ExternalOutput").ap()

    msk_t = nc.inline_tensor(_lfb_masks(), name="lfb_masks")
    idb_t = nc.inline_tensor(np.eye(128, dtype=np.float32), name="lfb_idf")

    g, t, v, s, sy = nc.gpsimd, nc.tensor, nc.vector, nc.scalar, nc.sync
    es = ExitStack()
    pool = ConstPool(nc, es, [8.0])
    q_sb = es.enter_context(nc.sbuf_tensor("q_sb", [64, 2 * 16 * 256], BF16))
    k_sb = es.enter_context(nc.sbuf_tensor("k_sb", [64, 2 * 16 * 512], BF16))
    v_sb = es.enter_context(nc.sbuf_tensor("v_sb", [128, 2 * 4 * 1024], BF16))
    msk_sb = es.enter_context(nc.sbuf_tensor("msk_sb", [128, 4 * 512], F32))
    idf_sb = es.enter_context(nc.sbuf_tensor("idf_sb", [128, 128], F32))
    sm_sb = es.enter_context(nc.sbuf_tensor("sm_sb", [128, 2 * 512], F32))
    e_sb = es.enter_context(nc.sbuf_tensor("e_sb", [128, 2 * 512], F32))
    en_sb = es.enter_context(nc.sbuf_tensor("en_sb", [128, 2 * 512], F32))
    ptb_sb = es.enter_context(nc.sbuf_tensor("ptb_sb", [128, 2 * 512], BF16))
    ao_sb = es.enter_context(nc.sbuf_tensor("ao_sb", [128, 2 * 2048], BF16))
    red_sb = es.enter_context(nc.sbuf_tensor("red_sb", [128, 8], F32))
    psS = es.enter_context(nc.psum_tensor("psS", [128, 2 * 512], F32))
    psT = es.enter_context(nc.psum_tensor("psT", [128, 2 * 512], F32))
    psO = es.enter_context(nc.psum_tensor("psO", [128, 2 * 512], F32))
    LD0 = es.enter_context(nc.semaphore("LD0"))
    LD1 = es.enter_context(nc.semaphore("LD1"))
    ST0 = es.enter_context(nc.semaphore("ST0"))
    ST1 = es.enter_context(nc.semaphore("ST1"))
    MM = es.enter_context(nc.semaphore("MM"))
    EV = es.enter_context(nc.semaphore("EV"))
    SQ = es.enter_context(nc.semaphore("SQ"))
    C = Ctr()
    mm_n = ev_n = sq_n = 0

    sy.dma_start(out=msk_sb[:, :], in_=msk_t.ap()[:, :]).then_inc(LD0, 16)
    sy.dma_start(out=idf_sb[:, :], in_=idb_t.ap()[:, :]).then_inc(LD0, 16)
    C.add(LD0, 2)
    s.wait_ge(pool.CN, 1)

    NW, NH = 16, 16
    G = NW * NH * 2
    s_mm = {}
    maskadd_ev = {}
    negb_ev = {}
    exp_sq = {}
    rr_ev = {}
    pnorm_sq = {}
    trans_mm = {}
    ptevac_ev = {}
    out_mm = {}
    aoevac_ev = {}
    ld_target = {}
    out_mm_wend = {}
    st_count = {0: 0, 1: 0}
    memset_ev = 0

    def emit_loads(wdw):
        nonlocal ev_n, memset_ev
        par = wdw % 2
        LD = LD0 if par == 0 else LD1
        qo = par * 4096
        ko = par * 8192
        vo = par * 4096
        if wdw >= 2:
            sy.wait_ge(MM, out_mm_wend[wdw - 2])
            sy.wait_ge(LD, 16 * C.val(LD))
        for h in range(NH):
            m = h // 2
            r0 = m * 128 + (h % 2) * 64
            sy.dma_start(out=q_sb[0:64, qo + h * 256:qo + (h + 1) * 256],
                         in_=qrT[r0:r0 + 64, wdw * 256:(wdw + 1) * 256]).then_inc(LD, 16)
            if wdw == 0:
                sy.dma_start(out=k_sb[0:64, ko + h * 512 + 256:ko + h * 512 + 512],
                             in_=krT[r0:r0 + 64, 0:256]).then_inc(LD, 16)
            else:
                sy.dma_start(out=k_sb[0:64, ko + h * 512:ko + (h + 1) * 512],
                             in_=krT[r0:r0 + 64, (wdw - 1) * 256:(wdw + 1) * 256]).then_inc(LD, 16)
        C.add(LD, 2 * NH)
        nv = 0
        for jc in range(4):
            if wdw == 0:
                if jc < 2:
                    continue
                src0 = (jc - 2) * 128
            else:
                src0 = (wdw - 1) * 256 + jc * 128
            sy.dma_start(out=v_sb[:, vo + jc * 1024:vo + (jc + 1) * 1024],
                         in_=vt[src0:src0 + 128, :]).then_inc(LD, 16)
            nv += 1
        C.add(LD, nv)
        ld_target[wdw] = 16 * C.val(LD)
        if wdw == 0:
            for h in range(NH):
                v.memset(k_sb[0:64, h * 512:h * 512 + 256], 0.0)
            li = v.memset(v_sb[:, 0:2048], 0.0)
            li.then_inc(EV, 1)
            ev_n += 1
            memset_ev = ev_n

    def emit_headS(gi):
        nonlocal mm_n
        wdw, h, hf = gi // 32, (gi % 32) // 2, gi % 2
        par = wdw % 2
        gp = gi % 2
        qo = par * 4096
        ko = par * 8192
        if gi % 32 == 0:
            t.wait_ge(LD0 if par == 0 else LD1, ld_target[wdw])
            if wdw == 0:
                t.wait_ge(EV, memset_ev)
        if gi >= 2:
            t.wait_ge(EV, maskadd_ev[gi - 2])
        t.matmul(psS[:, gp * 512:(gp + 1) * 512],
                 q_sb[0:64, qo + h * 256 + hf * 128:qo + h * 256 + (hf + 1) * 128],
                 k_sb[0:64, ko + h * 512:ko + (h + 1) * 512],
                 start=True, stop=True).then_inc(MM, 1)
        mm_n += 1
        s_mm[gi] = mm_n

    def emit_headV(gi):
        nonlocal ev_n
        wdw, hf = gi // 32, gi % 2
        gp = gi % 2
        wt = 0 if wdw == 0 else 1
        mo = (wt * 2 + hf) * 512
        v.wait_ge(MM, s_mm[gi])
        if gi >= 2:
            v.wait_ge(SQ, exp_sq[gi - 2])
        v.tensor_tensor(sm_sb[:, gp * 512:(gp + 1) * 512],
                        psS[:, gp * 512:(gp + 1) * 512],
                        msk_sb[:, mo:mo + 512], ALU.add).then_inc(EV, 1)
        ev_n += 1
        maskadd_ev[gi] = ev_n
        v.reduce_max(red_sb[:, gp * 4:gp * 4 + 1],
                     sm_sb[:, gp * 512:(gp + 1) * 512], axis=AXX).then_inc(EV, 1)
        ev_n += 1
        v.wait_ge(EV, ev_n)
        v.tensor_scalar(red_sb[:, gp * 4 + 1:gp * 4 + 2],
                        red_sb[:, gp * 4:gp * 4 + 1], -8.0, None,
                        ALU.mult).then_inc(EV, 1)
        ev_n += 1
        negb_ev[gi] = ev_n

    def emit_sExp(gi):
        nonlocal sq_n
        gp = gi % 2
        s.wait_ge(EV, negb_ev[gi])
        s.activation(e_sb[:, gp * 512:(gp + 1) * 512],
                     sm_sb[:, gp * 512:(gp + 1) * 512], AF.Exp,
                     scale=pool.ap(8.0),
                     bias=red_sb[:, gp * 4 + 1:gp * 4 + 2],
                     accum_out=red_sb[:, gp * 4 + 2:gp * 4 + 3]).then_inc(SQ, 1)
        sq_n += 1
        exp_sq[gi] = sq_n

    def emit_vRecip(gi):
        nonlocal ev_n
        gp = gi % 2
        v.wait_ge(SQ, exp_sq[gi])
        v.reciprocal(red_sb[:, gp * 4 + 3:gp * 4 + 4],
                     red_sb[:, gp * 4 + 2:gp * 4 + 3]).then_inc(EV, 1)
        ev_n += 1
        rr_ev[gi] = ev_n

    def emit_sPnorm(gi):
        nonlocal sq_n
        gp = gi % 2
        s.wait_ge(EV, rr_ev[gi])
        if gi >= 2:
            s.wait_ge(MM, trans_mm[gi - 2])
        s.activation(en_sb[:, gp * 512:(gp + 1) * 512],
                     e_sb[:, gp * 512:(gp + 1) * 512], AF.Copy,
                     scale=red_sb[:, gp * 4 + 3:gp * 4 + 4]).then_inc(SQ, 1)
        sq_n += 1
        pnorm_sq[gi] = sq_n

    def emit_tail(gi):
        nonlocal mm_n, ev_n
        wdw, h, hf = gi // 32, (gi % 32) // 2, gi % 2
        par = wdw % 2
        gp = gi % 2
        vo = par * 4096
        aopar = par
        ST = ST0 if par == 0 else ST1
        t.wait_ge(SQ, pnorm_sq[gi])
        if gi >= 2:
            t.wait_ge(EV, ptevac_ev[gi - 2])
        for jc in range(4):
            li = t.transpose(psT[:, gp * 512 + jc * 128:gp * 512 + (jc + 1) * 128],
                             en_sb[:, gp * 512 + jc * 128:gp * 512 + (jc + 1) * 128],
                             idf_sb[:, :])
        li.then_inc(MM, 1)
        mm_n += 1
        trans_mm[gi] = mm_n
        v.wait_ge(MM, trans_mm[gi])
        v.tensor_copy(ptb_sb[:, gp * 512:(gp + 1) * 512],
                      psT[:, gp * 512:(gp + 1) * 512]).then_inc(EV, 1)
        ev_n += 1
        ptevac_ev[gi] = ev_n
        t.wait_ge(EV, ptevac_ev[gi])
        for jc in range(4):
            li = t.matmul(psO[0:64, gp * 512:gp * 512 + 128],
                          v_sb[:, vo + jc * 1024 + h * 64:vo + jc * 1024 + (h + 1) * 64],
                          ptb_sb[:, gp * 512 + jc * 128:gp * 512 + (jc + 1) * 128],
                          start=(jc == 0), stop=(jc == 3))
        li.then_inc(MM, 1)
        mm_n += 1
        out_mm[gi] = mm_n
        v.wait_ge(MM, out_mm[gi])
        if gi % 32 == 0 and wdw >= 2:
            v.wait_ge(ST, 16 * st_count[aopar])
        po = (h % 2) * 64
        v.tensor_copy(ao_sb[po:po + 64,
                            aopar * 2048 + (h // 2) * 256 + hf * 128:
                            aopar * 2048 + (h // 2) * 256 + (hf + 1) * 128],
                      psO[0:64, gp * 512:gp * 512 + 128]).then_inc(EV, 1)
        ev_n += 1
        aoevac_ev[gi] = ev_n
        if gi % 32 == 31:
            out_mm_wend[wdw] = mm_n
            sy.wait_ge(EV, aoevac_ev[gi])
            sy.wait_ge(ST, 16 * st_count[aopar])
            for m in range(KD):
                sy.dma_start(out=aoT[m * 128:(m + 1) * 128, wdw * 256:(wdw + 1) * 256],
                             in_=ao_sb[:, aopar * 2048 + m * 256:aopar * 2048 + (m + 1) * 256]
                             ).then_inc(ST, 16)
            st_count[aopar] += KD

    for gi in range(G):
        if gi % 32 == 0:
            emit_loads(gi // 32)
        emit_headS(gi)
        if gi >= 2:
            emit_vRecip(gi - 2)
        if gi >= 3:
            emit_tail(gi - 3)
        emit_headV(gi)
        if gi >= 2:
            emit_sPnorm(gi - 2)
        emit_sExp(gi)
    for gi in range(G, G + 3):
        if gi - 2 < G:
            emit_vRecip(gi - 2)
        if 0 <= gi - 3 < G:
            emit_tail(gi - 3)
        if gi - 2 < G:
            emit_sPnorm(gi - 2)
    sy.wait_ge(ST0, 16 * st_count[0])
    sy.wait_ge(ST1, 16 * st_count[1])
    es.close()
    return nc



LFC_NP = 22
LFC_DBG = False

def _lfc_w(i):
    return min(128, 2730 - i * 128)


def build_lfc():
    nc = mknc()
    aoT = nc.dram_tensor("aoT", [D, S], BF16, kind="ExternalInput").ap()
    xT = nc.dram_tensor("xT", [D, S], F32, kind="ExternalInput").ap()
    Wo = nc.dram_tensor("Wo", [D, D], F32, kind="ExternalInput").ap()
    ln_g = nc.dram_tensor("ln_g", [D, 1], F32, kind="ExternalInput").ap()
    ln_b = nc.dram_tensor("ln_b", [D, 1], F32, kind="ExternalInput").ap()
    W1 = nc.dram_tensor("W1", [D, 5460], F32, kind="ExternalInput").ap()
    b1 = nc.dram_tensor("b1", [5460, 1], F32, kind="ExternalInput").ap()
    W2 = nc.dram_tensor("W2", [2730, D], F32, kind="ExternalInput").ap()
    b2 = nc.dram_tensor("b2", [D, 1], F32, kind="ExternalInput").ap()
    x2T = nc.dram_tensor("x2T", [D, S], F32, kind="ExternalOutput").ap()
    x2Tb = nc.dram_tensor("x2Tb", [D, S], BF16, kind="ExternalOutput").ap()
    _ik = "ExternalOutput" if LFC_DBG else "Internal"
    x1s = nc.dram_tensor("x1s", [D, S], F32, kind=_ik)
    hsrc = nc.dram_tensor("hsrc", [D, S], BF16, kind=_ik)
    hgs = nc.dram_tensor("hgs", [LFC_NP * 128, S], BF16, kind=_ik)
    ones128 = nc.inline_tensor(np.ones((128, 1), np.float32), name="ones128")
    ones1x128 = nc.inline_tensor(np.ones((1, 128), np.float32), name="ones1x128")
    g, t, v, s, sy = nc.gpsimd, nc.tensor, nc.vector, nc.scalar, nc.sync
    es = ExitStack()
    pool = ConstPool(nc, es, [1e-5])
    w_sb = es.enter_context(nc.sbuf_tensor("w_sb", [128, 8 * 2816], BF16))
    a_sb = es.enter_context(nc.sbuf_tensor("a_sb", [128, 2 * KD * 512], BF16))
    x_sb = es.enter_context(nc.sbuf_tensor("x_sb", [128, KD * 512], F32))
    x1_sb = es.enter_context(nc.sbuf_tensor("x1_sb", [128, 2 * KD * 512], F32))
    xsh_sb = es.enter_context(nc.sbuf_tensor("xsh_sb", [128, KD * 512], F32))
    h_sb = es.enter_context(nc.sbuf_tensor("h_sb", [128, 2 * KD * 512], BF16))
    out_sb = es.enter_context(nc.sbuf_tensor("out_sb", [128, KD * 512], F32))
    gg_sb = es.enter_context(nc.sbuf_tensor("gg_sb", [128, 2 * 512], F32))
    hh_sb = es.enter_context(nc.sbuf_tensor("hh_sb", [128, 2 * 512], BF16))
    sq_sb = es.enter_context(nc.sbuf_tensor("sq_sb", [128, 512], F32))
    t1_sb = es.enter_context(nc.sbuf_tensor("t1_sb", [128, 512], F32))
    st_sb = es.enter_context(nc.sbuf_tensor("st_sb", [1, 4 * 512], F32))
    bvec_sb = es.enter_context(nc.sbuf_tensor("bvec_sb", [128, 96], F32))
    pcol_sb = es.enter_context(nc.sbuf_tensor("pcol_sb", [128, 4], F32))
    ones_sb = es.enter_context(nc.sbuf_tensor("ones_sb", [128, 129], F32))
    ps = es.enter_context(nc.psum_tensor("ps", [128, 8 * 512], F32))
    LD0 = es.enter_context(nc.semaphore("LD0"))
    LD1 = es.enter_context(nc.semaphore("LD1"))
    LW = es.enter_context(nc.semaphore("LW"))
    MM = es.enter_context(nc.semaphore("MM"))
    EV = es.enter_context(nc.semaphore("EV"))
    SQ = es.enter_context(nc.semaphore("SQ"))
    ST0 = es.enter_context(nc.semaphore("ST0"))
    ST1 = es.enter_context(nc.semaphore("ST1"))
    C = Ctr()
    mm_n = ev_n = sq_n = 0

    sy.dma_start(out=ones_sb[:, 0:1], in_=ones128.ap()[:, :]).then_inc(LD0, 16)
    sy.dma_start(out=ones_sb[0:1, 1:129], in_=ones1x128.ap()[:, :]).then_inc(LD0, 16)
    C.add(LD0, 2)
    for k in range(KD):
        sy.dma_start(out=bvec_sb[:, k:k + 1], in_=ln_g[k * 128:(k + 1) * 128, :]).then_inc(LD0, 16)
        sy.dma_start(out=bvec_sb[:, 8 + k:9 + k], in_=ln_b[k * 128:(k + 1) * 128, :]).then_inc(LD0, 16)
        sy.dma_start(out=bvec_sb[:, 16 + k:17 + k], in_=b2[k * 128:(k + 1) * 128, :]).then_inc(LD0, 16)
    for i in range(LFC_NP):
        w = _lfc_w(i)
        sy.dma_start(out=bvec_sb[0:w, 24 + i:25 + i], in_=b1[i * 128:i * 128 + w, :]).then_inc(LD0, 16)
        sy.dma_start(out=bvec_sb[0:w, 46 + i:47 + i], in_=b1[2730 + i * 128:2730 + i * 128 + w, :]).then_inc(LD0, 16)
    C.add(LD0, 3 * KD + 2 * LFC_NP)
    for kk in range(KD):
        g.dma_start(out=w_sb[:, kk * 1024:(kk + 1) * 1024],
                    in_=Wo[kk * 128:(kk + 1) * 128, :]).then_inc(LW, 16)
    C.add(LW, KD)

    def ln_stats(src_sb):
        nonlocal mm_n, ev_n, sq_n
        for k in range(KD):
            li = t.matmul(ps[0:1, 0:512], ones_sb[:, 0:1],
                          src_sb[:, k * 512:(k + 1) * 512],
                          start=(k == 0), stop=(k == KD - 1))
        li.then_inc(MM, 1)
        mm_n += 1
        for k in range(KD):
            if k > 0:
                s.wait_ge(MM, mm_n)
            s.activation(sq_sb[:, :], src_sb[:, k * 512:(k + 1) * 512],
                         AF.Square).then_inc(SQ, 1)
            sq_n += 1
            t.wait_ge(SQ, sq_n)
            t.matmul(ps[0:1, 512:1024], ones_sb[:, 0:1], sq_sb[:, :],
                     start=(k == 0), stop=(k == KD - 1)).then_inc(MM, 1)
            mm_n += 1
        v.wait_ge(MM, mm_n)
        v.tensor_scalar(st_sb[0:1, 0:512], ps[0:1, 0:512], 1.0 / D, None, ALU.mult)
        v.tensor_scalar(st_sb[0:1, 512:1024], ps[0:1, 512:1024], 1.0 / D, None, ALU.mult)
        v.tensor_tensor(st_sb[0:1, 1024:1536], st_sb[0:1, 0:512], st_sb[0:1, 0:512], ALU.mult)
        v.tensor_tensor(st_sb[0:1, 1024:1536], st_sb[0:1, 512:1024], st_sb[0:1, 1024:1536], ALU.subtract).then_inc(EV, 1)
        ev_n += 1
        s.wait_ge(EV, ev_n)
        s.activation(st_sb[0:1, 1024:1536], st_sb[0:1, 1024:1536], AF.Sqrt,
                     bias=pool.ap(1e-5, rows=1)).then_inc(SQ, 1)
        sq_n += 1
        v.wait_ge(SQ, sq_n)
        v.reciprocal(st_sb[0:1, 1024:1536], st_sb[0:1, 1024:1536])
        v.tensor_tensor(st_sb[0:1, 1536:2048], st_sb[0:1, 0:512], st_sb[0:1, 1024:1536], ALU.mult).then_inc(EV, 1)
        ev_n += 1
        t.wait_ge(EV, ev_n)
        t.matmul(ps[:, 2 * 512:3 * 512], ones_sb[0:1, 1:129], st_sb[0:1, 1024:1536], start=True, stop=True)
        li = t.matmul(ps[:, 3 * 512:4 * 512], ones_sb[0:1, 1:129], st_sb[0:1, 1536:2048], start=True, stop=True)
        li.then_inc(MM, 1)
        mm_n += 1
        v.wait_ge(MM, mm_n)

    s.wait_ge(pool.CN, 1)
    t.wait_ge(LW, 16 * C.val(LW))
    t.wait_ge(LD0, 16 * C.val(LD0))
    v.wait_ge(LD0, 16 * C.val(LD0))
    s.wait_ge(LD0, 16 * C.val(LD0))

    cons = {}
    x1ev = {}
    for nt in range(8):
        par = nt % 2
        LD = LD0 if par == 0 else LD1
        ST = ST0 if par == 0 else ST1
        xo = par * KD * 512
        tok0 = nt * 512
        if nt >= 2:
            sy.wait_ge(EV, cons[nt - 2])
            sy.wait_ge(LD, 16 * C.val(LD))
        for k in range(KD):
            sy.dma_start(out=a_sb[:, xo + k * 512:xo + (k + 1) * 512],
                         in_=aoT[k * 128:(k + 1) * 128, tok0:tok0 + 512]).then_inc(LD, 16)
        C.add(LD, KD)
        if nt >= 1:
            sy.wait_ge(EV, x1ev[nt - 1])
        for k in range(KD):
            sy.dma_start(out=x_sb[:, k * 512:(k + 1) * 512],
                         in_=xT[k * 128:(k + 1) * 128, tok0:tok0 + 512]).then_inc(LD, 16)
        C.add(LD, KD)
        t.wait_ge(LD, 16 * C.val(LD))
        v.wait_ge(LD, 16 * C.val(LD))
        for m in range(KD):
            bank = 4 + m % 4
            for kk in range(KD):
                li = t.matmul(ps[:, bank * 512:(bank + 1) * 512],
                              w_sb[:, kk * 1024 + m * 128:kk * 1024 + (m + 1) * 128],
                              a_sb[:, xo + kk * 512:xo + (kk + 1) * 512],
                              start=(kk == 0), stop=(kk == KD - 1))
            li.then_inc(MM, 1)
            mm_n += 1
            v.wait_ge(MM, mm_n)
            li = v.tensor_tensor(x1_sb[:, xo + m * 512:xo + (m + 1) * 512],
                                 ps[:, bank * 512:(bank + 1) * 512],
                                 x_sb[:, m * 512:(m + 1) * 512], ALU.add)
            li.then_inc(EV, 1)
            ev_n += 1
            if m >= 3:
                t.wait_ge(EV, ev_n - 3)
        x1ev[nt] = ev_n
        g.wait_ge(EV, ev_n)
        for k in range(KD):
            g.dma_start(out=x1s.ap()[k * 128:(k + 1) * 128, tok0:tok0 + 512],
                        in_=x1_sb[:, xo + k * 512:xo + (k + 1) * 512]).then_inc(ST, 16)
        C.add(ST, KD)
        for k in range(4):
            v.tensor_copy(xsh_sb[:, k * 512:(k + 1) * 512],
                          x1_sb[:, xo + k * 512:xo + (k + 1) * 512])
        for k in range(4, KD):
            v.tensor_copy(xsh_sb[:, k * 512 + 1:(k + 1) * 512],
                          x1_sb[:, xo + k * 512:xo + (k + 1) * 512 - 1])
            if nt == 0:
                v.memset(xsh_sb[:, k * 512:k * 512 + 1], 0.0)
            else:
                v.tensor_copy(xsh_sb[:, k * 512:k * 512 + 1], pcol_sb[:, k - 4:k - 3])
        li = v.tensor_copy(xsh_sb[:, 0:1], xsh_sb[:, 0:1])
        li.then_inc(EV, 1)
        ev_n += 1
        v.wait_ge(EV, ev_n)
        for k in range(4, KD):
            li = v.tensor_copy(pcol_sb[:, k - 4:k - 3],
                               x1_sb[:, xo + (k + 1) * 512 - 1:xo + (k + 1) * 512])
        li.then_inc(EV, 1)
        ev_n += 1
        s.wait_ge(EV, ev_n)
        t.wait_ge(EV, ev_n)
        ln_stats(xsh_sb)
        if nt >= 2:
            v.wait_ge(ST, 16 * C.val(ST))
        for k in range(KD):
            v.tensor_tensor(t1_sb[:], xsh_sb[:, k * 512:(k + 1) * 512], ps[:, 2 * 512:3 * 512], ALU.mult)
            v.tensor_tensor(t1_sb[:], t1_sb[:], ps[:, 3 * 512:4 * 512], ALU.subtract)
            li = v.tensor_scalar(h_sb[:, xo + k * 512:xo + (k + 1) * 512], t1_sb[:],
                                 bvec_sb[:, k:k + 1], bvec_sb[:, 8 + k:9 + k], ALU.mult, ALU.add)
        li.then_inc(EV, 1)
        ev_n += 1
        cons[nt] = ev_n
        g.wait_ge(EV, ev_n)
        for k in range(KD):
            g.dma_start(out=hsrc.ap()[k * 128:(k + 1) * 128, tok0:tok0 + 512],
                        in_=h_sb[:, xo + k * 512:xo + (k + 1) * 512]).then_inc(ST, 16)
        C.add(ST, KD)
    c1_mm = mm_n

    pair_ev = []
    pair_mm = []
    for hp in range(2):
        pairs = list(range(11 * hp, min(11 * (hp + 1), LFC_NP)))
        g.wait_ge(MM, mm_n)
        g.wait_ge(LW, 16 * C.val(LW))
        for kk in range(KD):
            for ii, i in enumerate(pairs):
                w = _lfc_w(i)
                g.dma_start(out=w_sb[:, kk * 2816 + ii * 256:kk * 2816 + ii * 256 + w],
                            in_=W1[kk * 128:(kk + 1) * 128, i * 128:i * 128 + w]).then_inc(LW, 16)
                g.dma_start(out=w_sb[:, kk * 2816 + ii * 256 + 128:kk * 2816 + ii * 256 + 128 + w],
                            in_=W1[kk * 128:(kk + 1) * 128, 2730 + i * 128:2730 + i * 128 + w]).then_inc(LW, 16)
        C.add(LW, 2 * KD * len(pairs))
        t.wait_ge(LW, 16 * C.val(LW))
        for nt in range(8):
            par = nt % 2
            LD = LD0 if par == 0 else LD1
            xo = par * KD * 512
            tok0 = nt * 512
            sy.wait_ge(ST0, 16 * C.val(ST0))
            sy.wait_ge(ST1, 16 * C.val(ST1))
            if hp == 0 and nt < 2:
                sy.wait_ge(MM, c1_mm)
            elif len(pair_mm) > len(pairs):
                sy.wait_ge(MM, pair_mm[-(len(pairs) + 1)])
            sy.wait_ge(LD, 16 * C.val(LD))
            for k in range(KD):
                sy.dma_start(out=a_sb[:, xo + k * 512:xo + (k + 1) * 512],
                             in_=hsrc.ap()[k * 128:(k + 1) * 128, tok0:tok0 + 512]).then_inc(LD, 16)
            C.add(LD, KD)
            t.wait_ge(LD, 16 * C.val(LD))
            for ii, i in enumerate(pairs):
                w = _lfc_w(i)
                gidx = len(pair_ev)
                if gidx == 0:
                    t.wait_ge(EV, cons[7])
                bA = gidx % 2
                bG = 2 + gidx % 2
                hp2 = gidx % 2
                STh = ST0 if hp2 == 0 else ST1
                if gidx >= 2:
                    t.wait_ge(EV, pair_ev[gidx - 2][0])
                    t.wait_ge(SQ, pair_ev[gidx - 2][1])
                for kk in range(KD):
                    t.matmul(ps[0:w, bA * 512:bA * 512 + 512],
                             w_sb[:, kk * 2816 + ii * 256:kk * 2816 + ii * 256 + w],
                             a_sb[:, xo + kk * 512:xo + (kk + 1) * 512],
                             start=(kk == 0), stop=(kk == KD - 1))
                for kk in range(KD):
                    li = t.matmul(ps[0:w, bG * 512:bG * 512 + 512],
                                  w_sb[:, kk * 2816 + ii * 256 + 128:kk * 2816 + ii * 256 + 128 + w],
                                  a_sb[:, xo + kk * 512:xo + (kk + 1) * 512],
                                  start=(kk == 0), stop=(kk == KD - 1))
                li.then_inc(MM, 1)
                mm_n += 1
                pair_mm.append(mm_n)
                s.wait_ge(MM, mm_n)
                if gidx >= 2:
                    s.wait_ge(EV, pair_ev[gidx - 2][0])
                s.activation(gg_sb[0:w, hp2 * 512:hp2 * 512 + 512],
                             ps[0:w, bG * 512:bG * 512 + 512], AF.Gelu,
                             bias=bvec_sb[0:w, 46 + i:47 + i]).then_inc(SQ, 1)
                sq_n += 1
                v.wait_ge(SQ, sq_n)
                v.wait_ge(STh, 16 * C.val(STh))
                v.tensor_scalar(t1_sb[0:w, :], ps[0:w, bA * 512:bA * 512 + 512],
                                bvec_sb[0:w, 24 + i:25 + i], None, ALU.add)
                li = v.tensor_tensor(hh_sb[0:w, hp2 * 512:hp2 * 512 + 512],
                                     t1_sb[0:w, :], gg_sb[0:w, hp2 * 512:hp2 * 512 + 512],
                                     ALU.mult)
                li.then_inc(EV, 1)
                ev_n += 1
                pair_ev.append((ev_n, sq_n))
                g.wait_ge(EV, ev_n)
                g.dma_start(out=hgs.ap()[i * 128:i * 128 + w, tok0:tok0 + 512],
                            in_=hh_sb[0:w, hp2 * 512:hp2 * 512 + 512]).then_inc(STh, 16)
                C.add(STh, 1)
    c2a_mm = mm_n

    g.wait_ge(MM, mm_n)
    g.wait_ge(ST0, 16 * C.val(ST0))
    g.wait_ge(ST1, 16 * C.val(ST1))
    sy.wait_ge(ST0, 16 * C.val(ST0))
    sy.wait_ge(ST1, 16 * C.val(ST1))
    g.wait_ge(LW, 16 * C.val(LW))
    for kk in range(LFC_NP):
        w = _lfc_w(kk)
        g.dma_start(out=w_sb[0:w, kk * 1024:(kk + 1) * 1024],
                    in_=W2[kk * 128:kk * 128 + w, :]).then_inc(LW, 16)
    C.add(LW, LFC_NP)
    t.wait_ge(LW, 16 * C.val(LW))
    t.wait_ge(EV, pair_ev[-1][0])
    t.wait_ge(SQ, pair_ev[-1][1])
    evprev = {}
    for nt in range(8):
        par = nt % 2
        LD = LD0 if par == 0 else LD1
        ST = ST0 if par == 0 else ST1
        STo = ST1 if par == 0 else ST0
        xo = par * KD * 512
        tok0 = nt * 512
        if nt >= 2:
            sy.wait_ge(EV, evprev[nt - 2])
            sy.wait_ge(LD, 16 * C.val(LD))
        for k in range(KD):
            sy.dma_start(out=x1_sb[:, xo + k * 512:xo + (k + 1) * 512],
                         in_=x1s.ap()[k * 128:(k + 1) * 128, tok0:tok0 + 512]).then_inc(LD, 16)
        C.add(LD, KD)
        tile_mm_base = mm_n
        for kk in range(LFC_NP):
            w = _lfc_w(kk)
            hp = 0
            if kk >= 1:
                sy.wait_ge(MM, tile_mm_base + kk)
            sy.dma_start(out=a_sb[0:w, xo + hp * 512:xo + hp * 512 + 512],
                         in_=hgs.ap()[kk * 128:kk * 128 + w, tok0:tok0 + 512]).then_inc(LD, 16)
            C.add(LD, 1)
            t.wait_ge(LD, 16 * C.val(LD))
            for m in range(KD):
                li = t.matmul(ps[:, m * 512:(m + 1) * 512],
                              w_sb[0:w, kk * 1024 + m * 128:kk * 1024 + (m + 1) * 128],
                              a_sb[0:w, xo + hp * 512:xo + hp * 512 + 512],
                              start=(kk == 0), stop=(kk == LFC_NP - 1))
            li.then_inc(MM, 1)
            mm_n += 1
        v.wait_ge(MM, mm_n)
        v.wait_ge(LD, 16 * C.val(LD))
        if nt >= 1:
            v.wait_ge(STo, 16 * C.val(STo))
            v.wait_ge(ST, 16 * C.val(ST))
        for m in range(KD):
            v.tensor_scalar(t1_sb[:], ps[:, m * 512:(m + 1) * 512],
                            bvec_sb[:, 16 + m:17 + m], None, ALU.add)
            li = v.tensor_tensor(out_sb[:, m * 512:(m + 1) * 512], t1_sb[:],
                                 x1_sb[:, xo + m * 512:xo + (m + 1) * 512], ALU.add)
        li.then_inc(EV, 1)
        ev_n += 1
        evprev[nt] = ev_n
        t.wait_ge(EV, ev_n)
        g.wait_ge(EV, ev_n)
        for k in range(KD):
            g.dma_start(out=x2T[k * 128:(k + 1) * 128, tok0:tok0 + 512],
                        in_=out_sb[:, k * 512:(k + 1) * 512]).then_inc(ST, 16)
            g.dma_start(out=x2Tb[k * 128:(k + 1) * 128, tok0:tok0 + 512],
                        in_=out_sb[:, k * 512:(k + 1) * 512]).then_inc(ST, 16)
        C.add(ST, 2 * KD)
    g.wait_ge(ST0, 16 * C.val(ST0))
    g.wait_ge(ST1, 16 * C.val(ST1))
    es.close()
    return nc
